# version 02
# baseline (speedup 1.0000x reference)
import numpy as np
import ml_dtypes
from contextlib import ExitStack
import concourse.bass as bass
import concourse.mybir as mybir
from concourse.bass_utils import run_bass_kernel_spmd

F32 = mybir.dt.float32
BF16 = mybir.dt.bfloat16
I32 = mybir.dt.int32
AF = mybir.ActivationFunctionType
ALU = mybir.AluOpType
AX = mybir.AxisListType
BF = ml_dtypes.bfloat16

T, D = 2048, 1024
NTT = T // 128
NKC = D // 128
NTQ = T // 512
NH, DH = 16, 64
FFH = 3072
NE, EH = 8, 4096
DST, DTR = 16, 64
LN_EPS = 1e-5
QSCALE = 1.0 / 8.0
DEQ = 1.0 / 127.0

ENGINES = ("tensor", "vector", "scalar", "gpsimd", "sync")
ENG_EV = {"tensor": "Et", "vector": "Ev", "scalar": "Es", "gpsimd": "Eg", "sync": "Ey"}
DMA_SLOTS = {"gpsimd": 8, "sync": 6, "scalar": 2}
DMA_PREFIX = {"gpsimd": "Dg", "sync": "Dy", "scalar": "Da"}


class Plan:

    def __init__(self, nc, stack):
        self.nc = nc
        self.q = {e: [] for e in ENGINES}
        self.sems = {}
        self.cnt = {}
        self.rr = {e: 0 for e in DMA_SLOTS}
        for s in ENG_EV.values():
            self.sems[s] = stack.enter_context(nc.semaphore(s))
            self.cnt[s] = 0
        for e, n in DMA_SLOTS.items():
            for i in range(n):
                s = f"{DMA_PREFIX[e]}{i}"
                self.sems[s] = stack.enter_context(nc.semaphore(s))
                self.cnt[s] = 0

    def op(self, eng, fn, waits=(), dma=False):
        waits = [w for w in waits if w is not None]
        if dma:
            slot = self.rr[eng]
            self.rr[eng] = (slot + 1) % DMA_SLOTS[eng]
            sem = f"{DMA_PREFIX[eng]}{slot}"
            amt = 16
            if self.cnt[sem] > 0:
                waits = waits + [(sem, self.cnt[sem])]
        else:
            sem = ENG_EV[eng]
            amt = 1
            if self.cnt[sem] > 0:
                waits = waits + [(sem, self.cnt[sem])]
        self.cnt[sem] += amt
        ev = (sem, self.cnt[sem])
        wmax = {}
        for s, v in waits:
            if v > wmax.get(s, -1):
                wmax[s] = v
        sems = self.sems

        def g(e, fn=fn, wmax=wmax, sem=sem, amt=amt):
            for s, v in wmax.items():
                e.wait_ge(sems[s], v)
            fn(e).then_inc(sems[sem], amt)
        self.q[eng].append(g)
        return ev

    def wait(self, eng, waits):
        wmax = {}
        for wv in waits:
            if wv is None:
                continue
            s, v = wv
            if v > wmax.get(s, -1):
                wmax[s] = v
        sems = self.sems

        def g(e, wmax=wmax):
            for s, v in wmax.items():
                e.wait_ge(sems[s], v)
        self.q[eng].append(g)

    def now(self):
        return [(s, v) for s, v in self.cnt.items() if v > 0]

    def fence(self):
        evs = self.now()
        for e in ENGINES:
            self.wait(e, evs)

    def build(self, block):
        for name in ENGINES:
            dec = getattr(block, name)
            fns = self.q[name]

            def body(e, fns=fns):
                for f in fns:
                    f(e)
            dec(body)


class PBuf:

    def __init__(self, t):
        self.t = t
        self.ev_w = None
        self.ev_r = None


class Ctx:
    pass


def mk_ctx(nc, stack, p):
    c = Ctx()
    c.nc = nc
    c.st = stack
    c.p = p
    c.stacks = [stack]

    def sb(name, shape, dt):
        return c.stacks[-1].enter_context(nc.sbuf_tensor(name, shape, dt))
    c.sb = sb

    def ps(name, shape, dt):
        return PBuf(stack.enter_context(nc.psum_tensor(name, shape, dt)))
    c.ps = ps

    c.dummy = c.sb("dummy_cf", [1, 16], F32)
    p.op("vector", lambda e: e.memset(c.dummy[:, :], 0.0))

    def confirm(eng="vector"):
        if eng == "scalar":
            return p.op(eng, lambda e: e.copy(c.dummy[0:1, 8:16], c.dummy[0:1, 0:8]))
        return p.op(eng, lambda e: e.tensor_copy(c.dummy[0:1, 8:16], c.dummy[0:1, 0:8]))
    c.confirm = confirm
    return c



def alloc_common(c, with_fmA=True):
    c.xr = c.sb("xr", [128, NTT * D], F32)
    if with_fmA:
        c.fmA = c.sb("fmA", [128, NKC * T], BF16)
    c.ident_bf = c.sb("identb_sb", [128, 128], BF16)
    c.ident_f = c.sb("identf_sb", [128, 128], F32)
    c.stat1 = c.sb("stat1", [128, NTT], F32)
    c.stat2 = c.sb("stat2", [128, NTT], F32)
    c.mcols = c.sb("mcols", [128, NTT], F32)
    c.rcols = c.sb("rcols", [128, NTT], F32)
    c.tmp16 = c.sb("tmp16", [128, NTT], F32)
    c.halfcol = c.sb("halfcol", [128, 1], F32)
    c.ln_scr = c.sb("ln_scr", [128, D], BF16)
    c.stageF = c.sb("stageF", [128, 512], F32)
    c.pA = c.ps("pA", [128, 512], F32)
    c.pB = c.ps("pB", [128, 512], F32)
    c.pC = c.ps("pC", [128, 512], F32)
    c.pD = c.ps("pD", [128, 512], F32)
    c.pT1 = c.ps("pT1", [128, 1024], BF16)
    c.pT2 = c.ps("pT2", [128, 1024], BF16)
    c.pF1 = c.ps("pF1", [128, 512], F32)
    c.pF2 = c.ps("pF2", [128, 512], F32)


def emit_load_consts(c, ident_bf_d, ident_f_d):
    p = c.p
    c.ev_ident = p.op("sync", lambda e, o=c.ident_bf, i=ident_bf_d: e.dma_start(out=o[:, :], in_=i[:, :]), dma=True)
    c.ev_identf = p.op("sync", lambda e, o=c.ident_f, i=ident_f_d: e.dma_start(out=o[:, :], in_=i[:, :]), dma=True)


def emit_load_x(c, x_d):
    p = c.p
    src = x_d.rearrange("(tt p) d -> p tt d", p=128)
    dst = c.xr[:, :].rearrange("p (tt d) -> p tt d", d=D)
    c.ev_x = p.op("sync", lambda e, o=dst, i=src: e.dma_start(out=o, in_=i), dma=True)


def emit_store_x(c, out_d, waits=None):
    p = c.p
    if waits is None:
        waits = p.now()
    dst = out_d.rearrange("(tt p) d -> p tt d", p=128)
    src = c.xr[:, :].rearrange("p (tt d) -> p tt d", d=D)
    ev = p.op("sync", lambda e, o=dst, i=src: e.dma_start(out=o, in_=i), waits=waits, dma=True)
    return ev


def emit_ln_stats(c, deps):
    p = c.p
    evs = []
    prev_hc = None
    for tt in range(NTT):
        xv = c.xr[:, tt * D:(tt + 1) * D]
        e1 = p.op("vector", lambda e, o=c.stat1[:, tt:tt + 1], i=xv: e.reduce_sum(o, i, axis=AX.X), waits=deps)
        e2a = p.op("scalar", lambda e, o=c.stageF, i=c.xr[:, tt * D: tt * D + 512], a=c.stat2[:, tt:tt + 1]:
                   e.activation(o[:, :], i, AF.Square, accum_out=a), waits=deps)
        e2b = p.op("scalar", lambda e, o=c.stageF, i=c.xr[:, tt * D + 512: (tt + 1) * D], a=c.halfcol:
                   e.activation(o[:, :], i, AF.Square, accum_out=a[:, :]), waits=deps + [prev_hc])
        e2c = p.op("vector", lambda e, o=c.stat2[:, tt:tt + 1]:
                   e.tensor_add(o, o, c.halfcol[:, :]), waits=[e2a, e2b])
        prev_hc = e2c
        evs += [e1, e2c]
    em = p.op("vector", lambda e: e.tensor_scalar_mul(c.mcols[:, :], c.stat1[:, :], 1.0 / D), waits=evs)
    ev = p.op("vector", lambda e: e.tensor_scalar_mul(c.rcols[:, :], c.stat2[:, :], 1.0 / D), waits=evs)
    e3 = p.op("vector", lambda e: e.tensor_mul(c.tmp16[:, :], c.mcols[:, :], c.mcols[:, :]), waits=[em])
    e4 = p.op("vector", lambda e: e.tensor_sub(c.rcols[:, :], c.rcols[:, :], c.tmp16[:, :]), waits=[ev, e3])
    e5 = p.op("vector", lambda e: e.tensor_scalar_add(c.rcols[:, :], c.rcols[:, :], LN_EPS), waits=[e4])
    e6 = p.op("vector", lambda e: e.reciprocal(c.tmp16[:, :], c.rcols[:, :]), waits=[e5])
    e7 = p.op("scalar", lambda e: e.activation(c.rcols[:, :], c.tmp16[:, :], AF.Sqrt), waits=[e6])
    return [em, e7]


def emit_ln_to_fm(c, deps, dst=None, dst_stride=T):
    p = c.p
    if dst is None:
        dst = c.fmA
    sevs = emit_ln_stats(c, deps)
    out_evs = []
    pts = [c.pT1, c.pT2]
    prev_tr = None
    for tt in range(NTT):
        xv = c.xr[:, tt * D:(tt + 1) * D]
        ea = p.op("vector", lambda e, o=c.ln_scr, i=xv, m=c.mcols[:, tt:tt + 1], r=c.rcols[:, tt:tt + 1]:
                  e.tensor_scalar(o[:, :], i, m, r, ALU.subtract, ALU.mult),
                  waits=sevs + deps + [prev_tr])
        pt = pts[tt % 2]
        tr = None
        for k in range(NKC):
            tr = p.op("tensor", lambda e, o=pt.t[:, k * 128:(k + 1) * 128], i=c.ln_scr[:, k * 128:(k + 1) * 128],
                      idn=c.ident_bf: e.transpose(o, i, idn[:, :]),
                      waits=[ea, c.ev_ident, pt.ev_r] if k == 0 else [ea])
        pt.ev_w = tr
        prev_tr = tr
        dv = dst[:, :].rearrange("p (k r) -> p k r", r=dst_stride)[:, :, tt * 128:(tt + 1) * 128]
        sv = pt.t[:, :].rearrange("p (k r) -> p k r", r=128)
        ee = p.op("vector", lambda e, o=dv, i=sv: e.tensor_copy(o, i), waits=[tr])
        pt.ev_r = c.confirm()
        out_evs.append(ee)
    return out_evs


def emit_fm_psum_to_resid(c, src_sb_f32, ot, tq, deps, scale=None):
    p = c.p
    evs = []
    pfs = [c.pF1, c.pF2]
    for j in range(4):
        pf = pfs[j % 2]
        tt = tq * 4 + j
        tr = p.op("tensor", lambda e, o=pf.t[:, 0:128], i=src_sb_f32[:, j * 128:(j + 1) * 128], idn=c.ident_f:
                  e.transpose(o, i, idn[:, :]), waits=deps + [c.ev_identf, pf.ev_r])
        pf.ev_w = tr
        xv = c.xr[:, tt * D + ot * 128: tt * D + (ot + 1) * 128]
        ee = p.op("vector", lambda e, o=xv, i=pf.t[:, 0:128], x=xv:
                  e.tensor_add(o, i, x), waits=[tr])
        pf.ev_r = c.confirm()
        evs.append(ee)
    return evs



def emit_l0attn(c, wq_d, wk_d, wv_d, wo_d, mask_d, ones_d, dbg=False):
    nc = c.nc
    p = c.p
    mask = c.sb("mask_sb", [128, 4 * 512], BF16)
    ones_col = c.sb("ones_sb", [128, 1], BF16)
    ev_mask = p.op("sync", lambda e: e.dma_start(out=mask[:, :], in_=mask_d[:, :]), dma=True)
    ev_ones = p.op("sync", lambda e: e.dma_start(out=ones_col[:, :], in_=ones_d[:, :]), dma=True)

    wbuf = c.sb("wbuf", [128, NKC * D], BF16)
    qc = c.sb("qc", [128, T], BF16)
    kc = c.sb("kc", [128, T], BF16)
    vc = c.sb("vc", [128, NTT * 128], BF16)
    ptb = c.sb("ptb", [128, 16 * 512], BF16)
    att = c.sb("att", [128, NTT * 128], BF16)
    inv = c.sb("inv", [128, 1], F32)

    ln_evs = emit_ln_to_fm(c, [c.ev_x])
    if dbg:
        dfm = nc.dram_tensor("dbg_fm", [128, NKC * T], BF16, kind="ExternalOutput").ap()
        p.op("sync", lambda e: e.dma_start(out=dfm[:, :], in_=c.fmA[:, :]), waits=ln_evs, dma=True)

    c.wbuf_guard = None

    def load_w(wd):
        src = wd.rearrange("(k p) o -> p k o", p=128)
        dst = wbuf[:, :].rearrange("p (k o) -> p k o", o=D)
        ev = p.op("gpsimd", lambda e, o=dst, i=src: e.dma_start(out=o, in_=i),
                  waits=[c.wbuf_guard], dma=True)
        return ev

    wo_stage_deps = []
    attn_fm = c.sb("attn_fm", [128, NKC * T], BF16)

    for kc_i in range(NKC):
        prev_evq = []
        for (w_d, dstbuf, isv) in ((wq_d, qc, False), (wk_d, kc, False), (wv_d, vc, True)):
            lw = load_w(w_d)
            if not isv:
                for tq in range(NTQ):
                    pb = [c.pA, c.pB][tq % 2]
                    mws = ln_evs + [lw, pb.ev_r]
                    mm = None
                    for kk in range(NKC):
                        lh = wbuf[:, kk * D + kc_i * 128: kc_i * 128 + kk * D + 128]
                        rh = c.fmA[:, kk * T + tq * 512: kk * T + (tq + 1) * 512]
                        mm = p.op("tensor", lambda e, o=pb.t[:, 0:512], l=lh, r=rh,
                                  s=(kk == 0), z=(kk == NKC - 1):
                                  e.matmul(o, l, r, start=s, stop=z),
                                  waits=mws if kk == 0 else [])
                    pb.ev_w = mm
                    c.wbuf_guard = mm
                    ee = p.op("vector", lambda e, o=dstbuf[:, tq * 512:(tq + 1) * 512], i=pb.t[:, 0:512]:
                              e.tensor_copy(o, i), waits=[pb.ev_w])
                    pb.ev_r = ee
                    prev_evq.append(ee)
            else:
                for tt in range(NTT):
                    pb = [c.pA, c.pB][tt % 2]
                    mws = ln_evs + [lw, pb.ev_r]
                    mm = None
                    for kk in range(NKC):
                        lh = c.fmA[:, kk * T + tt * 128: kk * T + (tt + 1) * 128]
                        rh = wbuf[:, kk * D + kc_i * 128: kc_i * 128 + kk * D + 128]
                        mm = p.op("tensor", lambda e, o=pb.t[:, 0:128], l=lh, r=rh,
                                  s=(kk == 0), z=(kk == NKC - 1):
                                  e.matmul(o, l, r, start=s, stop=z),
                                  waits=mws if kk == 0 else [])
                    pb.ev_w = mm
                    c.wbuf_guard = mm
                    ee = p.op("vector", lambda e, o=dstbuf[:, tt * 128:(tt + 1) * 128], i=pb.t[:, 0:128]:
                              e.tensor_copy(o, i), waits=[pb.ev_w])
                    pb.ev_r = ee
                    prev_evq.append(ee)
        qkv_done = list(prev_evq)
        if dbg and kc_i == 0:
            dq = nc.dram_tensor("dbg_q", [128, T], BF16, kind="ExternalOutput").ap()
            dk = nc.dram_tensor("dbg_k", [128, T], BF16, kind="ExternalOutput").ap()
            dv = nc.dram_tensor("dbg_v", [128, NTT * 128], BF16, kind="ExternalOutput").ap()
            p.op("sync", lambda e: e.dma_start(out=dq[:, :], in_=qc[:, :]), waits=qkv_done, dma=True)
            p.op("sync", lambda e: e.dma_start(out=dk[:, :], in_=kc[:, :]), waits=qkv_done, dma=True)
            p.op("sync", lambda e: e.dma_start(out=dv[:, :], in_=vc[:, :]), waits=qkv_done, dma=True)

        att_evs = []
        for hh in range(2):
            hbase = hh * 64
            for sq in range(NTQ):
                nkt = 4 * sq + 4
                exp_evs = []
                for kt in range(nkt):
                    pb = [c.pA, c.pB][kt % 2]
                    mm = p.op("tensor", lambda e, o=pb.t[:, 0:512],
                              l=kc[hbase:hbase + 64, kt * 128:(kt + 1) * 128],
                              r=qc[hbase:hbase + 64, sq * 512:(sq + 1) * 512]:
                              e.matmul(o, l, r, start=True, stop=True),
                              waits=qkv_done + [pb.ev_r])
                    pb.ev_w = mm
                    ex = p.op("scalar", lambda e, o=ptb[:, kt * 512:(kt + 1) * 512], i=pb.t[:, 0:512]:
                              e.activation(o, i, AF.Exp, scale=QSCALE), waits=[mm])
                    pb.ev_r = ex
                    exp_evs.append(ex)
                    if kt >= 4 * sq:
                        rel = kt - 4 * sq
                        me = p.op("vector", lambda e, o=ptb[:, kt * 512:(kt + 1) * 512],
                                  m=mask[:, rel * 512:(rel + 1) * 512]:
                                  e.tensor_mul(o, o, m), waits=[ex, ev_mask])
                        exp_evs.append(me)
                for si in range(4):
                    stile = sq * 4 + si
                    oevs = []
                    for kt in range(nkt):
                        lh = ptb[:, kt * 512 + si * 128: kt * 512 + (si + 1) * 128]
                        mm1 = p.op("tensor", lambda e, o=c.pC.t[:, 0:64], l=lh,
                                   r=vc[:, kt * 128 + hbase: kt * 128 + hbase + 64],
                                   s=(kt == 0), z=(kt == nkt - 1):
                                   e.matmul(o, l, r, start=s, stop=z),
                                   waits=exp_evs + [c.pC.ev_r] if kt == 0 else [])
                        mm2 = p.op("tensor", lambda e, o=c.pD.t[:, 0:1], l=lh,
                                   r=ones_col[:, 0:1],
                                   s=(kt == 0), z=(kt == nkt - 1):
                                   e.matmul(o, l, r, start=s, stop=z),
                                   waits=[ev_ones, c.pD.ev_r] if kt == 0 else [])
                        oevs += [mm1, mm2]
                    c.pC.ev_w = oevs[-2]
                    c.pD.ev_w = oevs[-1]
                    rc = p.op("vector", lambda e: e.reciprocal(inv[:, :], c.pD.t[:, 0:1]), waits=[c.pD.ev_w])
                    c.pD.ev_r = rc
                    ee = p.op("vector", lambda e, o=att[:, stile * 128 + hbase: stile * 128 + hbase + 64],
                              i=c.pC.t[:, 0:64]:
                              e.tensor_scalar(o, i, inv[:, :], None, ALU.mult), waits=[rc, c.pC.ev_w])
                    c.pC.ev_r = ee
                    att_evs.append(ee)
        if dbg and kc_i == 0:
            da = nc.dram_tensor("dbg_att", [128, NTT * 128], BF16, kind="ExternalOutput").ap()
            p.op("sync", lambda e: e.dma_start(out=da[:, :], in_=att[:, :]), waits=att_evs, dma=True)
        if dbg and kc_i == 7:
            da7 = nc.dram_tensor("dbg_att7", [128, NTT * 128], BF16, kind="ExternalOutput").ap()
            p.op("sync", lambda e: e.dma_start(out=da7[:, :], in_=att[:, :]), waits=att_evs, dma=True)
        pts = [c.pT1, c.pT2]
        for tt in range(NTT):
            pt = pts[tt % 2]
            tr = p.op("tensor", lambda e, o=pt.t[:, 0:128], i=att[:, tt * 128:(tt + 1) * 128], idn=c.ident_bf:
                      e.transpose(o, i, idn[:, :]), waits=att_evs + [c.ev_ident, pt.ev_r])
            pt.ev_w = tr
            ee = p.op("vector", lambda e, o=attn_fm[:, kc_i * T + tt * 128: kc_i * T + (tt + 1) * 128],
                      i=pt.t[:, 0:128]: e.tensor_copy(o, i), waits=[tr])
            pt.ev_r = ee
            wo_stage_deps.append(ee)

    if dbg:
        dafm = nc.dram_tensor("dbg_attnfm", [128, NKC * T], BF16, kind="ExternalOutput").ap()
        p.op("sync", lambda e: e.dma_start(out=dafm[:, :], in_=attn_fm[:, :]), waits=wo_stage_deps, dma=True)
    lwo = load_w(wo_d)
    for ot in range(NKC):
        for tq in range(NTQ):
            pb = [c.pA, c.pB][(ot * NTQ + tq) % 2]
            mm = None
            for kk in range(NKC):
                lh = wbuf[:, kk * D + ot * 128: kk * D + (ot + 1) * 128]
                rh = attn_fm[:, kk * T + tq * 512: kk * T + (tq + 1) * 512]
                mm = p.op("tensor", lambda e, o=pb.t[:, 0:512], l=lh, r=rh,
                          s=(kk == 0), z=(kk == NKC - 1):
                          e.matmul(o, l, r, start=s, stop=z),
                          waits=wo_stage_deps + [lwo, pb.ev_r] if kk == 0 else [])
            pb.ev_w = mm
            c.wbuf_guard = mm
            ec = p.op("scalar", lambda e, o=c.stageF, i=pb.t[:, 0:512]: e.copy(o[:, :], i), waits=[pb.ev_w])
            pb.ev_r = ec
            emit_fm_psum_to_resid(c, c.stageF, ot, tq, [ec])



def emit_l0ffn(c, wg_d, wu_d, wd_d, dbg=False):
    nc = c.nc
    p = c.p
    NHC = FFH // 128
    NHG = FFH // 512
    h_sb = c.sb("h_sb", [128, NHC * 512], BF16)
    g_grp = c.sb("g_grp", [128, 4 * 512], BF16)
    wd_sb = c.sb("wd_sb", [128, NHC * D], BF16)
    wgu_sb = c.sb("wgu_sb", [128, NKC * 512], BF16)

    ln_evs = emit_ln_to_fm(c, [c.ev_x])

    wdv = wd_d.rearrange("(k p) o -> p k o", p=128)
    lwd = p.op("gpsimd", lambda e, o=wd_sb[:, :].rearrange("p (k o) -> p k o", o=D), i=wdv:
               e.dma_start(out=o, in_=i), dma=True)

    c.wgu_guard = None
    g_guard = [None] * 4
    sil_ev = [None] * 4
    for tq in range(NTQ):
        hevs = []
        for hg in range(NHG):
            for (w_d, is_gate) in ((wg_d, True), (wu_d, False)):
                src = w_d[:, hg * 512:(hg + 1) * 512].rearrange("(k p) h -> p k h", p=128)
                lw = p.op("gpsimd", lambda e, o=wgu_sb[:, :].rearrange("p (k h) -> p k h", h=512), i=src:
                          e.dma_start(out=o, in_=i), waits=[c.wgu_guard], dma=True)
                for hti in range(4):
                    ht = hg * 4 + hti
                    pb = c.pA if is_gate else c.pB
                    mm = None
                    for kk in range(NKC):
                        lh = wgu_sb[:, kk * 512 + hti * 128: kk * 512 + (hti + 1) * 128]
                        rh = c.fmA[:, kk * T + tq * 512: kk * T + (tq + 1) * 512]
                        mm = p.op("tensor", lambda e, o=pb.t[:, 0:512], l=lh, r=rh,
                                  s=(kk == 0), z=(kk == NKC - 1):
                                  e.matmul(o, l, r, start=s, stop=z),
                                  waits=ln_evs + [lw, pb.ev_r] if kk == 0 else [])
                    pb.ev_w = mm
                    c.wgu_guard = mm
                    if is_gate:
                        ee = p.op("scalar", lambda e, o=g_grp[:, hti * 512:(hti + 1) * 512], i=pb.t[:, 0:512]:
                                  e.activation(o, i, AF.Silu, scale=DEQ),
                                  waits=[pb.ev_w, g_guard[hti]])
                        sil_ev[hti] = ee
                    else:
                        ee = p.op("vector", lambda e, o=h_sb[:, ht * 512:(ht + 1) * 512],
                                  i=g_grp[:, hti * 512:(hti + 1) * 512], u=pb.t[:, 0:512]:
                                  e.tensor_mul(o, i, u), waits=[pb.ev_w, sil_ev[hti]])
                        g_guard[hti] = ee
                    pb.ev_r = ee
                    hevs.append(ee)
        for ot in range(NKC):
            pb = [c.pC, c.pD][ot % 2]
            mm = None
            for hk in range(NHC):
                lh = wd_sb[:, hk * D + ot * 128: hk * D + (ot + 1) * 128]
                rh = h_sb[:, hk * 512:(hk + 1) * 512]
                mm = p.op("tensor", lambda e, o=pb.t[:, 0:512], l=lh, r=rh,
                          s=(hk == 0), z=(hk == NHC - 1):
                          e.matmul(o, l, r, start=s, stop=z),
                          waits=hevs + [lwd, pb.ev_r] if hk == 0 else [])
            pb.ev_w = mm
            ec = p.op("scalar", lambda e, o=c.stageF, i=pb.t[:, 0:512]:
                      e.activation(o[:, :], i, AF.Copy, scale=DEQ * DEQ), waits=[pb.ev_w])
            pb.ev_r = ec
            emit_fm_psum_to_resid(c, c.stageF, ot, tq, [ec])



def emit_l1attn(c, wxp1_d, wxp2_d, wdt_d, are_d, bdt_d, dd_d, sel_d, dbg=False):
    nc = c.nc
    p = c.p
    wxp1 = c.sb("wxp1", [128, NKC * 80], BF16)
    wxp2 = c.sb("wxp2", [128, NKC * 16], BF16)
    wdt = c.sb("wdt", [64, D], BF16)
    a_re = c.sb("a_re", [128, 128], F32)
    a_sc = c.sb("a_sc", [128, 128], F32)
    bdt = c.sb("bdt", [128, 8], F32)
    dcol = c.sb("dcol", [128, 8], F32)
    sel16 = c.sb("sel16_sb", [16, 16 * 128], BF16)
    dtl = c.sb("dtl", [64, T], BF16)
    b_bf = c.sb("b_bf", [16, T], BF16)
    c_bf = c.sb("c_bf", [16, T], BF16)
    dt_sb = c.sb("dt_sb", [128, NKC * T], BF16)
    u_sb = c.sb("u_sb", [128, NKC * 512], BF16)
    y_acc = c.sb("y_acc", [128, NKC * 512], F32)
    carry = c.sb("carry", [128, NKC * DST], F32)
    brep = c.sb("brep", [128, 512], BF16)
    crep = c.sb("crep", [128, 512], BF16)
    sA = c.sb("sA", [128, 512], F32)
    sB = c.sb("sB", [128, 512], F32)
    a_t = [c.sb("a_t0", [128, 512], BF16), c.sb("a_t1", [128, 512], BF16)]
    b_t = [c.sb("b_t0", [128, 512], BF16), c.sb("b_t1", [128, 512], BF16)]
    h_t = [c.sb("h_t0", [128, 512], BF16), c.sb("h_t1", [128, 512], BF16)]
    y_t = [c.sb("y_t0", [128, 512], BF16), c.sb("y_t1", [128, 512], BF16)]

    lw1 = p.op("gpsimd", lambda e, o=wxp1[:, :].rearrange("p (k o) -> p k o", o=80),
               i=wxp1_d.rearrange("(k p) o -> p k o", p=128): e.dma_start(out=o, in_=i), dma=True)
    lw2 = p.op("gpsimd", lambda e, o=wxp2[:, :].rearrange("p (k o) -> p k o", o=16),
               i=wxp2_d.rearrange("(k p) o -> p k o", p=128): e.dma_start(out=o, in_=i), dma=True)
    lw3 = p.op("gpsimd", lambda e: e.dma_start(out=wdt[:, :], in_=wdt_d[:, :]), dma=True)
    lw4 = p.op("sync", lambda e: e.dma_start(out=a_re[:, :], in_=are_d[:, :]), dma=True)
    lw5 = p.op("sync", lambda e: e.dma_start(out=bdt[:, :], in_=bdt_d[:, :]), dma=True)
    lw6 = p.op("sync", lambda e: e.dma_start(out=dcol[:, :], in_=dd_d[:, :]), dma=True)
    lw7 = p.op("sync", lambda e: e.dma_start(out=sel16[:, :], in_=sel_d[:, :]), dma=True)

    ea1 = p.op("scalar", lambda e: e.activation(a_sc[:, :], a_re[:, :], AF.Exp), waits=[lw4])
    ea2 = p.op("vector", lambda e: e.tensor_scalar_mul(a_sc[:, :], a_sc[:, :], -1.0), waits=[ea1])

    ln_evs = emit_ln_to_fm(c, [c.ev_x])

    xdbl_evs = []
    for tq in range(NTQ):
        p1 = c.pA
        p2 = c.pB
        mm1 = None
        mm2 = None
        for kk in range(NKC):
            lh = wxp1[:, kk * 80:(kk + 1) * 80]
            rh = c.fmA[:, kk * T + tq * 512: kk * T + (tq + 1) * 512]
            mm1 = p.op("tensor", lambda e, o=p1.t[0:80, 0:512], l=lh, r=rh,
                       s=(kk == 0), z=(kk == NKC - 1): e.matmul(o, l, r, start=s, stop=z),
                       waits=ln_evs + [lw1, p1.ev_r] if kk == 0 else [])
        for kk in range(NKC):
            lh = wxp2[:, kk * 16:(kk + 1) * 16]
            rh = c.fmA[:, kk * T + tq * 512: kk * T + (tq + 1) * 512]
            mm2 = p.op("tensor", lambda e, o=p2.t[0:16, 0:512], l=lh, r=rh,
                       s=(kk == 0), z=(kk == NKC - 1): e.matmul(o, l, r, start=s, stop=z),
                       waits=[lw2, p2.ev_r] if kk == 0 else [])
        p1.ev_w = mm1
        p2.ev_w = mm2
        e1 = p.op("vector", lambda e, o=dtl[:, tq * 512:(tq + 1) * 512], i=p1.t[0:64, 0:512]:
                  e.tensor_copy(o, i), waits=[mm1])
        e2 = p.op("vector", lambda e, o=b_bf[:, tq * 512:(tq + 1) * 512], i=p1.t[64:80, 0:512]:
                  e.tensor_copy(o, i), waits=[mm1])
        e3 = p.op("vector", lambda e, o=c_bf[:, tq * 512:(tq + 1) * 512], i=p2.t[0:16, 0:512]:
                  e.tensor_copy(o, i), waits=[mm2])
        p1.ev_r = e2
        p2.ev_r = e3
        xdbl_evs += [e1, e2, e3]

    dt_evs = []
    prev_ad = None
    for dc in range(NKC):
        for tq in range(NTQ):
            pb = [c.pA, c.pB][(dc * NTQ + tq) % 2]
            bcol = bdt[:, dc:dc + 1]
            mm = p.op("tensor", lambda e, o=pb.t[:, 0:512], l=wdt[:, dc * 128:(dc + 1) * 128],
                      r=dtl[0:64, tq * 512:(tq + 1) * 512]:
                      e.matmul(o, l, r, start=True, stop=True),
                      waits=xdbl_evs + [lw3, pb.ev_r])
            pb.ev_w = mm
            a1 = p.op("scalar", lambda e, o=sA, i=pb.t[:, 0:512], b=bcol:
                      e.activation(o[:, :], i, AF.Abs, bias=b), waits=[mm, lw5, prev_ad])
            a2 = p.op("scalar", lambda e, o=sB, i=sA: e.activation(o[:, :], i[:, :], AF.Exp, scale=-1.0),
                      waits=[a1])
            a3 = p.op("scalar", lambda e, o=sA, i=sB: e.activation(o[:, :], i[:, :], AF.Ln, bias=1.0),
                      waits=[a2])
            a4 = p.op("scalar", lambda e, o=sB, i=pb.t[:, 0:512], b=bcol:
                      e.activation(o[:, :], i, AF.Relu, bias=b), waits=[a3])
            pb.ev_r = a4
            ad = p.op("vector", lambda e, o=dt_sb[:, dc * T + tq * 512: dc * T + (tq + 1) * 512],
                      i=sA, j=sB: e.tensor_add(o, i[:, :], j[:, :]), waits=[a3, a4])
            dt_evs.append(ad)
            prev_ad = ad

    if dbg:
        ddtl = nc.dram_tensor("dbg_dtl", [64, T], BF16, kind="ExternalOutput").ap()
        dbb = nc.dram_tensor("dbg_b", [16, T], BF16, kind="ExternalOutput").ap()
        dcb = nc.dram_tensor("dbg_c", [16, T], BF16, kind="ExternalOutput").ap()
        ddt = nc.dram_tensor("dbg_dt", [128, NKC * T], BF16, kind="ExternalOutput").ap()
        p.op("sync", lambda e: e.dma_start(out=ddtl[:, :], in_=dtl[:, :]), waits=xdbl_evs, dma=True)
        p.op("sync", lambda e: e.dma_start(out=dbb[:, :], in_=b_bf[:, :]), waits=xdbl_evs, dma=True)
        p.op("sync", lambda e: e.dma_start(out=dcb[:, :], in_=c_bf[:, :]), waits=xdbl_evs, dma=True)
        p.op("sync", lambda e: e.dma_start(out=ddt[:, :], in_=dt_sb[:, :]), waits=dt_evs, dma=True)

    c.stage_guard = None
    y_guard = [None] * NKC
    fin_evs = []
    for tq in range(NTQ):
        u_evs = []
        for dc in range(NKC):
            ue = p.op("vector", lambda e, o=u_sb[:, dc * 512:(dc + 1) * 512],
                      i=dt_sb[:, dc * T + tq * 512: dc * T + (tq + 1) * 512],
                      j=c.fmA[:, dc * T + tq * 512: dc * T + (tq + 1) * 512]:
                      e.tensor_mul(o, i, j), waits=dt_evs + ln_evs + [y_guard[dc]])
            u_evs.append(ue)
        scan_prev = [None, None]
        pool_prev = [None, None]
        for n in range(DST):
            mmB = p.op("tensor", lambda e, o=c.pC.t[:, 0:512], l=sel16[:, n * 128:(n + 1) * 128],
                       r=b_bf[:, tq * 512:(tq + 1) * 512]:
                       e.matmul(o, l, r, start=True, stop=True),
                       waits=xdbl_evs + [lw7, c.pC.ev_r])
            c.pC.ev_w = mmB
            eb = p.op("vector", lambda e, o=brep, i=c.pC.t[:, 0:512]: e.tensor_copy(o[:, :], i),
                      waits=[mmB, scan_prev[0], scan_prev[1]])
            c.pC.ev_r = eb
            mmC = p.op("tensor", lambda e, o=c.pD.t[:, 0:512], l=sel16[:, n * 128:(n + 1) * 128],
                       r=c_bf[:, tq * 512:(tq + 1) * 512]:
                       e.matmul(o, l, r, start=True, stop=True),
                       waits=[c.pD.ev_r])
            c.pD.ev_w = mmC
            ec = p.op("vector", lambda e, o=crep, i=c.pD.t[:, 0:512]: e.tensor_copy(o[:, :], i),
                      waits=[mmC, pool_prev[0], pool_prev[1]])
            c.pD.ev_r = ec
            for dc in range(NKC):
                par = dc % 2
                av = p.op("scalar", lambda e, o=a_t[par], i=dt_sb[:, dc * T + tq * 512: dc * T + (tq + 1) * 512],
                          s=a_sc[:, dc * 16 + n: dc * 16 + n + 1]:
                          e.activation(o[:, :], i, AF.Exp, scale=s),
                          waits=[ea2, scan_prev[par]] + dt_evs)
                bv = p.op("vector", lambda e, o=b_t[par], i=u_sb[:, dc * 512:(dc + 1) * 512], j=brep:
                          e.tensor_mul(o[:, :], i, j[:, :]), waits=u_evs + [eb, scan_prev[par]])
                init = 0.0 if tq == 0 else carry[:, dc * DST + n: dc * DST + n + 1]
                hv = p.op("vector", lambda e, o=h_t[par], i=a_t[par], j=b_t[par], z=init:
                          e.tensor_tensor_scan(o[:, :], i[:, :], j[:, :], z, ALU.mult, ALU.add),
                          waits=[av, bv, pool_prev[par]])
                cv = p.op("vector", lambda e, o=carry[:, dc * DST + n: dc * DST + n + 1], i=h_t[par]:
                          e.tensor_copy(o, i[:, 511:512]), waits=[hv])
                yeng = getattr(build_l1attn, "Y_ENGINE", "gpsimd")
                yv = p.op(yeng, lambda e, o=y_t[par], i=h_t[par], j=crep:
                          e.tensor_mul(o[:, :], i[:, :], j[:, :]), waits=[hv, ec])
                if n == 0:
                    ya = p.op(yeng, lambda e, o=y_acc[:, dc * 512:(dc + 1) * 512], i=y_t[par]:
                              e.tensor_copy(o, i[:, :]), waits=[yv, y_guard[dc]])
                else:
                    ya = p.op(yeng, lambda e, o=y_acc[:, dc * 512:(dc + 1) * 512], i=y_t[par]:
                              e.tensor_add(o, o, i[:, :]), waits=[yv])
                scan_prev[par] = cv
                pool_prev[par] = ya
        if dbg and tq == 0:
            dya = nc.dram_tensor("dbg_yacc", [128, NKC * 512], F32, kind="ExternalOutput").ap()
            dcr = nc.dram_tensor("dbg_carry", [128, NKC * DST], F32, kind="ExternalOutput").ap()
            p.op("sync", lambda e: e.dma_start(out=dya[:, :], in_=y_acc[:, :]),
                 waits=[pool_prev[0], pool_prev[1]], dma=True)
            p.op("sync", lambda e: e.dma_start(out=dcr[:, :], in_=carry[:, :]),
                 waits=[scan_prev[0], scan_prev[1]], dma=True)
        fence_mode = getattr(build_l1attn, "FENCE_MODE", "fence")
        if fence_mode == "fence":
            p.fence()
        elif fence_mode == "sblock":
            p.wait("scalar", [("Ev", p.cnt["Ev"]), ("Eg", p.cnt["Eg"])])
        elif fence_mode == "tblock":
            p.wait("tensor", [("Ev", p.cnt["Ev"]), ("Eg", p.cnt["Eg"])])
        elif fence_mode == "gblock":
            p.wait("gpsimd", [("Ev", p.cnt["Ev"]), ("Es", p.cnt["Es"])])
        elif fence_mode == "slack":
            yeng0 = getattr(build_l1attn, "Y_ENGINE", "gpsimd")
            nslack = getattr(build_l1attn, "NSLACK", 4)
            sl = None
            for _ in range(nslack):
                sl = p.op(yeng0, lambda e: e.tensor_copy(sB[:, 0:512], sA[:, 0:512]),
                          waits=[pool_prev[0], pool_prev[1]])
            pool_prev = [sl, sl]
        elif fence_mode == "et":
            p.wait("vector", [("Et", p.cnt["Et"])])
        elif fence_mode == "es":
            p.wait("vector", [("Es", p.cnt["Es"])])
        elif fence_mode == "eg":
            p.wait("vector", [("Eg", p.cnt["Eg"])])
        if dbg and tq == 1:
            park = c.sb("park", [128, 2 * 512], F32)
            pk = p.op("vector", lambda e: e.tensor_copy(park[:, :], y_acc[:, 6 * 512:8 * 512]),
                      waits=[pool_prev[0], pool_prev[1]])
            dpk = nc.dram_tensor("dbg_park", [128, 2 * 512], F32, kind="ExternalOutput").ap()
            p.op("sync", lambda e: e.dma_start(out=dpk[:, :], in_=park[:, :]), waits=[pk], dma=True)
        for dc in range(NKC):
            fe = p.op("vector", lambda e, o=c.stageF,
                      i=c.fmA[:, dc * T + tq * 512: dc * T + (tq + 1) * 512],
                      s=dcol[:, dc:dc + 1], j=y_acc[:, dc * 512:(dc + 1) * 512]:
                      e.scalar_tensor_tensor(o[:, :], i, s, j, ALU.mult, ALU.add),
                      waits=[pool_prev[0], pool_prev[1], lw6, c.stage_guard])
            evs = emit_fm_psum_to_resid(c, c.stageF, dc, tq, [fe])
            c.stage_guard = evs[-1]
            y_guard[dc] = fe
            fin_evs += evs



def emit_l1ffn(c, w1_d, w2_d, b1_d, b2_d, gw_d, gb_d, temp_d, ones_row_d, sel8_d, dbg=False):
    nc = c.nc
    p = c.p
    NHC2 = EH // 128
    h_half = c.sb("h_half", [128, 16 * 1024], BF16)
    out_acc = c.sb("out_acc", [128, NKC * 1024], F32)
    w1_sb = c.sb("w1_sb", [128, NKC * 512], BF16)
    w2_sb = c.sb("w2_sb", [128, 16 * 128], BF16)
    gw_sb = c.sb("gw_sb", [128, NKC * 8], BF16)
    gbr = c.sb("gbr", [128, 8], F32)
    onesr = c.sb("onesr", [1, 128], BF16)
    sel8 = c.sb("sel8_sb", [8, 8 * 128], BF16)
    b1sb = c.sb("b1sb", [8, EH], F32)
    b1T = c.sb("b1T", [128, NHC2 * 8], F32)
    b2sb = c.sb("b2sb", [8, D], BF16)
    itemp = c.sb("itemp", [1, 1], F32)
    itcol = c.sb("itcol", [128, 1], F32)
    gates = c.sb("gates", [128, 8], F32)
    m1c = c.sb("m1c", [128, 1], F32)
    m2c = c.sb("m2c", [128, 1], F32)
    eq1 = c.sb("eq1", [128, 8], F32)
    eq2 = c.sb("eq2", [128, 8], F32)
    s1c = c.sb("s1c", [128, 1], F32)
    s2c = c.sb("s2c", [128, 1], F32)
    tmp8 = c.sb("tmp8", [128, 8], F32)
    w_tok = c.sb("w_tok", [128, NTT * 8], F32)
    w_fm = c.sb("w_fm", [8, T], BF16)
    wrep = c.sb("wrep", [128, 1024], BF16)
    g_t = [c.sb("g_t0", [128, 512], BF16), c.sb("g_t1", [128, 512], BF16)]

    lo = p.op("sync", lambda e: e.dma_start(out=onesr[:, :], in_=ones_row_d[:, :]), dma=True)
    ls8 = p.op("sync", lambda e: e.dma_start(out=sel8[:, :], in_=sel8_d[:, :]), dma=True)
    lgw = p.op("gpsimd", lambda e, o=gw_sb[:, :].rearrange("p (k e) -> p k e", e=8),
               i=gw_d.rearrange("(k p) e -> p k e", p=128): e.dma_start(out=o, in_=i), dma=True)
    gbf32 = c.sb("gbf32", [1, 8], F32)
    lgb = p.op("sync", lambda e: e.dma_start(out=gbf32[:, :], in_=gb_d[:, :]), dma=True)
    ltm = p.op("sync", lambda e: e.dma_start(out=itemp[:, :], in_=temp_d[:, :]), dma=True)
    lb1 = p.op("sync", lambda e: e.dma_start(out=b1sb[:, :], in_=b1_d[:, :]), dma=True)
    lb2 = p.op("gpsimd", lambda e: e.dma_start(out=b2sb[:, :], in_=b2_d[:, :]), dma=True)
    gb_bf = c.sb("gb_bf", [1, 8], BF16)
    egb = p.op("vector", lambda e: e.tensor_copy(gb_bf[:, :], gbf32[:, :]), waits=[lgb])

    ln_evs = emit_ln_to_fm(c, [c.ev_x])
    p.fence()

    e1t = p.op("vector", lambda e: e.reciprocal(itemp[:, :], itemp[:, :]), waits=[ltm])
    it_bf = c.sb("it_bf", [1, 1], BF16)
    e2t = p.op("vector", lambda e: e.tensor_copy(it_bf[:, :], itemp[:, :]), waits=[e1t])
    mmt = p.op("tensor", lambda e: e.matmul(c.pF1.t[:, 0:1], onesr[:, :], it_bf[:, :], start=True, stop=True),
               waits=[e2t, lo, c.pF1.ev_r])
    c.pF1.ev_w = mmt
    e3t = p.op("vector", lambda e: e.tensor_copy(itcol[:, :], c.pF1.t[:, 0:1]), waits=[mmt])
    c.pF1.ev_r = c.confirm()
    mmg = p.op("tensor", lambda e: e.matmul(c.pF2.t[:, 0:8], onesr[:, :], gb_bf[:, :], start=True, stop=True),
               waits=[egb, lo, c.pF2.ev_r])
    c.pF2.ev_w = mmg
    e4t = p.op("vector", lambda e, s=itcol: e.tensor_scalar(gbr[:, :], c.pF2.t[:, 0:8], s[:, :], None, ALU.mult),
               waits=[mmg, e3t])
    c.pF2.ev_r = c.confirm()

    b1evs = []
    for hc in range(NHC2):
        pf = [c.pF1, c.pF2][hc % 2]
        tr = p.op("tensor", lambda e, o=pf.t[:, 0:8], i=b1sb[:, hc * 128:(hc + 1) * 128]:
                  e.transpose(o, i, c.ident_f[0:8, 0:8]),
                  waits=[lb1, c.ev_identf, pf.ev_r])
        pf.ev_w = tr
        ee = p.op("vector", lambda e, o=b1T[:, hc * 8:(hc + 1) * 8], i=pf.t[:, 0:8]:
                  e.tensor_copy(o, i), waits=[tr])
        pf.ev_r = c.confirm()
        b1evs.append(ee)

    p.fence()
    wtok_evs = []
    prev8 = None
    prev10 = None
    for tt in range(NTT):
        pb = [c.pC, c.pD][tt % 2]
        mm = None
        for kk in range(NKC):
            lh = c.fmA[:, kk * T + tt * 128: kk * T + (tt + 1) * 128]
            rh = gw_sb[:, kk * 8:(kk + 1) * 8]
            mm = p.op("tensor", lambda e, o=pb.t[:, 0:8], l=lh, r=rh, s=(kk == 0), z=(kk == NKC - 1):
                      e.matmul(o, l, r, start=s, stop=z),
                      waits=ln_evs + [lgw, pb.ev_r] if kk == 0 else [])
        pb.ev_w = mm
        ge = p.op("vector", lambda e, o=gates, i=pb.t[:, 0:8]:
                  e.scalar_tensor_tensor(o[:, :], i, itcol[:, :], gbr[:, :], ALU.mult, ALU.add),
                  waits=[mm, e4t, e3t] + ([c.dbg_gev] if dbg and getattr(c, "dbg_gev", None) else []))
        pb.ev_r = c.confirm()
        if dbg:
            if tt == 0:
                c.dbg_g = nc.dram_tensor("dbg_gates", [128, NTT * 8], F32, kind="ExternalOutput").ap()
                c.dbg_praw = nc.dram_tensor("dbg_praw", [128, NTT * 8], F32, kind="ExternalOutput").ap()
            dgr = p.op("vector", lambda e, o=c.tmp16[:, 0:8], i=pb.t[:, 0:8]: e.tensor_copy(o, i),
                       waits=[mm, getattr(c, "dbg_gev2", None)])
            c.dbg_gev = p.op("sync", lambda e, o=c.dbg_g[:, tt * 8:(tt + 1) * 8]: e.dma_start(out=o, in_=gates[:, :]),
                             waits=[ge], dma=True)
            c.dbg_gev2 = p.op("sync", lambda e, o=c.dbg_praw[:, tt * 8:(tt + 1) * 8]:
                              e.dma_start(out=o, in_=c.tmp16[:, 0:8]), waits=[dgr], dma=True)
        ee1 = p.op("vector", lambda e: e.reduce_max(m1c[:, :], gates[:, :], axis=AX.X), waits=[ge])
        ee2 = p.op("vector", lambda e: e.tensor_single_scalar(eq1[:, :], gates[:, :], m1c[:, :], ALU.is_ge),
                   waits=[ee1])
        ee3 = p.op("vector", lambda e: e.scalar_tensor_tensor(tmp8[:, :], eq1[:, :], -1e30, gates[:, :],
                                                             ALU.mult, ALU.add), waits=[ee2])
        ee4 = p.op("vector", lambda e: e.reduce_max(m2c[:, :], tmp8[:, :], axis=AX.X), waits=[ee3])
        ee5 = p.op("vector", lambda e: e.tensor_single_scalar(eq2[:, :], tmp8[:, :], m2c[:, :], ALU.is_ge),
                   waits=[ee4])
        ee6 = p.op("vector", lambda e: e.tensor_sub(c.halfcol[:, :], m1c[:, :], m2c[:, :]), waits=[ee4, prev8])
        ee7 = p.op("scalar", lambda e: e.activation(s1c[:, :], c.halfcol[:, :], AF.Sigmoid), waits=[ee6, prev10])
        ee8 = p.op("scalar", lambda e: e.activation(s2c[:, :], s1c[:, :], AF.Copy, scale=-1.0, bias=1.0),
                   waits=[ee7])
        ee9 = p.op("vector", lambda e: e.tensor_scalar(tmp8[:, :], eq1[:, :], s1c[:, :], None, ALU.mult),
                   waits=[ee7, ee5])
        ee10 = p.op("vector", lambda e, o=w_tok[:, tt * 8:(tt + 1) * 8]:
                    e.scalar_tensor_tensor(o, eq2[:, :], s2c[:, :], tmp8[:, :], ALU.mult, ALU.add),
                    waits=[ee8, ee9])
        prev8 = ee8
        prev10 = ee10
        wtok_evs.append(ee10)

    p.fence()
    if dbg:
        dwt = nc.dram_tensor("dbg_wtok", [128, NTT * 8], F32, kind="ExternalOutput").ap()
        p.op("sync", lambda e: e.dma_start(out=dwt[:, :], in_=w_tok[:, :]), waits=wtok_evs, dma=True)
        db1 = nc.dram_tensor("dbg_b1T", [128, NHC2 * 8], F32, kind="ExternalOutput").ap()
        p.op("sync", lambda e: e.dma_start(out=db1[:, :], in_=b1T[:, :]), waits=b1evs, dma=True)

    wfm_evs = []
    for tt in range(NTT):
        pf = [c.pF1, c.pF2][tt % 2]
        tr = p.op("tensor", lambda e, o=pf.t[0:8, 0:128], i=w_tok[:, tt * 8:(tt + 1) * 8]:
                  e.transpose(o, i, c.ident_f[:, :]),
                  waits=wtok_evs + [c.ev_identf, pf.ev_r])
        pf.ev_w = tr
        ee = p.op("vector", lambda e, o=w_fm[:, tt * 128:(tt + 1) * 128], i=pf.t[0:8, 0:128]:
                  e.tensor_copy(o, i), waits=[tr])
        pf.ev_r = ee
        wfm_evs.append(ee)

    p.fence()

    gate_done = ("Ev", p.cnt["Ev"])
    c.w1_guard = gate_done
    c.w2_guard = gate_done
    g_guard = [None, None]
    gel_ev = [None, None]
    for th in range(2):
        oacc_evs = []
        first_e = True
        for ei in range(NE):
            wr_evs = []
            for tsub in range(2):
                mmr = p.op("tensor", lambda e, o=c.pD.t[:, 0:512],
                           l=sel8[:, ei * 128:(ei + 1) * 128],
                           r=w_fm[:, th * 1024 + tsub * 512: th * 1024 + (tsub + 1) * 512],
                           s=True, z=True:
                           e.matmul(o, l, r, start=s, stop=z),
                           waits=wfm_evs + [ls8, c.pD.ev_r])
                c.pD.ev_w = mmr
                er = p.op("vector", lambda e, o=wrep[:, tsub * 512:(tsub + 1) * 512], i=c.pD.t[:, 0:512]:
                          e.tensor_copy(o, i), waits=[mmr])
                c.pD.ev_r = er
                wr_evs.append(er)
            for hh in range(2):
                ph_evs = []
                for hg in range(4):
                    hbase = hh * 2048 + hg * 512
                    src = w1_d[ei, :, hbase:hbase + 512].rearrange("(k p) h -> p k h", p=128)
                    lw1 = p.op("gpsimd", lambda e, o=w1_sb[:, :].rearrange("p (k h) -> p k h", h=512), i=src:
                               e.dma_start(out=o, in_=i), waits=[c.w1_guard], dma=True)
                    for hti in range(4):
                        hc = hh * 16 + hg * 4 + hti
                        hloc = hg * 4 + hti
                        for tsub in range(2):
                            pb = [c.pA, c.pB][tsub]
                            mm = None
                            for kk in range(NKC):
                                lh = w1_sb[:, kk * 512 + hti * 128: kk * 512 + (hti + 1) * 128]
                                rh = c.fmA[:, kk * T + th * 1024 + tsub * 512: kk * T + th * 1024 + (tsub + 1) * 512]
                                mm = p.op("tensor", lambda e, o=pb.t[:, 0:512], l=lh, r=rh,
                                          s=(kk == 0), z=(kk == NKC - 1):
                                          e.matmul(o, l, r, start=s, stop=z),
                                          waits=ln_evs + [lw1, pb.ev_r] if kk == 0 else [])
                            pb.ev_w = mm
                            c.w1_guard = mm
                            gl = p.op("scalar", lambda e, o=g_t[tsub], i=pb.t[:, 0:512],
                                      b=b1T[:, hc * 8 + ei: hc * 8 + ei + 1]:
                                      e.activation(o[:, :], i, AF.Gelu, bias=b),
                                      waits=b1evs + [pb.ev_w, g_guard[tsub]])
                            gel_ev[tsub] = gl
                            wm = p.op("vector", lambda e, o=h_half[:, hloc * 1024 + tsub * 512: hloc * 1024 + (tsub + 1) * 512],
                                      i=g_t[tsub], w=wrep[:, tsub * 512:(tsub + 1) * 512]:
                                      e.tensor_mul(o, i[:, :], w[:, :]),
                                      waits=wr_evs + [gl])
                            g_guard[tsub] = wm
                            pb.ev_r = wm
                            ph_evs.append(wm)
                for ot in range(NKC):
                    src2 = w2_d[ei, hh * 2048:(hh + 1) * 2048, ot * 128:(ot + 1) * 128].rearrange(
                        "(k p) o -> p k o", p=128)
                    lw2 = p.op("gpsimd", lambda e, o=w2_sb[:, :].rearrange("p (k o) -> p k o", o=128), i=src2:
                               e.dma_start(out=o, in_=i), waits=[c.w2_guard], dma=True)
                    for tsub in range(2):
                        pb = [c.pC, c.pD][tsub]
                        mm = None
                        for kk in range(16):
                            lh = w2_sb[:, kk * 128:(kk + 1) * 128]
                            rh = h_half[:, kk * 1024 + tsub * 512: kk * 1024 + (tsub + 1) * 512]
                            mm = p.op("tensor", lambda e, o=pb.t[:, 0:512], l=lh, r=rh,
                                      s=(kk == 0), z=(kk == 15):
                                      e.matmul(o, l, r, start=s, stop=z),
                                      waits=ph_evs + [lw2, pb.ev_r] if kk == 0 else [])
                        pb.ev_w = mm
                        c.w2_guard = mm
                        if first_e and hh == 0:
                            ea = p.op("vector", lambda e, o=out_acc[:, ot * 1024 + tsub * 512: ot * 1024 + (tsub + 1) * 512],
                                      i=pb.t[:, 0:512]: e.tensor_copy(o, i), waits=[mm])
                        else:
                            ea = p.op("vector", lambda e, o=out_acc[:, ot * 1024 + tsub * 512: ot * 1024 + (tsub + 1) * 512],
                                      i=pb.t[:, 0:512]: e.tensor_add(o, i, o), waits=[mm])
                        pb.ev_r = ea
                        oacc_evs.append(ea)
            first_e = False
        p.fence()
        for ot in range(NKC):
            for tsub in range(2):
                tq = th * 2 + tsub
                pb = [c.pA, c.pB][tsub]
                mm = p.op("tensor", lambda e, o=pb.t[:, 0:512],
                          l=b2sb[:, ot * 128:(ot + 1) * 128],
                          r=w_fm[:, tq * 512:(tq + 1) * 512], s=True, z=True:
                          e.matmul(o, l, r, start=s, stop=z),
                          waits=wfm_evs + [lb2, pb.ev_r])
                pb.ev_w = mm
                fe = p.op("vector", lambda e, o=c.stageF, i=pb.t[:, 0:512],
                          j=out_acc[:, ot * 1024 + tsub * 512: ot * 1024 + (tsub + 1) * 512]:
                          e.scalar_tensor_tensor(o[:, :], i, 1.0, j, ALU.mult, ALU.add),
                          waits=[mm] + oacc_evs)
                pb.ev_r = fe
                emit_fm_psum_to_resid(c, c.stageF, ot, tq, [fe])
        p.fence()



def _mk(nc_args=()):
    nc = bass.Bass("TRN2", target_bir_lowering=False)
    return nc


def _declare_common(nc):
    x_d = nc.dram_tensor("x", [T, D], F32, kind="ExternalInput").ap()
    identb_d = nc.dram_tensor("ident_bf", [128, 128], BF16, kind="ExternalInput").ap()
    identf_d = nc.dram_tensor("ident_f", [128, 128], F32, kind="ExternalInput").ap()
    out_d = nc.dram_tensor("xout", [T, D], F32, kind="ExternalOutput").ap()
    return x_d, identb_d, identf_d, out_d


def _decl_l0attn(nc):
    return (nc.dram_tensor("wqT", [D, D], F32, kind="ExternalInput").ap(),
            nc.dram_tensor("wkT", [D, D], F32, kind="ExternalInput").ap(),
            nc.dram_tensor("wvT", [D, D], F32, kind="ExternalInput").ap(),
            nc.dram_tensor("woT", [D, D], F32, kind="ExternalInput").ap(),
            nc.dram_tensor("mask4", [128, 4 * 512], BF16, kind="ExternalInput").ap(),
            nc.dram_tensor("ones_col", [128, 1], BF16, kind="ExternalInput").ap())


def _decl_l0ffn(nc):
    return (nc.dram_tensor("wgT", [D, FFH], I32, kind="ExternalInput").ap(),
            nc.dram_tensor("wuT", [D, FFH], I32, kind="ExternalInput").ap(),
            nc.dram_tensor("wdT", [FFH, D], I32, kind="ExternalInput").ap())


def _decl_l1attn(nc):
    return (nc.dram_tensor("wxp1T", [D, 80], F32, kind="ExternalInput").ap(),
            nc.dram_tensor("wxp2T", [D, 16], F32, kind="ExternalInput").ap(),
            nc.dram_tensor("wdtT", [DTR, D], F32, kind="ExternalInput").ap(),
            nc.dram_tensor("A_re", [128, 128], F32, kind="ExternalInput").ap(),
            nc.dram_tensor("b_dt_c", [128, 8], F32, kind="ExternalInput").ap(),
            nc.dram_tensor("D_c", [128, 8], F32, kind="ExternalInput").ap(),
            nc.dram_tensor("sel16", [16, 16 * 128], BF16, kind="ExternalInput").ap())


def _decl_l1ffn(nc):
    return (nc.dram_tensor("w1", [NE, D, EH], F32, kind="ExternalInput").ap(),
            nc.dram_tensor("w2", [NE, EH, D], F32, kind="ExternalInput").ap(),
            nc.dram_tensor("b1", [NE, EH], F32, kind="ExternalInput").ap(),
            nc.dram_tensor("b2", [NE, D], F32, kind="ExternalInput").ap(),
            nc.dram_tensor("gw", [D, NE], F32, kind="ExternalInput").ap(),
            nc.dram_tensor("gb", [1, NE], F32, kind="ExternalInput").ap(),
            nc.dram_tensor("temp", [1, 1], F32, kind="ExternalInput").ap(),
            nc.dram_tensor("ones_row", [1, 128], BF16, kind="ExternalInput").ap(),
            nc.dram_tensor("sel8", [8, 8 * 128], BF16, kind="ExternalInput").ap())


def _standalone(decl_fn, emit_fn, dbg=False):
    nc = bass.Bass("TRN2", target_bir_lowering=False)
    x_d, identb_d, identf_d, out_d = _declare_common(nc)
    drs = decl_fn(nc)
    with ExitStack() as stack:
        p = Plan(nc, stack)
        c = mk_ctx(nc, stack, p)
        alloc_common(c)
        emit_load_consts(c, identb_d, identf_d)
        emit_load_x(c, x_d)
        with ExitStack() as bst:
            c.stacks.append(bst)
            emit_fn(c, *drs, dbg=dbg)
            p.fence()
            c.stacks.pop()
        emit_store_x(c, out_d)
        p.wait("gpsimd", p.now())
        with nc.Block() as block:
            p.build(block)
    return nc


def build_l0attn(dbg=False):
    return _standalone(_decl_l0attn, emit_l0attn, dbg)


def build_l0ffn(dbg=False):
    return _standalone(_decl_l0ffn, emit_l0ffn, dbg)


def build_l1attn(dbg=False):
    return _standalone(_decl_l1attn, emit_l1attn, dbg)


def build_l1ffn(dbg=False):
    return _standalone(_decl_l1ffn, emit_l1ffn, dbg)


def build_fused():
    nc = bass.Bass("TRN2", target_bir_lowering=False)
    x_d, identb_d, identf_d, out_d = _declare_common(nc)
    dr0a = _decl_l0attn(nc)
    dr0f = _decl_l0ffn(nc)
    dr1a = _decl_l1attn(nc)
    dr1f = _decl_l1ffn(nc)
    with ExitStack() as stack:
        p = Plan(nc, stack)
        c = mk_ctx(nc, stack, p)
        alloc_common(c)
        emit_load_consts(c, identb_d, identf_d)
        emit_load_x(c, x_d)
        for emit_fn, drs in ((emit_l0attn, dr0a), (emit_l0ffn, dr0f),
                             (emit_l1attn, dr1a), (emit_l1ffn, dr1f)):
            with ExitStack() as bst:
                c.stacks.append(bst)
                emit_fn(c, *drs)
                p.fence()
                c.stacks.pop()
        emit_store_x(c, out_d)
        p.wait("gpsimd", p.now())
        with nc.Block() as block:
            p.build(block)
    return nc

def run_l1ffn(x_full, W1, b1, W2, b2, gate_w, gate_b, temperature):
    nc = build_l1ffn()
    ident_bf, ident_f, _, _ = host_consts()
    sel8 = np.zeros((8, 8 * 128), dtype=BF)
    for e in range(8):
        sel8[e, e * 128:(e + 1) * 128] = 1.0
    ones_row = np.ones((1, 128), dtype=BF)
    in_maps = []
    for i in range(8):
        in_maps.append(dict(x=np.ascontiguousarray(x_full[i]),
                            w1=np.asarray(W1), w2=np.asarray(W2),
                            b1=np.asarray(b1), b2=np.asarray(b2),
                            gw=np.asarray(gate_w), gb=np.asarray(gate_b).reshape(1, 8),
                            temp=np.asarray(temperature).reshape(1, 1),
                            ones_row=ones_row, sel8=sel8,
                            ident_bf=ident_bf, ident_f=ident_f))
    res = run_bass_kernel_spmd(nc, in_maps, core_ids=list(range(8)))
    return np.stack([r["xout"] for r in res.results])



def host_consts():
    ident_bf = np.eye(128, dtype=BF)
    ident_f = np.eye(128, dtype=np.float32)
    mask4 = np.zeros((128, 4 * 512), dtype=BF)
    for rel in range(4):
        r = np.arange(128)[:, None]
        j = np.arange(512)[None, :]
        mask4[:, rel * 512:(rel + 1) * 512] = (j >= r + rel * 128).astype(BF)
    ones_col = np.ones((128, 1), dtype=BF)
    return ident_bf, ident_f, mask4, ones_col


def run_l0attn(x_full, Wq, Wk, Wv, Wo):
    nc = build_l0attn()
    ident_bf, ident_f, mask4, ones_col = host_consts()
    wqT = np.ascontiguousarray(Wq.T)
    wkT = np.ascontiguousarray(Wk.T)
    wvT = np.ascontiguousarray(Wv.T)
    woT = np.ascontiguousarray(Wo.T)
    in_maps = []
    for i in range(8):
        in_maps.append(dict(x=np.ascontiguousarray(x_full[i]), wqT=wqT, wkT=wkT, wvT=wvT, woT=woT,
                            ident_bf=ident_bf, ident_f=ident_f, mask4=mask4, ones_col=ones_col))
    res = run_bass_kernel_spmd(nc, in_maps, core_ids=list(range(8)))
    return np.stack([r["xout"] for r in res.results])


def run_l0ffn(x_full, w_gate, w_up, w_down):
    nc = build_l0ffn()
    ident_bf, ident_f, _, _ = host_consts()
    wgT = np.ascontiguousarray(w_gate.T)
    wuT = np.ascontiguousarray(w_up.T)
    wdT = np.ascontiguousarray(w_down.T)
    in_maps = []
    for i in range(8):
        in_maps.append(dict(x=np.ascontiguousarray(x_full[i]), wgT=wgT, wuT=wuT, wdT=wdT,
                            ident_bf=ident_bf, ident_f=ident_f))
    res = run_bass_kernel_spmd(nc, in_maps, core_ids=list(range(8)))
    return np.stack([r["xout"] for r in res.results])


def _l1attn_host(A_log, Dp, W_xproj, W_dt, b_dt):
    wxpT = np.ascontiguousarray(np.asarray(W_xproj).T)
    sel16 = np.zeros((16, 16 * 128), dtype=BF)
    for n in range(16):
        sel16[n, n * 128:(n + 1) * 128] = 1.0
    return dict(
        wxp1T=np.ascontiguousarray(wxpT[:, 0:80]),
        wxp2T=np.ascontiguousarray(wxpT[:, 80:96]),
        wdtT=np.ascontiguousarray(np.asarray(W_dt).T),
        A_re=np.ascontiguousarray(np.asarray(A_log).reshape(8, 128, 16).transpose(1, 0, 2).reshape(128, 128)),
        b_dt_c=np.ascontiguousarray(np.asarray(b_dt).reshape(8, 128).T),
        D_c=np.ascontiguousarray(np.asarray(Dp).reshape(8, 128).T),
        sel16=sel16)


def run_l1attn(x_full, A_log, Dp, W_xproj, W_dt, b_dt):
    nc = build_l1attn()
    ident_bf, ident_f, _, _ = host_consts()
    hm = _l1attn_host(A_log, Dp, W_xproj, W_dt, b_dt)
    in_maps = []
    for i in range(8):
        in_maps.append(dict(x=np.ascontiguousarray(x_full[i]), ident_bf=ident_bf, ident_f=ident_f, **hm))
    res = run_bass_kernel_spmd(nc, in_maps, core_ids=list(range(8)))
    return np.stack([r["xout"] for r in res.results])


def _sel8_host():
    sel8 = np.zeros((8, 8 * 128), dtype=BF)
    for e in range(8):
        sel8[e, e * 128:(e + 1) * 128] = 1.0
    return sel8


def kernel(**inputs):
    g = lambda k: np.asarray(inputs[k])
    x = np.asarray(inputs["L0_attn_x"], dtype=np.float32)
    nc = build_fused()
    ident_bf, ident_f, mask4, ones_col = host_consts()
    hm1a = _l1attn_host(g("L1_attn_A_log"), g("L1_attn_D"), g("L1_attn_W_xproj"),
                        g("L1_attn_W_dt"), g("L1_attn_b_dt"))
    shared = dict(
        ident_bf=ident_bf, ident_f=ident_f, mask4=mask4, ones_col=ones_col,
        wqT=np.ascontiguousarray(g("L0_attn_Wq").T), wkT=np.ascontiguousarray(g("L0_attn_Wk").T),
        wvT=np.ascontiguousarray(g("L0_attn_Wv").T), woT=np.ascontiguousarray(g("L0_attn_Wo").T),
        wgT=np.ascontiguousarray(g("L0_ffn_w_gate").T), wuT=np.ascontiguousarray(g("L0_ffn_w_up").T),
        wdT=np.ascontiguousarray(g("L0_ffn_w_down").T),
        w1=g("L1_ffn_W1"), w2=g("L1_ffn_W2"), b1=g("L1_ffn_b1"), b2=g("L1_ffn_b2"),
        gw=g("L1_ffn_gate_w"), gb=g("L1_ffn_gate_b").reshape(1, 8),
        temp=g("L1_ffn_temperature").reshape(1, 1),
        ones_row=np.ones((1, 128), dtype=BF), sel8=_sel8_host(),
        **hm1a)
    in_maps = [dict(x=np.ascontiguousarray(x[i]), **shared) for i in range(8)]
    res = run_bass_kernel_spmd(nc, in_maps, core_ids=list(range(8)))
    return np.stack([r["xout"] for r in res.results])


# version 03
# speedup vs baseline: 1.1707x; 1.1707x over previous
import numpy as np
import ml_dtypes
from contextlib import ExitStack
import concourse.bass as bass
import concourse.mybir as mybir
from concourse.bass_utils import run_bass_kernel_spmd

F32 = mybir.dt.float32
BF16 = mybir.dt.bfloat16
I32 = mybir.dt.int32
AF = mybir.ActivationFunctionType
ALU = mybir.AluOpType
AX = mybir.AxisListType
BF = ml_dtypes.bfloat16

T, D = 2048, 1024
NTT = T // 128
NKC = D // 128
NTQ = T // 512
NH, DH = 16, 64
FFH = 3072
NE, EH = 8, 4096
DST, DTR = 16, 64
LN_EPS = 1e-5
QSCALE = 1.0 / 8.0
DEQ = 1.0 / 127.0

ENGINES = ("tensor", "vector", "scalar", "gpsimd", "sync")
ENG_EV = {"tensor": "Et", "vector": "Ev", "scalar": "Es", "gpsimd": "Eg", "sync": "Ey"}
DMA_SLOTS = {"gpsimd": 8, "sync": 6, "scalar": 2}
DMA_PREFIX = {"gpsimd": "Dg", "sync": "Dy", "scalar": "Da"}


class Plan:

    def __init__(self, nc, stack):
        self.nc = nc
        self.q = {e: [] for e in ENGINES}
        self.sems = {}
        self.cnt = {}
        self.rr = {e: 0 for e in DMA_SLOTS}
        for s in ENG_EV.values():
            self.sems[s] = stack.enter_context(nc.semaphore(s))
            self.cnt[s] = 0
        for e, n in DMA_SLOTS.items():
            for i in range(n):
                s = f"{DMA_PREFIX[e]}{i}"
                self.sems[s] = stack.enter_context(nc.semaphore(s))
                self.cnt[s] = 0

    def op(self, eng, fn, waits=(), dma=False):
        waits = [w for w in waits if w is not None]
        if dma:
            slot = self.rr[eng]
            self.rr[eng] = (slot + 1) % DMA_SLOTS[eng]
            sem = f"{DMA_PREFIX[eng]}{slot}"
            amt = 16
            if self.cnt[sem] > 0:
                waits = waits + [(sem, self.cnt[sem])]
        else:
            sem = ENG_EV[eng]
            amt = 1
            if self.cnt[sem] > 0:
                waits = waits + [(sem, self.cnt[sem])]
        self.cnt[sem] += amt
        ev = (sem, self.cnt[sem])
        wmax = {}
        for s, v in waits:
            if v > wmax.get(s, -1):
                wmax[s] = v
        sems = self.sems

        def g(e, fn=fn, wmax=wmax, sem=sem, amt=amt):
            for s, v in wmax.items():
                e.wait_ge(sems[s], v)
            fn(e).then_inc(sems[sem], amt)
        self.q[eng].append(g)
        return ev

    def wait(self, eng, waits):
        wmax = {}
        for wv in waits:
            if wv is None:
                continue
            s, v = wv
            if v > wmax.get(s, -1):
                wmax[s] = v
        sems = self.sems

        def g(e, wmax=wmax):
            for s, v in wmax.items():
                e.wait_ge(sems[s], v)
        self.q[eng].append(g)

    def now(self):
        return [(s, v) for s, v in self.cnt.items() if v > 0]

    def fence(self):
        evs = self.now()
        for e in ENGINES:
            self.wait(e, evs)

    def build(self, block):
        for name in ENGINES:
            dec = getattr(block, name)
            fns = self.q[name]

            def body(e, fns=fns):
                for f in fns:
                    f(e)
            dec(body)


class PBuf:

    def __init__(self, t):
        self.t = t
        self.ev_w = None
        self.ev_r = None


class Ctx:
    pass


def mk_ctx(nc, stack, p):
    c = Ctx()
    c.nc = nc
    c.st = stack
    c.p = p
    c.stacks = [stack]

    def sb(name, shape, dt):
        return c.stacks[-1].enter_context(nc.sbuf_tensor(name, shape, dt))
    c.sb = sb

    def ps(name, shape, dt):
        return PBuf(stack.enter_context(nc.psum_tensor(name, shape, dt)))
    c.ps = ps

    c.dummy = c.sb("dummy_cf", [1, 16], F32)
    p.op("vector", lambda e: e.memset(c.dummy[:, :], 0.0))

    def confirm(eng="vector"):
        if eng == "scalar":
            return p.op(eng, lambda e: e.copy(c.dummy[0:1, 8:16], c.dummy[0:1, 0:8]))
        return p.op(eng, lambda e: e.tensor_copy(c.dummy[0:1, 8:16], c.dummy[0:1, 0:8]))
    c.confirm = confirm
    return c



def alloc_common(c, with_fmA=True):
    c.xr = c.sb("xr", [128, NTT * D], F32)
    if with_fmA:
        c.fmA = c.sb("fmA", [128, NKC * T], BF16)
    c.ident_bf = c.sb("identb_sb", [128, 128], BF16)
    c.ident_f = c.sb("identf_sb", [128, 128], F32)
    c.stat1 = c.sb("stat1", [128, NTT], F32)
    c.stat2 = c.sb("stat2", [128, NTT], F32)
    c.mcols = c.sb("mcols", [128, NTT], F32)
    c.rcols = c.sb("rcols", [128, NTT], F32)
    c.tmp16 = c.sb("tmp16", [128, NTT], F32)
    c.halfcol = c.sb("halfcol", [128, 1], F32)
    c.ln_scr = c.sb("ln_scr", [128, D], BF16)
    c.stageF = c.sb("stageF", [128, 512], F32)
    c.pA = c.ps("pA", [128, 512], F32)
    c.pB = c.ps("pB", [128, 512], F32)
    c.pC = c.ps("pC", [128, 512], F32)
    c.pD = c.ps("pD", [128, 512], F32)
    c.pT1 = c.ps("pT1", [128, 1024], BF16)
    c.pT2 = c.ps("pT2", [128, 1024], BF16)
    c.pF1 = c.ps("pF1", [128, 512], F32)
    c.pF2 = c.ps("pF2", [128, 512], F32)


def emit_load_consts(c, ident_bf_d, ident_f_d):
    p = c.p
    c.ev_ident = p.op("sync", lambda e, o=c.ident_bf, i=ident_bf_d: e.dma_start(out=o[:, :], in_=i[:, :]), dma=True)
    c.ev_identf = p.op("sync", lambda e, o=c.ident_f, i=ident_f_d: e.dma_start(out=o[:, :], in_=i[:, :]), dma=True)


def emit_load_x(c, x_d):
    p = c.p
    src = x_d.rearrange("(tt p) d -> p tt d", p=128)
    dst = c.xr[:, :].rearrange("p (tt d) -> p tt d", d=D)
    c.ev_x = p.op("sync", lambda e, o=dst, i=src: e.dma_start(out=o, in_=i), dma=True)


def emit_store_x(c, out_d, waits=None):
    p = c.p
    if waits is None:
        waits = p.now()
    dst = out_d.rearrange("(tt p) d -> p tt d", p=128)
    src = c.xr[:, :].rearrange("p (tt d) -> p tt d", d=D)
    ev = p.op("sync", lambda e, o=dst, i=src: e.dma_start(out=o, in_=i), waits=waits, dma=True)
    return ev


def emit_ln_stats(c, deps):
    p = c.p
    evs = []
    prev_hc = None
    for tt in range(NTT):
        xv = c.xr[:, tt * D:(tt + 1) * D]
        e1 = p.op("vector", lambda e, o=c.stat1[:, tt:tt + 1], i=xv: e.reduce_sum(o, i, axis=AX.X), waits=deps)
        e2a = p.op("scalar", lambda e, o=c.stageF, i=c.xr[:, tt * D: tt * D + 512], a=c.stat2[:, tt:tt + 1]:
                   e.activation(o[:, :], i, AF.Square, accum_out=a), waits=deps)
        e2b = p.op("scalar", lambda e, o=c.stageF, i=c.xr[:, tt * D + 512: (tt + 1) * D], a=c.halfcol:
                   e.activation(o[:, :], i, AF.Square, accum_out=a[:, :]), waits=deps + [prev_hc])
        e2c = p.op("vector", lambda e, o=c.stat2[:, tt:tt + 1]:
                   e.tensor_add(o, o, c.halfcol[:, :]), waits=[e2a, e2b])
        prev_hc = e2c
        evs += [e1, e2c]
    em = p.op("vector", lambda e: e.tensor_scalar_mul(c.mcols[:, :], c.stat1[:, :], 1.0 / D), waits=evs)
    ev = p.op("vector", lambda e: e.tensor_scalar_mul(c.rcols[:, :], c.stat2[:, :], 1.0 / D), waits=evs)
    e3 = p.op("vector", lambda e: e.tensor_mul(c.tmp16[:, :], c.mcols[:, :], c.mcols[:, :]), waits=[em])
    e4 = p.op("vector", lambda e: e.tensor_sub(c.rcols[:, :], c.rcols[:, :], c.tmp16[:, :]), waits=[ev, e3])
    e5 = p.op("vector", lambda e: e.tensor_scalar_add(c.rcols[:, :], c.rcols[:, :], LN_EPS), waits=[e4])
    e6 = p.op("vector", lambda e: e.reciprocal(c.tmp16[:, :], c.rcols[:, :]), waits=[e5])
    e7 = p.op("scalar", lambda e: e.activation(c.rcols[:, :], c.tmp16[:, :], AF.Sqrt), waits=[e6])
    return [em, e7]


def emit_ln_to_fm(c, deps, dst=None, dst_stride=T):
    p = c.p
    if dst is None:
        dst = c.fmA
    sevs = emit_ln_stats(c, deps)
    out_evs = []
    pts = [c.pT1, c.pT2]
    prev_tr = None
    for tt in range(NTT):
        xv = c.xr[:, tt * D:(tt + 1) * D]
        ea = p.op("vector", lambda e, o=c.ln_scr, i=xv, m=c.mcols[:, tt:tt + 1], r=c.rcols[:, tt:tt + 1]:
                  e.tensor_scalar(o[:, :], i, m, r, ALU.subtract, ALU.mult),
                  waits=sevs + deps + [prev_tr])
        pt = pts[tt % 2]
        tr = None
        for k in range(NKC):
            tr = p.op("tensor", lambda e, o=pt.t[:, k * 128:(k + 1) * 128], i=c.ln_scr[:, k * 128:(k + 1) * 128],
                      idn=c.ident_bf: e.transpose(o, i, idn[:, :]),
                      waits=[ea, c.ev_ident, pt.ev_r] if k == 0 else [ea])
        pt.ev_w = tr
        prev_tr = tr
        dv = dst[:, :].rearrange("p (k r) -> p k r", r=dst_stride)[:, :, tt * 128:(tt + 1) * 128]
        sv = pt.t[:, :].rearrange("p (k r) -> p k r", r=128)
        ee = p.op("vector", lambda e, o=dv, i=sv: e.tensor_copy(o, i), waits=[tr])
        pt.ev_r = c.confirm()
        out_evs.append(ee)
    return out_evs


def emit_fm_psum_to_resid(c, src_sb_f32, ot, tq, deps, scale=None):
    p = c.p
    evs = []
    pfs = [c.pF1, c.pF2]
    for j in range(4):
        pf = pfs[j % 2]
        tt = tq * 4 + j
        tr = p.op("tensor", lambda e, o=pf.t[:, 0:128], i=src_sb_f32[:, j * 128:(j + 1) * 128], idn=c.ident_f:
                  e.transpose(o, i, idn[:, :]), waits=deps + [c.ev_identf, pf.ev_r])
        pf.ev_w = tr
        xv = c.xr[:, tt * D + ot * 128: tt * D + (ot + 1) * 128]
        ee = p.op("vector", lambda e, o=xv, i=pf.t[:, 0:128], x=xv:
                  e.tensor_add(o, i, x), waits=[tr])
        pf.ev_r = c.confirm()
        evs.append(ee)
    return evs



def emit_l0attn(c, wq_d, wk_d, wv_d, wo_d, mask_d, ones_d, dbg=False):
    nc = c.nc
    p = c.p
    mask = c.sb("mask_sb", [128, 4 * 512], BF16)
    ones_col = c.sb("ones_sb", [128, 1], BF16)
    ev_mask = p.op("sync", lambda e: e.dma_start(out=mask[:, :], in_=mask_d[:, :]), dma=True)
    ev_ones = p.op("sync", lambda e: e.dma_start(out=ones_col[:, :], in_=ones_d[:, :]), dma=True)

    wbuf = [c.sb("wbuf0", [128, NKC * 128], BF16),
            c.sb("wbuf1", [128, NKC * 128], BF16)]
    wobuf = c.sb("wobuf", [128, NKC * D], BF16)
    qc = c.sb("qc", [128, T], BF16)
    kc = c.sb("kc", [128, T], BF16)
    vc = c.sb("vc", [128, NTT * 128], BF16)
    ptb = c.sb("ptb", [128, 16 * 512], BF16)
    att = c.sb("att", [128, NTT * 128], BF16)
    inv = c.sb("inv", [128, 1], F32)

    ln_evs = emit_ln_to_fm(c, [c.ev_x])
    if dbg:
        dfm = nc.dram_tensor("dbg_fm", [128, NKC * T], BF16, kind="ExternalOutput").ap()
        p.op("sync", lambda e: e.dma_start(out=dfm[:, :], in_=c.fmA[:, :]), waits=ln_evs, dma=True)

    qkv_seq = [(kc_i, pi) for kc_i in range(NKC) for pi in range(3)]
    qkv_evs = {}
    qkv_guardb = [None, None]
    qkv_next = [0]

    def issue_qkv():
        i = qkv_next[0]
        if i >= len(qkv_seq):
            return
        qkv_next[0] = i + 1
        kc_i, pi = qkv_seq[i]
        par = i % 2
        w_d = (wq_d, wk_d, wv_d)[pi]
        src = w_d[:, kc_i * 128:(kc_i + 1) * 128].rearrange("(k p) o -> p k o", p=128)
        qkv_evs[i] = p.op("gpsimd", lambda e, o=wbuf[par][:, :].rearrange("p (k o) -> p k o", o=128), i_=src:
                          e.dma_start(out=o, in_=i_), waits=[qkv_guardb[par]], dma=True)

    issue_qkv()
    issue_qkv()
    qkv_i = [0]
    lwo = p.op("gpsimd", lambda e, o=wobuf[:, :].rearrange("p (k o) -> p k o", o=D),
               i_=wo_d.rearrange("(k p) o -> p k o", p=128): e.dma_start(out=o, in_=i_), dma=True)

    wo_stage_deps = []
    attn_fm = c.sb("attn_fm", [128, NKC * T], BF16)

    for kc_i in range(NKC):
        prev_evq = []
        for (w_d, dstbuf, isv) in ((wq_d, qc, False), (wk_d, kc, False), (wv_d, vc, True)):
            wi = qkv_i[0]
            qkv_i[0] = wi + 1
            wpar = wi % 2
            lw = qkv_evs[wi]
            if not isv:
                for tq in range(NTQ):
                    pb = [c.pA, c.pB][tq % 2]
                    mws = ln_evs + [lw, pb.ev_r]
                    mm = None
                    for kk in range(NKC):
                        lh = wbuf[wpar][:, kk * 128:(kk + 1) * 128]
                        rh = c.fmA[:, kk * T + tq * 512: kk * T + (tq + 1) * 512]
                        mm = p.op("tensor", lambda e, o=pb.t[:, 0:512], l=lh, r=rh,
                                  s=(kk == 0), z=(kk == NKC - 1):
                                  e.matmul(o, l, r, start=s, stop=z),
                                  waits=mws if kk == 0 else [])
                    pb.ev_w = mm
                    qkv_guardb[wpar] = mm
                    ee = p.op("vector", lambda e, o=dstbuf[:, tq * 512:(tq + 1) * 512], i=pb.t[:, 0:512]:
                              e.tensor_copy(o, i), waits=[pb.ev_w])
                    pb.ev_r = ee
                    prev_evq.append(ee)
            else:
                for tt in range(NTT):
                    pb = [c.pA, c.pB][tt % 2]
                    mws = ln_evs + [lw, pb.ev_r]
                    mm = None
                    for kk in range(NKC):
                        lh = c.fmA[:, kk * T + tt * 128: kk * T + (tt + 1) * 128]
                        rh = wbuf[wpar][:, kk * 128:(kk + 1) * 128]
                        mm = p.op("tensor", lambda e, o=pb.t[:, 0:128], l=lh, r=rh,
                                  s=(kk == 0), z=(kk == NKC - 1):
                                  e.matmul(o, l, r, start=s, stop=z),
                                  waits=mws if kk == 0 else [])
                    pb.ev_w = mm
                    qkv_guardb[wpar] = mm
                    ee = p.op("vector", lambda e, o=dstbuf[:, tt * 128:(tt + 1) * 128], i=pb.t[:, 0:128]:
                              e.tensor_copy(o, i), waits=[pb.ev_w])
                    pb.ev_r = ee
                    prev_evq.append(ee)
            issue_qkv()
        qkv_done = list(prev_evq)
        if dbg and kc_i == 0:
            dq = nc.dram_tensor("dbg_q", [128, T], BF16, kind="ExternalOutput").ap()
            dk = nc.dram_tensor("dbg_k", [128, T], BF16, kind="ExternalOutput").ap()
            dv = nc.dram_tensor("dbg_v", [128, NTT * 128], BF16, kind="ExternalOutput").ap()
            p.op("sync", lambda e: e.dma_start(out=dq[:, :], in_=qc[:, :]), waits=qkv_done, dma=True)
            p.op("sync", lambda e: e.dma_start(out=dk[:, :], in_=kc[:, :]), waits=qkv_done, dma=True)
            p.op("sync", lambda e: e.dma_start(out=dv[:, :], in_=vc[:, :]), waits=qkv_done, dma=True)

        att_evs = []
        for hh in range(2):
            hbase = hh * 64
            for sq in range(NTQ):
                nkt = 4 * sq + 4
                exp_evs = []
                for kt in range(nkt):
                    pb = [c.pA, c.pB][kt % 2]
                    mm = p.op("tensor", lambda e, o=pb.t[:, 0:512],
                              l=kc[hbase:hbase + 64, kt * 128:(kt + 1) * 128],
                              r=qc[hbase:hbase + 64, sq * 512:(sq + 1) * 512]:
                              e.matmul(o, l, r, start=True, stop=True),
                              waits=qkv_done + [pb.ev_r])
                    pb.ev_w = mm
                    ex = p.op("scalar", lambda e, o=ptb[:, kt * 512:(kt + 1) * 512], i=pb.t[:, 0:512]:
                              e.activation(o, i, AF.Exp, scale=QSCALE), waits=[mm])
                    pb.ev_r = ex
                    exp_evs.append(ex)
                    if kt >= 4 * sq:
                        rel = kt - 4 * sq
                        me = p.op("vector", lambda e, o=ptb[:, kt * 512:(kt + 1) * 512],
                                  m=mask[:, rel * 512:(rel + 1) * 512]:
                                  e.tensor_mul(o, o, m), waits=[ex, ev_mask])
                        exp_evs.append(me)
                for si in range(4):
                    stile = sq * 4 + si
                    oevs = []
                    for kt in range(nkt):
                        lh = ptb[:, kt * 512 + si * 128: kt * 512 + (si + 1) * 128]
                        mm1 = p.op("tensor", lambda e, o=c.pC.t[:, 0:64], l=lh,
                                   r=vc[:, kt * 128 + hbase: kt * 128 + hbase + 64],
                                   s=(kt == 0), z=(kt == nkt - 1):
                                   e.matmul(o, l, r, start=s, stop=z),
                                   waits=exp_evs + [c.pC.ev_r] if kt == 0 else [])
                        mm2 = p.op("tensor", lambda e, o=c.pD.t[:, 0:1], l=lh,
                                   r=ones_col[:, 0:1],
                                   s=(kt == 0), z=(kt == nkt - 1):
                                   e.matmul(o, l, r, start=s, stop=z),
                                   waits=[ev_ones, c.pD.ev_r] if kt == 0 else [])
                        oevs += [mm1, mm2]
                    c.pC.ev_w = oevs[-2]
                    c.pD.ev_w = oevs[-1]
                    rc = p.op("vector", lambda e: e.reciprocal(inv[:, :], c.pD.t[:, 0:1]), waits=[c.pD.ev_w])
                    c.pD.ev_r = rc
                    ee = p.op("vector", lambda e, o=att[:, stile * 128 + hbase: stile * 128 + hbase + 64],
                              i=c.pC.t[:, 0:64]:
                              e.tensor_scalar(o, i, inv[:, :], None, ALU.mult), waits=[rc, c.pC.ev_w])
                    c.pC.ev_r = ee
                    att_evs.append(ee)
        if dbg and kc_i == 0:
            da = nc.dram_tensor("dbg_att", [128, NTT * 128], BF16, kind="ExternalOutput").ap()
            p.op("sync", lambda e: e.dma_start(out=da[:, :], in_=att[:, :]), waits=att_evs, dma=True)
        if dbg and kc_i == 7:
            da7 = nc.dram_tensor("dbg_att7", [128, NTT * 128], BF16, kind="ExternalOutput").ap()
            p.op("sync", lambda e: e.dma_start(out=da7[:, :], in_=att[:, :]), waits=att_evs, dma=True)
        pts = [c.pT1, c.pT2]
        for tt in range(NTT):
            pt = pts[tt % 2]
            tr = p.op("tensor", lambda e, o=pt.t[:, 0:128], i=att[:, tt * 128:(tt + 1) * 128], idn=c.ident_bf:
                      e.transpose(o, i, idn[:, :]), waits=att_evs + [c.ev_ident, pt.ev_r])
            pt.ev_w = tr
            ee = p.op("vector", lambda e, o=attn_fm[:, kc_i * T + tt * 128: kc_i * T + (tt + 1) * 128],
                      i=pt.t[:, 0:128]: e.tensor_copy(o, i), waits=[tr])
            pt.ev_r = ee
            wo_stage_deps.append(ee)

    if dbg:
        dafm = nc.dram_tensor("dbg_attnfm", [128, NKC * T], BF16, kind="ExternalOutput").ap()
        p.op("sync", lambda e: e.dma_start(out=dafm[:, :], in_=attn_fm[:, :]), waits=wo_stage_deps, dma=True)
    for tt in range(NTT):
        for oh in range(2):
            pb = [c.pA, c.pB][(tt * 2 + oh) % 2]
            mm = None
            for kk in range(NKC):
                lh = attn_fm[:, kk * T + tt * 128: kk * T + (tt + 1) * 128]
                rh = wobuf[:, kk * D + oh * 512: kk * D + (oh + 1) * 512]
                mm = p.op("tensor", lambda e, o=pb.t[:, 0:512], l=lh, r=rh,
                          s=(kk == 0), z=(kk == NKC - 1):
                          e.matmul(o, l, r, start=s, stop=z),
                          waits=wo_stage_deps + [lwo, pb.ev_r] if kk == 0 else [])
            pb.ev_w = mm
            xv = c.xr[:, tt * D + oh * 512: tt * D + (oh + 1) * 512]
            ec = p.op("vector", lambda e, o=xv, i=pb.t[:, 0:512]:
                      e.tensor_add(o, i, o), waits=[pb.ev_w])
            pb.ev_r = c.confirm()



def emit_l0ffn(c, wg_d, wu_d, wd_d, dbg=False):
    nc = c.nc
    p = c.p
    NHC = FFH // 128
    NHG = FFH // 512
    h_sb = c.sb("h_sb", [128, NHC * 512], BF16)
    g_grp = c.sb("g_grp", [128, 4 * 512], BF16)
    wd_sb = c.sb("wd_sb", [128, NHC * D], BF16)
    wgu_sb = [c.sb("wgu_sb0", [128, NKC * 512], BF16),
              c.sb("wgu_sb1", [128, NKC * 512], BF16)]

    ln_evs = emit_ln_to_fm(c, [c.ev_x])

    wdv = wd_d.rearrange("(k p) o -> p k o", p=128)
    lwd = p.op("gpsimd", lambda e, o=wd_sb[:, :].rearrange("p (k o) -> p k o", o=D), i=wdv:
               e.dma_start(out=o, in_=i), dma=True)

    g_guard = [None] * 4
    sil_ev = [None] * 4
    gu_seq = [(tq, hg, gi) for tq in range(NTQ) for hg in range(NHG) for gi in range(2)]
    gu_evs = {}
    gu_guardb = [None, None]
    gu_next = [0]

    def issue_gu():
        i = gu_next[0]
        if i >= len(gu_seq):
            return
        gu_next[0] = i + 1
        _tq, hg, gi = gu_seq[i]
        par = i % 2
        w_d = (wg_d, wu_d)[gi]
        src = w_d[:, hg * 512:(hg + 1) * 512].rearrange("(k p) h -> p k h", p=128)
        gu_evs[i] = p.op("gpsimd", lambda e, o=wgu_sb[par][:, :].rearrange("p (k h) -> p k h", h=512), i_=src:
                         e.dma_start(out=o, in_=i_), waits=[gu_guardb[par]], dma=True)

    issue_gu()
    issue_gu()
    gu_i = [0]
    for tq in range(NTQ):
        hevs = []
        for hg in range(NHG):
            for (w_d, is_gate) in ((wg_d, True), (wu_d, False)):
                gi_idx = gu_i[0]
                gu_i[0] = gi_idx + 1
                par = gi_idx % 2
                lw = gu_evs[gi_idx]
                for hti in range(4):
                    ht = hg * 4 + hti
                    pb = c.pA if is_gate else c.pB
                    mm = None
                    for kk in range(NKC):
                        lh = wgu_sb[par][:, kk * 512 + hti * 128: kk * 512 + (hti + 1) * 128]
                        rh = c.fmA[:, kk * T + tq * 512: kk * T + (tq + 1) * 512]
                        mm = p.op("tensor", lambda e, o=pb.t[:, 0:512], l=lh, r=rh,
                                  s=(kk == 0), z=(kk == NKC - 1):
                                  e.matmul(o, l, r, start=s, stop=z),
                                  waits=ln_evs + [lw, pb.ev_r] if kk == 0 else [])
                    pb.ev_w = mm
                    gu_guardb[par] = mm
                    if is_gate:
                        ee = p.op("scalar", lambda e, o=g_grp[:, hti * 512:(hti + 1) * 512], i=pb.t[:, 0:512]:
                                  e.activation(o, i, AF.Silu, scale=DEQ),
                                  waits=[pb.ev_w, g_guard[hti]])
                        sil_ev[hti] = ee
                    else:
                        ee = p.op("vector", lambda e, o=h_sb[:, ht * 512:(ht + 1) * 512],
                                  i=g_grp[:, hti * 512:(hti + 1) * 512], u=pb.t[:, 0:512]:
                                  e.tensor_mul(o, i, u), waits=[pb.ev_w, sil_ev[hti]])
                        g_guard[hti] = ee
                    pb.ev_r = ee
                    hevs.append(ee)
                issue_gu()
        for ttl in range(4):
            tt = tq * 4 + ttl
            for oh in range(2):
                pb = [c.pC, c.pD][(ttl * 2 + oh) % 2]
                mm = None
                for hk in range(NHC):
                    lh = h_sb[:, hk * 512 + ttl * 128: hk * 512 + (ttl + 1) * 128]
                    rh = wd_sb[:, hk * D + oh * 512: hk * D + (oh + 1) * 512]
                    mm = p.op("tensor", lambda e, o=pb.t[:, 0:512], l=lh, r=rh,
                              s=(hk == 0), z=(hk == NHC - 1):
                              e.matmul(o, l, r, start=s, stop=z),
                              waits=hevs + [lwd, pb.ev_r] if hk == 0 else [])
                pb.ev_w = mm
                xv = c.xr[:, tt * D + oh * 512: tt * D + (oh + 1) * 512]
                ec = p.op("vector", lambda e, o=xv, i=pb.t[:, 0:512]:
                          e.scalar_tensor_tensor(o, i, DEQ * DEQ, o, ALU.mult, ALU.add),
                          waits=[pb.ev_w])
                pb.ev_r = c.confirm()



def emit_l1attn(c, wxp1_d, wxp2_d, wdt_d, are_d, bdt_d, dd_d, sel_d, dbg=False):
    nc = c.nc
    p = c.p
    wxp1 = c.sb("wxp1", [128, NKC * 80], BF16)
    wxp2 = c.sb("wxp2", [128, NKC * 16], BF16)
    wdt = c.sb("wdt", [64, D], BF16)
    a_re = c.sb("a_re", [128, 128], F32)
    a_sc = c.sb("a_sc", [128, 128], F32)
    bdt = c.sb("bdt", [128, 8], F32)
    dcol = c.sb("dcol", [128, 8], F32)
    sel16 = c.sb("sel16_sb", [16, 16 * 128], BF16)
    dtl = c.sb("dtl", [64, T], BF16)
    b_bf = c.sb("b_bf", [16, T], BF16)
    c_bf = c.sb("c_bf", [16, T], BF16)
    dt_sb = c.sb("dt_sb", [128, NKC * T], BF16)
    u_sb = c.sb("u_sb", [128, NKC * 512], BF16)
    y_acc = c.sb("y_acc", [128, NKC * 512], F32)
    carry = c.sb("carry", [128, NKC * DST], F32)
    brep = c.sb("brep", [128, 512], BF16)
    crep = c.sb("crep", [128, 512], BF16)
    sA = c.sb("sA", [128, 512], F32)
    sB = c.sb("sB", [128, 512], F32)
    a_t = [c.sb("a_t0", [128, 512], BF16), c.sb("a_t1", [128, 512], BF16)]
    b_t = [c.sb("b_t0", [128, 512], BF16), c.sb("b_t1", [128, 512], BF16)]
    h_t = [c.sb("h_t0", [128, 512], BF16), c.sb("h_t1", [128, 512], BF16)]
    y_t = [c.sb("y_t0", [128, 512], BF16), c.sb("y_t1", [128, 512], BF16)]

    lw1 = p.op("gpsimd", lambda e, o=wxp1[:, :].rearrange("p (k o) -> p k o", o=80),
               i=wxp1_d.rearrange("(k p) o -> p k o", p=128): e.dma_start(out=o, in_=i), dma=True)
    lw2 = p.op("gpsimd", lambda e, o=wxp2[:, :].rearrange("p (k o) -> p k o", o=16),
               i=wxp2_d.rearrange("(k p) o -> p k o", p=128): e.dma_start(out=o, in_=i), dma=True)
    lw3 = p.op("gpsimd", lambda e: e.dma_start(out=wdt[:, :], in_=wdt_d[:, :]), dma=True)
    lw4 = p.op("sync", lambda e: e.dma_start(out=a_re[:, :], in_=are_d[:, :]), dma=True)
    lw5 = p.op("sync", lambda e: e.dma_start(out=bdt[:, :], in_=bdt_d[:, :]), dma=True)
    lw6 = p.op("sync", lambda e: e.dma_start(out=dcol[:, :], in_=dd_d[:, :]), dma=True)
    lw7 = p.op("sync", lambda e: e.dma_start(out=sel16[:, :], in_=sel_d[:, :]), dma=True)

    ea1 = p.op("scalar", lambda e: e.activation(a_sc[:, :], a_re[:, :], AF.Exp), waits=[lw4])
    ea2 = p.op("vector", lambda e: e.tensor_scalar_mul(a_sc[:, :], a_sc[:, :], -1.0), waits=[ea1])

    ln_evs = emit_ln_to_fm(c, [c.ev_x])

    xdbl_evs = []
    for tq in range(NTQ):
        p1 = c.pA
        p2 = c.pB
        mm1 = None
        mm2 = None
        for kk in range(NKC):
            lh = wxp1[:, kk * 80:(kk + 1) * 80]
            rh = c.fmA[:, kk * T + tq * 512: kk * T + (tq + 1) * 512]
            mm1 = p.op("tensor", lambda e, o=p1.t[0:80, 0:512], l=lh, r=rh,
                       s=(kk == 0), z=(kk == NKC - 1): e.matmul(o, l, r, start=s, stop=z),
                       waits=ln_evs + [lw1, p1.ev_r] if kk == 0 else [])
        for kk in range(NKC):
            lh = wxp2[:, kk * 16:(kk + 1) * 16]
            rh = c.fmA[:, kk * T + tq * 512: kk * T + (tq + 1) * 512]
            mm2 = p.op("tensor", lambda e, o=p2.t[0:16, 0:512], l=lh, r=rh,
                       s=(kk == 0), z=(kk == NKC - 1): e.matmul(o, l, r, start=s, stop=z),
                       waits=[lw2, p2.ev_r] if kk == 0 else [])
        p1.ev_w = mm1
        p2.ev_w = mm2
        e1 = p.op("vector", lambda e, o=dtl[:, tq * 512:(tq + 1) * 512], i=p1.t[0:64, 0:512]:
                  e.tensor_copy(o, i), waits=[mm1])
        e2 = p.op("vector", lambda e, o=b_bf[:, tq * 512:(tq + 1) * 512], i=p1.t[64:80, 0:512]:
                  e.tensor_copy(o, i), waits=[mm1])
        e3 = p.op("vector", lambda e, o=c_bf[:, tq * 512:(tq + 1) * 512], i=p2.t[0:16, 0:512]:
                  e.tensor_copy(o, i), waits=[mm2])
        p1.ev_r = e2
        p2.ev_r = e3
        xdbl_evs += [e1, e2, e3]

    dt_evs = []
    prev_ad = None
    for dc in range(NKC):
        for tq in range(NTQ):
            pb = [c.pA, c.pB][(dc * NTQ + tq) % 2]
            bcol = bdt[:, dc:dc + 1]
            mm = p.op("tensor", lambda e, o=pb.t[:, 0:512], l=wdt[:, dc * 128:(dc + 1) * 128],
                      r=dtl[0:64, tq * 512:(tq + 1) * 512]:
                      e.matmul(o, l, r, start=True, stop=True),
                      waits=xdbl_evs + [lw3, pb.ev_r])
            pb.ev_w = mm
            a1 = p.op("scalar", lambda e, o=sA, i=pb.t[:, 0:512], b=bcol:
                      e.activation(o[:, :], i, AF.Abs, bias=b), waits=[mm, lw5, prev_ad])
            a2 = p.op("scalar", lambda e, o=sB, i=sA: e.activation(o[:, :], i[:, :], AF.Exp, scale=-1.0),
                      waits=[a1])
            a3 = p.op("scalar", lambda e, o=sA, i=sB: e.activation(o[:, :], i[:, :], AF.Ln, bias=1.0),
                      waits=[a2])
            a4 = p.op("scalar", lambda e, o=sB, i=pb.t[:, 0:512], b=bcol:
                      e.activation(o[:, :], i, AF.Relu, bias=b), waits=[a3])
            pb.ev_r = a4
            ad = p.op("vector", lambda e, o=dt_sb[:, dc * T + tq * 512: dc * T + (tq + 1) * 512],
                      i=sA, j=sB: e.tensor_add(o, i[:, :], j[:, :]), waits=[a3, a4])
            dt_evs.append(ad)
            prev_ad = ad

    if dbg:
        ddtl = nc.dram_tensor("dbg_dtl", [64, T], BF16, kind="ExternalOutput").ap()
        dbb = nc.dram_tensor("dbg_b", [16, T], BF16, kind="ExternalOutput").ap()
        dcb = nc.dram_tensor("dbg_c", [16, T], BF16, kind="ExternalOutput").ap()
        ddt = nc.dram_tensor("dbg_dt", [128, NKC * T], BF16, kind="ExternalOutput").ap()
        p.op("sync", lambda e: e.dma_start(out=ddtl[:, :], in_=dtl[:, :]), waits=xdbl_evs, dma=True)
        p.op("sync", lambda e: e.dma_start(out=dbb[:, :], in_=b_bf[:, :]), waits=xdbl_evs, dma=True)
        p.op("sync", lambda e: e.dma_start(out=dcb[:, :], in_=c_bf[:, :]), waits=xdbl_evs, dma=True)
        p.op("sync", lambda e: e.dma_start(out=ddt[:, :], in_=dt_sb[:, :]), waits=dt_evs, dma=True)

    c.stage_guard = None
    y_guard = [None] * NKC
    fin_evs = []
    for tq in range(NTQ):
        u_evs = []
        for dc in range(NKC):
            ue = p.op("vector", lambda e, o=u_sb[:, dc * 512:(dc + 1) * 512],
                      i=dt_sb[:, dc * T + tq * 512: dc * T + (tq + 1) * 512],
                      j=c.fmA[:, dc * T + tq * 512: dc * T + (tq + 1) * 512]:
                      e.tensor_mul(o, i, j), waits=dt_evs + ln_evs + [y_guard[dc]])
            u_evs.append(ue)
        scan_prev = [None, None]
        pool_prev = [None, None]
        for n in range(DST):
            mmB = p.op("tensor", lambda e, o=c.pC.t[:, 0:512], l=sel16[:, n * 128:(n + 1) * 128],
                       r=b_bf[:, tq * 512:(tq + 1) * 512]:
                       e.matmul(o, l, r, start=True, stop=True),
                       waits=xdbl_evs + [lw7, c.pC.ev_r])
            c.pC.ev_w = mmB
            eb = p.op("vector", lambda e, o=brep, i=c.pC.t[:, 0:512]: e.tensor_copy(o[:, :], i),
                      waits=[mmB, scan_prev[0], scan_prev[1]])
            c.pC.ev_r = eb
            mmC = p.op("tensor", lambda e, o=c.pD.t[:, 0:512], l=sel16[:, n * 128:(n + 1) * 128],
                       r=c_bf[:, tq * 512:(tq + 1) * 512]:
                       e.matmul(o, l, r, start=True, stop=True),
                       waits=[c.pD.ev_r])
            c.pD.ev_w = mmC
            ec = p.op("vector", lambda e, o=crep, i=c.pD.t[:, 0:512]: e.tensor_copy(o[:, :], i),
                      waits=[mmC, pool_prev[0], pool_prev[1]])
            c.pD.ev_r = ec
            for dc in range(NKC):
                par = dc % 2
                av = p.op("scalar", lambda e, o=a_t[par], i=dt_sb[:, dc * T + tq * 512: dc * T + (tq + 1) * 512],
                          s=a_sc[:, dc * 16 + n: dc * 16 + n + 1]:
                          e.activation(o[:, :], i, AF.Exp, scale=s),
                          waits=[ea2, scan_prev[par]] + dt_evs)
                bv = p.op("vector", lambda e, o=b_t[par], i=u_sb[:, dc * 512:(dc + 1) * 512], j=brep:
                          e.tensor_mul(o[:, :], i, j[:, :]), waits=u_evs + [eb, scan_prev[par]])
                init = 0.0 if tq == 0 else carry[:, dc * DST + n: dc * DST + n + 1]
                hv = p.op("vector", lambda e, o=h_t[par], i=a_t[par], j=b_t[par], z=init:
                          e.tensor_tensor_scan(o[:, :], i[:, :], j[:, :], z, ALU.mult, ALU.add),
                          waits=[av, bv, pool_prev[par]])
                cv = p.op("vector", lambda e, o=carry[:, dc * DST + n: dc * DST + n + 1], i=h_t[par]:
                          e.tensor_copy(o, i[:, 511:512]), waits=[hv])
                yeng = getattr(build_l1attn, "Y_ENGINE", "gpsimd")
                yv = p.op(yeng, lambda e, o=y_t[par], i=h_t[par], j=crep:
                          e.tensor_mul(o[:, :], i[:, :], j[:, :]), waits=[hv, ec])
                if n == 0:
                    ya = p.op(yeng, lambda e, o=y_acc[:, dc * 512:(dc + 1) * 512], i=y_t[par]:
                              e.tensor_copy(o, i[:, :]), waits=[yv, y_guard[dc]])
                else:
                    ya = p.op(yeng, lambda e, o=y_acc[:, dc * 512:(dc + 1) * 512], i=y_t[par]:
                              e.tensor_add(o, o, i[:, :]), waits=[yv])
                scan_prev[par] = cv
                pool_prev[par] = ya
        if dbg and tq == 0:
            dya = nc.dram_tensor("dbg_yacc", [128, NKC * 512], F32, kind="ExternalOutput").ap()
            dcr = nc.dram_tensor("dbg_carry", [128, NKC * DST], F32, kind="ExternalOutput").ap()
            p.op("sync", lambda e: e.dma_start(out=dya[:, :], in_=y_acc[:, :]),
                 waits=[pool_prev[0], pool_prev[1]], dma=True)
            p.op("sync", lambda e: e.dma_start(out=dcr[:, :], in_=carry[:, :]),
                 waits=[scan_prev[0], scan_prev[1]], dma=True)
        fence_mode = getattr(build_l1attn, "FENCE_MODE", "fence")
        if fence_mode == "fence":
            p.fence()
        elif fence_mode == "sblock":
            p.wait("scalar", [("Ev", p.cnt["Ev"]), ("Eg", p.cnt["Eg"])])
        elif fence_mode == "tblock":
            p.wait("tensor", [("Ev", p.cnt["Ev"]), ("Eg", p.cnt["Eg"])])
        elif fence_mode == "gblock":
            p.wait("gpsimd", [("Ev", p.cnt["Ev"]), ("Es", p.cnt["Es"])])
        elif fence_mode == "slack":
            yeng0 = getattr(build_l1attn, "Y_ENGINE", "gpsimd")
            nslack = getattr(build_l1attn, "NSLACK", 4)
            sl = None
            for _ in range(nslack):
                sl = p.op(yeng0, lambda e: e.tensor_copy(sB[:, 0:512], sA[:, 0:512]),
                          waits=[pool_prev[0], pool_prev[1]])
            pool_prev = [sl, sl]
        elif fence_mode == "et":
            p.wait("vector", [("Et", p.cnt["Et"])])
        elif fence_mode == "es":
            p.wait("vector", [("Es", p.cnt["Es"])])
        elif fence_mode == "eg":
            p.wait("vector", [("Eg", p.cnt["Eg"])])
        if dbg and tq == 1:
            park = c.sb("park", [128, 2 * 512], F32)
            pk = p.op("vector", lambda e: e.tensor_copy(park[:, :], y_acc[:, 6 * 512:8 * 512]),
                      waits=[pool_prev[0], pool_prev[1]])
            dpk = nc.dram_tensor("dbg_park", [128, 2 * 512], F32, kind="ExternalOutput").ap()
            p.op("sync", lambda e: e.dma_start(out=dpk[:, :], in_=park[:, :]), waits=[pk], dma=True)
        for dc in range(NKC):
            fe = p.op("vector", lambda e, o=c.stageF,
                      i=c.fmA[:, dc * T + tq * 512: dc * T + (tq + 1) * 512],
                      s=dcol[:, dc:dc + 1], j=y_acc[:, dc * 512:(dc + 1) * 512]:
                      e.scalar_tensor_tensor(o[:, :], i, s, j, ALU.mult, ALU.add),
                      waits=[pool_prev[0], pool_prev[1], lw6, c.stage_guard])
            evs = emit_fm_psum_to_resid(c, c.stageF, dc, tq, [fe])
            c.stage_guard = evs[-1]
            y_guard[dc] = fe
            fin_evs += evs



def emit_l1ffn(c, w1_d, w2_d, b1_d, b2_d, gw_d, gb_d, temp_d, ones_row_d, sel8_d, dbg=False):
    nc = c.nc
    p = c.p
    NHC2 = EH // 128
    h_half = c.sb("h_half", [128, 16 * 1024], BF16)
    out_acc = c.sb("out_acc", [128, 8 * 1024], F32)
    wst = [c.sb("wst0", [128, 4096], BF16), c.sb("wst1", [128, 4096], BF16)]
    gw_sb = c.sb("gw_sb", [128, NKC * 8], BF16)
    gbr = c.sb("gbr", [128, 8], F32)
    onesr = c.sb("onesr", [1, 128], BF16)
    sel8 = c.sb("sel8_sb", [8, 8 * 128], BF16)
    b1sb = c.sb("b1sb", [8, 1024], F32)
    b1T = c.sb("b1T", [128, NHC2 * 8], F32)
    b2sb = c.sb("b2sb", [8, D], BF16)
    itemp = c.sb("itemp", [1, 1], F32)
    itcol = c.sb("itcol", [128, 1], F32)
    gates = c.sb("gates", [128, 8], F32)
    m1c = c.sb("m1c", [128, 1], F32)
    m2c = c.sb("m2c", [128, 1], F32)
    eq1 = c.sb("eq1", [128, 8], F32)
    eq2 = c.sb("eq2", [128, 8], F32)
    s1c = c.sb("s1c", [128, 1], F32)
    s2c = c.sb("s2c", [128, 1], F32)
    tmp8 = c.sb("tmp8", [128, 8], F32)
    w_tok = c.sb("w_tok", [128, NTT * 8], F32)
    w_fm = c.sb("w_fm", [8, T], BF16)
    wrep = c.sb("wrep", [128, 1024], BF16)
    g_t = [c.sb("g_t0", [128, 512], BF16), c.sb("g_t1", [128, 512], BF16)]

    lo = p.op("sync", lambda e: e.dma_start(out=onesr[:, :], in_=ones_row_d[:, :]), dma=True)
    ls8 = p.op("sync", lambda e: e.dma_start(out=sel8[:, :], in_=sel8_d[:, :]), dma=True)
    lgw = p.op("gpsimd", lambda e, o=gw_sb[:, :].rearrange("p (k e) -> p k e", e=8),
               i=gw_d.rearrange("(k p) e -> p k e", p=128): e.dma_start(out=o, in_=i), dma=True)
    gbf32 = c.sb("gbf32", [1, 8], F32)
    lgb = p.op("sync", lambda e: e.dma_start(out=gbf32[:, :], in_=gb_d[:, :]), dma=True)
    ltm = p.op("sync", lambda e: e.dma_start(out=itemp[:, :], in_=temp_d[:, :]), dma=True)
    lb2 = p.op("gpsimd", lambda e: e.dma_start(out=b2sb[:, :], in_=b2_d[:, :]), dma=True)
    gb_bf = c.sb("gb_bf", [1, 8], BF16)
    egb = p.op("vector", lambda e: e.tensor_copy(gb_bf[:, :], gbf32[:, :]), waits=[lgb])

    ln_evs = emit_ln_to_fm(c, [c.ev_x])
    p.fence()

    e1t = p.op("vector", lambda e: e.reciprocal(itemp[:, :], itemp[:, :]), waits=[ltm])
    it_bf = c.sb("it_bf", [1, 1], BF16)
    e2t = p.op("vector", lambda e: e.tensor_copy(it_bf[:, :], itemp[:, :]), waits=[e1t])
    mmt = p.op("tensor", lambda e: e.matmul(c.pF1.t[:, 0:1], onesr[:, :], it_bf[:, :], start=True, stop=True),
               waits=[e2t, lo, c.pF1.ev_r])
    c.pF1.ev_w = mmt
    e3t = p.op("vector", lambda e: e.tensor_copy(itcol[:, :], c.pF1.t[:, 0:1]), waits=[mmt])
    c.pF1.ev_r = c.confirm()
    mmg = p.op("tensor", lambda e: e.matmul(c.pF2.t[:, 0:8], onesr[:, :], gb_bf[:, :], start=True, stop=True),
               waits=[egb, lo, c.pF2.ev_r])
    c.pF2.ev_w = mmg
    e4t = p.op("vector", lambda e, s=itcol: e.tensor_scalar(gbr[:, :], c.pF2.t[:, 0:8], s[:, :], None, ALU.mult),
               waits=[mmg, e3t])
    c.pF2.ev_r = c.confirm()

    b1evs = []
    b1_rd_guard = None
    lb1 = None
    for hc in range(NHC2):
        if hc % 8 == 0:
            q = hc // 8
            lb1 = p.op("sync", lambda e, o=b1sb, i=b1_d[:, q * 1024:(q + 1) * 1024]:
                       e.dma_start(out=o[:, :], in_=i), waits=[b1_rd_guard], dma=True)
        pf = [c.pF1, c.pF2][hc % 2]
        tr = p.op("tensor", lambda e, o=pf.t[:, 0:8], i=b1sb[:, (hc % 8) * 128:(hc % 8 + 1) * 128]:
                  e.transpose(o, i, c.ident_f[0:8, 0:8]),
                  waits=[lb1, c.ev_identf, pf.ev_r])
        b1_rd_guard = tr
        pf.ev_w = tr
        ee = p.op("vector", lambda e, o=b1T[:, hc * 8:(hc + 1) * 8], i=pf.t[:, 0:8]:
                  e.tensor_copy(o, i), waits=[tr])
        pf.ev_r = c.confirm()
        b1evs.append(ee)

    p.fence()
    wtok_evs = []
    prev8 = None
    prev10 = None
    for tt in range(NTT):
        pb = [c.pC, c.pD][tt % 2]
        mm = None
        for kk in range(NKC):
            lh = c.fmA[:, kk * T + tt * 128: kk * T + (tt + 1) * 128]
            rh = gw_sb[:, kk * 8:(kk + 1) * 8]
            mm = p.op("tensor", lambda e, o=pb.t[:, 0:8], l=lh, r=rh, s=(kk == 0), z=(kk == NKC - 1):
                      e.matmul(o, l, r, start=s, stop=z),
                      waits=ln_evs + [lgw, pb.ev_r] if kk == 0 else [])
        pb.ev_w = mm
        ge = p.op("vector", lambda e, o=gates, i=pb.t[:, 0:8]:
                  e.scalar_tensor_tensor(o[:, :], i, itcol[:, :], gbr[:, :], ALU.mult, ALU.add),
                  waits=[mm, e4t, e3t] + ([c.dbg_gev] if dbg and getattr(c, "dbg_gev", None) else []))
        pb.ev_r = c.confirm()
        if dbg:
            if tt == 0:
                c.dbg_g = nc.dram_tensor("dbg_gates", [128, NTT * 8], F32, kind="ExternalOutput").ap()
                c.dbg_praw = nc.dram_tensor("dbg_praw", [128, NTT * 8], F32, kind="ExternalOutput").ap()
            dgr = p.op("vector", lambda e, o=c.tmp16[:, 0:8], i=pb.t[:, 0:8]: e.tensor_copy(o, i),
                       waits=[mm, getattr(c, "dbg_gev2", None)])
            c.dbg_gev = p.op("sync", lambda e, o=c.dbg_g[:, tt * 8:(tt + 1) * 8]: e.dma_start(out=o, in_=gates[:, :]),
                             waits=[ge], dma=True)
            c.dbg_gev2 = p.op("sync", lambda e, o=c.dbg_praw[:, tt * 8:(tt + 1) * 8]:
                              e.dma_start(out=o, in_=c.tmp16[:, 0:8]), waits=[dgr], dma=True)
        ee1 = p.op("vector", lambda e: e.reduce_max(m1c[:, :], gates[:, :], axis=AX.X), waits=[ge])
        ee2 = p.op("vector", lambda e: e.tensor_single_scalar(eq1[:, :], gates[:, :], m1c[:, :], ALU.is_ge),
                   waits=[ee1])
        ee3 = p.op("vector", lambda e: e.scalar_tensor_tensor(tmp8[:, :], eq1[:, :], -1e30, gates[:, :],
                                                             ALU.mult, ALU.add), waits=[ee2])
        ee4 = p.op("vector", lambda e: e.reduce_max(m2c[:, :], tmp8[:, :], axis=AX.X), waits=[ee3])
        ee5 = p.op("vector", lambda e: e.tensor_single_scalar(eq2[:, :], tmp8[:, :], m2c[:, :], ALU.is_ge),
                   waits=[ee4])
        ee6 = p.op("vector", lambda e: e.tensor_sub(c.halfcol[:, :], m1c[:, :], m2c[:, :]), waits=[ee4, prev8])
        ee7 = p.op("scalar", lambda e: e.activation(s1c[:, :], c.halfcol[:, :], AF.Sigmoid), waits=[ee6, prev10])
        ee8 = p.op("scalar", lambda e: e.activation(s2c[:, :], s1c[:, :], AF.Copy, scale=-1.0, bias=1.0),
                   waits=[ee7])
        ee9 = p.op("vector", lambda e: e.tensor_scalar(tmp8[:, :], eq1[:, :], s1c[:, :], None, ALU.mult),
                   waits=[ee7, ee5])
        ee10 = p.op("vector", lambda e, o=w_tok[:, tt * 8:(tt + 1) * 8]:
                    e.scalar_tensor_tensor(o, eq2[:, :], s2c[:, :], tmp8[:, :], ALU.mult, ALU.add),
                    waits=[ee8, ee9])
        prev8 = ee8
        prev10 = ee10
        wtok_evs.append(ee10)

    p.fence()
    if dbg:
        dwt = nc.dram_tensor("dbg_wtok", [128, NTT * 8], F32, kind="ExternalOutput").ap()
        p.op("sync", lambda e: e.dma_start(out=dwt[:, :], in_=w_tok[:, :]), waits=wtok_evs, dma=True)
        db1 = nc.dram_tensor("dbg_b1T", [128, NHC2 * 8], F32, kind="ExternalOutput").ap()
        p.op("sync", lambda e: e.dma_start(out=db1[:, :], in_=b1T[:, :]), waits=b1evs, dma=True)

    wfm_evs = []
    for tt in range(NTT):
        pf = [c.pF1, c.pF2][tt % 2]
        tr = p.op("tensor", lambda e, o=pf.t[0:8, 0:128], i=w_tok[:, tt * 8:(tt + 1) * 8]:
                  e.transpose(o, i, c.ident_f[:, :]),
                  waits=wtok_evs + [c.ev_identf, pf.ev_r])
        pf.ev_w = tr
        ee = p.op("vector", lambda e, o=w_fm[:, tt * 128:(tt + 1) * 128], i=pf.t[0:8, 0:128]:
                  e.tensor_copy(o, i), waits=[tr])
        pf.ev_r = ee
        wfm_evs.append(ee)

    p.fence()

    gate_done = ("Ev", p.cnt["Ev"])
    c.w1_guard = gate_done
    c.w2_guard = gate_done
    g_guard = [None, None]
    w_seq = []
    for _th in range(2):
        for ei in range(NE):
            for hh in range(2):
                for hg in range(4):
                    w_seq.append(("w1", ei, hh, hg))
                for grp in range(4):
                    w_seq.append(("w2", ei, hh, grp))
    w_evs = {}
    w_guardb = [gate_done, gate_done]
    w_next = [0]

    def issue_w():
        i = w_next[0]
        if i >= len(w_seq):
            return
        w_next[0] = i + 1
        kind, ei, hh, j = w_seq[i]
        par = i % 2
        if kind == "w1":
            hbase = hh * 2048 + j * 512
            src = w1_d[ei, :, hbase:hbase + 512].rearrange("(k p) h -> p k h", p=128)
            dst = wst[par][:, :].rearrange("p (k h) -> p k h", h=512)
        else:
            rbase = hh * 2048 + j * 512
            src = w2_d[ei, rbase:rbase + 512, :].rearrange("(k p) o -> p k o", p=128)
            dst = wst[par][:, :].rearrange("p (k o) -> p k o", o=D)
        w_evs[i] = p.op("gpsimd", lambda e, o=dst, i_=src:
                        e.dma_start(out=o, in_=i_), waits=[w_guardb[par]], dma=True)

    issue_w()
    issue_w()
    w_i = [0]

    for th in range(2):
        oacc_evs = []
        first_e = True
        for ei in range(NE):
            wr_evs = []
            for tsub in range(2):
                mmr = p.op("tensor", lambda e, o=c.pD.t[:, 0:512],
                           l=sel8[:, ei * 128:(ei + 1) * 128],
                           r=w_fm[:, th * 1024 + tsub * 512: th * 1024 + (tsub + 1) * 512],
                           s=True, z=True:
                           e.matmul(o, l, r, start=s, stop=z),
                           waits=wfm_evs + [ls8, c.pD.ev_r])
                c.pD.ev_w = mmr
                er = p.op("vector", lambda e, o=wrep[:, tsub * 512:(tsub + 1) * 512], i=c.pD.t[:, 0:512]:
                          e.tensor_copy(o, i), waits=[mmr])
                c.pD.ev_r = c.confirm()
                wr_evs.append(er)
            for hh in range(2):
                ph_evs = []
                for hg in range(4):
                    gi = w_i[0]
                    w_i[0] = gi + 1
                    par = gi % 2
                    lw1 = w_evs[gi]
                    for hti in range(4):
                        hc = hh * 16 + hg * 4 + hti
                        hloc = hg * 4 + hti
                        for tsub in range(2):
                            pb = [c.pA, c.pB][tsub]
                            mm = None
                            for kk in range(NKC):
                                lh = wst[par][:, kk * 512 + hti * 128: kk * 512 + (hti + 1) * 128]
                                rh = c.fmA[:, kk * T + th * 1024 + tsub * 512: kk * T + th * 1024 + (tsub + 1) * 512]
                                mm = p.op("tensor", lambda e, o=pb.t[:, 0:512], l=lh, r=rh,
                                          s=(kk == 0), z=(kk == NKC - 1):
                                          e.matmul(o, l, r, start=s, stop=z),
                                          waits=ln_evs + [lw1, pb.ev_r] if kk == 0 else [])
                            pb.ev_w = mm
                            w_guardb[par] = mm
                            gl = p.op("scalar", lambda e, o=g_t[tsub], i=pb.t[:, 0:512],
                                      b=b1T[:, hc * 8 + ei: hc * 8 + ei + 1]:
                                      e.activation(o[:, :], i, AF.Gelu, bias=b),
                                      waits=b1evs + [pb.ev_w, g_guard[tsub]])
                            wm = p.op("vector", lambda e, o=h_half[:, hloc * 1024 + tsub * 512: hloc * 1024 + (tsub + 1) * 512],
                                      i=g_t[tsub], w=wrep[:, tsub * 512:(tsub + 1) * 512]:
                                      e.tensor_mul(o, i[:, :], w[:, :]),
                                      waits=wr_evs + [gl])
                            g_guard[tsub] = wm
                            pb.ev_r = wm
                            ph_evs.append(wm)
                    issue_w()
                for grp in range(4):
                    gi = w_i[0]
                    w_i[0] = gi + 1
                    par = gi % 2
                    lw2 = w_evs[gi]
                    for ttl in range(8):
                        for oh in range(2):
                            pb = [c.pC, c.pD][(ttl * 2 + oh) % 2]
                            mm = None
                            for kk in range(4):
                                hloc = grp * 4 + kk
                                lh = h_half[:, hloc * 1024 + ttl * 128: hloc * 1024 + (ttl + 1) * 128]
                                rh = wst[par][:, kk * D + oh * 512: kk * D + (oh + 1) * 512]
                                mm = p.op("tensor", lambda e, o=pb.t[:, 0:512], l=lh, r=rh,
                                          s=(kk == 0), z=(kk == 3):
                                          e.matmul(o, l, r, start=s, stop=z),
                                          waits=ph_evs + [lw2, pb.ev_r] if kk == 0 else [])
                            pb.ev_w = mm
                            w_guardb[par] = mm
                            oslice = out_acc[:, ttl * 1024 + oh * 512: ttl * 1024 + (oh + 1) * 512]
                            if first_e and hh == 0 and grp == 0:
                                ea = p.op("vector", lambda e, o=oslice, i=pb.t[:, 0:512]:
                                          e.tensor_copy(o, i), waits=[mm])
                            else:
                                ea = p.op("vector", lambda e, o=oslice, i=pb.t[:, 0:512]:
                                          e.tensor_add(o, i, o), waits=[mm])
                            pb.ev_r = ea
                            oacc_evs.append(ea)
                    issue_w()
            first_e = False
        p.fence()
        for ttl in range(8):
            tt = th * 8 + ttl
            for oh in range(2):
                pb = [c.pA, c.pB][oh]
                mm = p.op("tensor", lambda e, o=pb.t[:, 0:512],
                          l=w_fm[:, tt * 128:(tt + 1) * 128],
                          r=b2sb[:, oh * 512:(oh + 1) * 512], s=True, z=True:
                          e.matmul(o, l, r, start=s, stop=z),
                          waits=wfm_evs + [lb2, pb.ev_r])
                pb.ev_w = mm
                xv = c.xr[:, tt * D + oh * 512: tt * D + (oh + 1) * 512]
                e1 = p.op("vector", lambda e, o=xv, i=out_acc[:, ttl * 1024 + oh * 512: ttl * 1024 + (oh + 1) * 512]:
                          e.tensor_add(o, i, o), waits=oacc_evs)
                e2 = p.op("vector", lambda e, o=xv, i=pb.t[:, 0:512]:
                          e.tensor_add(o, i, o), waits=[mm, e1])
                pb.ev_r = c.confirm()
        p.fence()



def _mk(nc_args=()):
    nc = bass.Bass("TRN2", target_bir_lowering=False)
    return nc


def _declare_common(nc):
    x_d = nc.dram_tensor("x", [T, D], F32, kind="ExternalInput").ap()
    identb_d = nc.dram_tensor("ident_bf", [128, 128], BF16, kind="ExternalInput").ap()
    identf_d = nc.dram_tensor("ident_f", [128, 128], F32, kind="ExternalInput").ap()
    out_d = nc.dram_tensor("xout", [T, D], F32, kind="ExternalOutput").ap()
    return x_d, identb_d, identf_d, out_d


def _decl_l0attn(nc):
    return (nc.dram_tensor("wqT", [D, D], F32, kind="ExternalInput").ap(),
            nc.dram_tensor("wkT", [D, D], F32, kind="ExternalInput").ap(),
            nc.dram_tensor("wvT", [D, D], F32, kind="ExternalInput").ap(),
            nc.dram_tensor("woT", [D, D], F32, kind="ExternalInput").ap(),
            nc.dram_tensor("mask4", [128, 4 * 512], BF16, kind="ExternalInput").ap(),
            nc.dram_tensor("ones_col", [128, 1], BF16, kind="ExternalInput").ap())


def _decl_l0ffn(nc):
    return (nc.dram_tensor("wgT", [D, FFH], I32, kind="ExternalInput").ap(),
            nc.dram_tensor("wuT", [D, FFH], I32, kind="ExternalInput").ap(),
            nc.dram_tensor("wdT", [FFH, D], I32, kind="ExternalInput").ap())


def _decl_l1attn(nc):
    return (nc.dram_tensor("wxp1T", [D, 80], F32, kind="ExternalInput").ap(),
            nc.dram_tensor("wxp2T", [D, 16], F32, kind="ExternalInput").ap(),
            nc.dram_tensor("wdtT", [DTR, D], F32, kind="ExternalInput").ap(),
            nc.dram_tensor("A_re", [128, 128], F32, kind="ExternalInput").ap(),
            nc.dram_tensor("b_dt_c", [128, 8], F32, kind="ExternalInput").ap(),
            nc.dram_tensor("D_c", [128, 8], F32, kind="ExternalInput").ap(),
            nc.dram_tensor("sel16", [16, 16 * 128], BF16, kind="ExternalInput").ap())


def _decl_l1ffn(nc):
    return (nc.dram_tensor("w1", [NE, D, EH], F32, kind="ExternalInput").ap(),
            nc.dram_tensor("w2", [NE, EH, D], F32, kind="ExternalInput").ap(),
            nc.dram_tensor("b1", [NE, EH], F32, kind="ExternalInput").ap(),
            nc.dram_tensor("b2", [NE, D], F32, kind="ExternalInput").ap(),
            nc.dram_tensor("gw", [D, NE], F32, kind="ExternalInput").ap(),
            nc.dram_tensor("gb", [1, NE], F32, kind="ExternalInput").ap(),
            nc.dram_tensor("temp", [1, 1], F32, kind="ExternalInput").ap(),
            nc.dram_tensor("ones_row", [1, 128], BF16, kind="ExternalInput").ap(),
            nc.dram_tensor("sel8", [8, 8 * 128], BF16, kind="ExternalInput").ap())


def _standalone(decl_fn, emit_fn, dbg=False):
    nc = bass.Bass("TRN2", target_bir_lowering=False)
    x_d, identb_d, identf_d, out_d = _declare_common(nc)
    drs = decl_fn(nc)
    with ExitStack() as stack:
        p = Plan(nc, stack)
        c = mk_ctx(nc, stack, p)
        alloc_common(c)
        emit_load_consts(c, identb_d, identf_d)
        emit_load_x(c, x_d)
        with ExitStack() as bst:
            c.stacks.append(bst)
            emit_fn(c, *drs, dbg=dbg)
            p.fence()
            c.stacks.pop()
        emit_store_x(c, out_d)
        p.wait("gpsimd", p.now())
        with nc.Block() as block:
            p.build(block)
    return nc


def build_l0attn(dbg=False):
    return _standalone(_decl_l0attn, emit_l0attn, dbg)


def build_l0ffn(dbg=False):
    return _standalone(_decl_l0ffn, emit_l0ffn, dbg)


def build_l1attn(dbg=False):
    return _standalone(_decl_l1attn, emit_l1attn, dbg)


def build_l1ffn(dbg=False):
    return _standalone(_decl_l1ffn, emit_l1ffn, dbg)


def build_fused():
    nc = bass.Bass("TRN2", target_bir_lowering=False)
    x_d, identb_d, identf_d, out_d = _declare_common(nc)
    dr0a = _decl_l0attn(nc)
    dr0f = _decl_l0ffn(nc)
    dr1a = _decl_l1attn(nc)
    dr1f = _decl_l1ffn(nc)
    with ExitStack() as stack:
        p = Plan(nc, stack)
        c = mk_ctx(nc, stack, p)
        alloc_common(c)
        emit_load_consts(c, identb_d, identf_d)
        emit_load_x(c, x_d)
        for emit_fn, drs in ((emit_l0attn, dr0a), (emit_l0ffn, dr0f),
                             (emit_l1attn, dr1a), (emit_l1ffn, dr1f)):
            with ExitStack() as bst:
                c.stacks.append(bst)
                emit_fn(c, *drs)
                p.fence()
                c.stacks.pop()
        emit_store_x(c, out_d)
        p.wait("gpsimd", p.now())
        with nc.Block() as block:
            p.build(block)
    return nc

def run_l1ffn(x_full, W1, b1, W2, b2, gate_w, gate_b, temperature):
    nc = build_l1ffn()
    ident_bf, ident_f, _, _ = host_consts()
    sel8 = np.zeros((8, 8 * 128), dtype=BF)
    for e in range(8):
        sel8[e, e * 128:(e + 1) * 128] = 1.0
    ones_row = np.ones((1, 128), dtype=BF)
    in_maps = []
    for i in range(8):
        in_maps.append(dict(x=np.ascontiguousarray(x_full[i]),
                            w1=np.asarray(W1), w2=np.asarray(W2),
                            b1=np.asarray(b1), b2=np.asarray(b2),
                            gw=np.asarray(gate_w), gb=np.asarray(gate_b).reshape(1, 8),
                            temp=np.asarray(temperature).reshape(1, 1),
                            ones_row=ones_row, sel8=sel8,
                            ident_bf=ident_bf, ident_f=ident_f))
    res = run_bass_kernel_spmd(nc, in_maps, core_ids=list(range(8)))
    return np.stack([r["xout"] for r in res.results])



def host_consts():
    ident_bf = np.eye(128, dtype=BF)
    ident_f = np.eye(128, dtype=np.float32)
    mask4 = np.zeros((128, 4 * 512), dtype=BF)
    for rel in range(4):
        r = np.arange(128)[:, None]
        j = np.arange(512)[None, :]
        mask4[:, rel * 512:(rel + 1) * 512] = (j >= r + rel * 128).astype(BF)
    ones_col = np.ones((128, 1), dtype=BF)
    return ident_bf, ident_f, mask4, ones_col


def run_l0attn(x_full, Wq, Wk, Wv, Wo):
    nc = build_l0attn()
    ident_bf, ident_f, mask4, ones_col = host_consts()
    wqT = np.ascontiguousarray(Wq.T)
    wkT = np.ascontiguousarray(Wk.T)
    wvT = np.ascontiguousarray(Wv.T)
    woT = np.ascontiguousarray(Wo.T)
    in_maps = []
    for i in range(8):
        in_maps.append(dict(x=np.ascontiguousarray(x_full[i]), wqT=wqT, wkT=wkT, wvT=wvT, woT=woT,
                            ident_bf=ident_bf, ident_f=ident_f, mask4=mask4, ones_col=ones_col))
    res = run_bass_kernel_spmd(nc, in_maps, core_ids=list(range(8)))
    return np.stack([r["xout"] for r in res.results])


def run_l0ffn(x_full, w_gate, w_up, w_down):
    nc = build_l0ffn()
    ident_bf, ident_f, _, _ = host_consts()
    wgT = np.ascontiguousarray(w_gate.T)
    wuT = np.ascontiguousarray(w_up.T)
    wdT = np.ascontiguousarray(w_down.T)
    in_maps = []
    for i in range(8):
        in_maps.append(dict(x=np.ascontiguousarray(x_full[i]), wgT=wgT, wuT=wuT, wdT=wdT,
                            ident_bf=ident_bf, ident_f=ident_f))
    res = run_bass_kernel_spmd(nc, in_maps, core_ids=list(range(8)))
    return np.stack([r["xout"] for r in res.results])


def _l1attn_host(A_log, Dp, W_xproj, W_dt, b_dt):
    wxpT = np.ascontiguousarray(np.asarray(W_xproj).T)
    sel16 = np.zeros((16, 16 * 128), dtype=BF)
    for n in range(16):
        sel16[n, n * 128:(n + 1) * 128] = 1.0
    return dict(
        wxp1T=np.ascontiguousarray(wxpT[:, 0:80]),
        wxp2T=np.ascontiguousarray(wxpT[:, 80:96]),
        wdtT=np.ascontiguousarray(np.asarray(W_dt).T),
        A_re=np.ascontiguousarray(np.asarray(A_log).reshape(8, 128, 16).transpose(1, 0, 2).reshape(128, 128)),
        b_dt_c=np.ascontiguousarray(np.asarray(b_dt).reshape(8, 128).T),
        D_c=np.ascontiguousarray(np.asarray(Dp).reshape(8, 128).T),
        sel16=sel16)


def run_l1attn(x_full, A_log, Dp, W_xproj, W_dt, b_dt):
    nc = build_l1attn()
    ident_bf, ident_f, _, _ = host_consts()
    hm = _l1attn_host(A_log, Dp, W_xproj, W_dt, b_dt)
    in_maps = []
    for i in range(8):
        in_maps.append(dict(x=np.ascontiguousarray(x_full[i]), ident_bf=ident_bf, ident_f=ident_f, **hm))
    res = run_bass_kernel_spmd(nc, in_maps, core_ids=list(range(8)))
    return np.stack([r["xout"] for r in res.results])


def _sel8_host():
    sel8 = np.zeros((8, 8 * 128), dtype=BF)
    for e in range(8):
        sel8[e, e * 128:(e + 1) * 128] = 1.0
    return sel8


def kernel(**inputs):
    g = lambda k: np.asarray(inputs[k])
    x = np.asarray(inputs["L0_attn_x"], dtype=np.float32)
    nc = build_fused()
    ident_bf, ident_f, mask4, ones_col = host_consts()
    hm1a = _l1attn_host(g("L1_attn_A_log"), g("L1_attn_D"), g("L1_attn_W_xproj"),
                        g("L1_attn_W_dt"), g("L1_attn_b_dt"))
    shared = dict(
        ident_bf=ident_bf, ident_f=ident_f, mask4=mask4, ones_col=ones_col,
        wqT=np.ascontiguousarray(g("L0_attn_Wq").T), wkT=np.ascontiguousarray(g("L0_attn_Wk").T),
        wvT=np.ascontiguousarray(g("L0_attn_Wv").T), woT=np.ascontiguousarray(g("L0_attn_Wo").T),
        wgT=np.ascontiguousarray(g("L0_ffn_w_gate").T), wuT=np.ascontiguousarray(g("L0_ffn_w_up").T),
        wdT=np.ascontiguousarray(g("L0_ffn_w_down").T),
        w1=g("L1_ffn_W1"), w2=g("L1_ffn_W2"), b1=g("L1_ffn_b1"), b2=g("L1_ffn_b2"),
        gw=g("L1_ffn_gate_w"), gb=g("L1_ffn_gate_b").reshape(1, 8),
        temp=g("L1_ffn_temperature").reshape(1, 1),
        ones_row=np.ones((1, 128), dtype=BF), sel8=_sel8_host(),
        **hm1a)
    in_maps = [dict(x=np.ascontiguousarray(x[i]), **shared) for i in range(8)]
    res = run_bass_kernel_spmd(nc, in_maps, core_ids=list(range(8)))
    return np.stack([r["xout"] for r in res.results])


# version 04
# speedup vs baseline: 2.5827x; 2.2061x over previous
import numpy as np
import ml_dtypes
from contextlib import ExitStack
import concourse.bass as bass
import concourse.mybir as mybir
from concourse.bass_utils import run_bass_kernel_spmd

F32 = mybir.dt.float32
BF16 = mybir.dt.bfloat16
I32 = mybir.dt.int32
AF = mybir.ActivationFunctionType
ALU = mybir.AluOpType
AX = mybir.AxisListType
BF = ml_dtypes.bfloat16

T, D = 2048, 1024
NTT = T // 128
NKC = D // 128
NTQ = T // 512
NH, DH = 16, 64
FFH = 3072
NE, EH = 8, 4096
DST, DTR = 16, 64
LN_EPS = 1e-5
QSCALE = 1.0 / 8.0
DEQ = 1.0 / 127.0

ENGINES = ("tensor", "vector", "scalar", "gpsimd", "sync")
ENG_EV = {"tensor": "Et", "vector": "Ev", "scalar": "Es", "gpsimd": "Eg", "sync": "Ey"}
DMA_SLOTS = {"gpsimd": 8, "sync": 6, "scalar": 2}
DMA_PREFIX = {"gpsimd": "Dg", "sync": "Dy", "scalar": "Da"}


class Plan:

    def __init__(self, nc, stack):
        self.nc = nc
        self.q = {e: [] for e in ENGINES}
        self.sems = {}
        self.cnt = {}
        self.rr = {e: 0 for e in DMA_SLOTS}
        for s in ENG_EV.values():
            self.sems[s] = stack.enter_context(nc.semaphore(s))
            self.cnt[s] = 0
        for e, n in DMA_SLOTS.items():
            for i in range(n):
                s = f"{DMA_PREFIX[e]}{i}"
                self.sems[s] = stack.enter_context(nc.semaphore(s))
                self.cnt[s] = 0

    def op(self, eng, fn, waits=(), dma=False):
        waits = [w for w in waits if w is not None]
        if dma:
            slot = self.rr[eng]
            self.rr[eng] = (slot + 1) % DMA_SLOTS[eng]
            sem = f"{DMA_PREFIX[eng]}{slot}"
            amt = 16
            if self.cnt[sem] > 0:
                waits = waits + [(sem, self.cnt[sem])]
        else:
            sem = ENG_EV[eng]
            amt = 1
            if self.cnt[sem] > 0 and eng != "tensor":
                waits = waits + [(sem, self.cnt[sem])]
        self.cnt[sem] += amt
        ev = (sem, self.cnt[sem])
        wmax = {}
        for s, v in waits:
            if v > wmax.get(s, -1):
                wmax[s] = v
        sems = self.sems

        def g(e, fn=fn, wmax=wmax, sem=sem, amt=amt):
            for s, v in wmax.items():
                e.wait_ge(sems[s], v)
            fn(e).then_inc(sems[sem], amt)
        self.q[eng].append(g)
        return ev

    def wait(self, eng, waits):
        wmax = {}
        for wv in waits:
            if wv is None:
                continue
            s, v = wv
            if v > wmax.get(s, -1):
                wmax[s] = v
        sems = self.sems

        def g(e, wmax=wmax):
            for s, v in wmax.items():
                e.wait_ge(sems[s], v)
        self.q[eng].append(g)

    def now(self):
        return [(s, v) for s, v in self.cnt.items() if v > 0]

    def fence(self):
        evs = self.now()
        for e in ENGINES:
            self.wait(e, evs)

    def build(self, block):
        for name in ENGINES:
            dec = getattr(block, name)
            fns = self.q[name]

            def body(e, fns=fns):
                for f in fns:
                    f(e)
            dec(body)


class PBuf:

    def __init__(self, t):
        self.t = t
        self.ev_w = None
        self.ev_r = None


class Ctx:
    pass


def mk_ctx(nc, stack, p):
    c = Ctx()
    c.nc = nc
    c.st = stack
    c.p = p
    c.stacks = [stack]

    def sb(name, shape, dt):
        return c.stacks[-1].enter_context(nc.sbuf_tensor(name, shape, dt))
    c.sb = sb

    def ps(name, shape, dt):
        return PBuf(stack.enter_context(nc.psum_tensor(name, shape, dt)))
    c.ps = ps

    c.dummy = c.sb("dummy_cf", [1, 16], F32)
    p.op("vector", lambda e: e.memset(c.dummy[:, :], 0.0))

    def confirm(eng="vector"):
        if eng == "scalar":
            return p.op(eng, lambda e: e.copy(c.dummy[0:1, 8:16], c.dummy[0:1, 0:8]))
        return p.op(eng, lambda e: e.tensor_copy(c.dummy[0:1, 8:16], c.dummy[0:1, 0:8]))
    c.confirm = confirm
    return c



def alloc_common(c, with_fmA=True):
    c.xr = c.sb("xr", [128, NTT * D], F32)
    if with_fmA:
        c.fmA = c.sb("fmA", [128, NKC * T], BF16)
    c.ident_bf = c.sb("identb_sb", [128, 128], BF16)
    c.ident_f = c.sb("identf_sb", [128, 128], F32)
    c.stat1 = c.sb("stat1", [128, NTT], F32)
    c.stat2 = c.sb("stat2", [128, NTT], F32)
    c.mcols = c.sb("mcols", [128, NTT], F32)
    c.rcols = c.sb("rcols", [128, NTT], F32)
    c.tmp16 = c.sb("tmp16", [128, NTT], F32)
    c.halfcol = c.sb("halfcol", [128, 1], F32)
    c.ln_scr = c.sb("ln_scr", [128, D], BF16)
    c.stageF = c.sb("stageF", [128, 512], F32)
    c.pA = c.ps("pA", [128, 512], F32)
    c.pB = c.ps("pB", [128, 512], F32)
    c.pC = c.ps("pC", [128, 512], F32)
    c.pD = c.ps("pD", [128, 512], F32)
    c.pT1 = c.ps("pT1", [128, 1024], BF16)
    c.pT2 = c.ps("pT2", [128, 1024], BF16)
    c.pF1 = c.ps("pF1", [128, 512], F32)
    c.pF2 = c.ps("pF2", [128, 512], F32)


def emit_load_consts(c, ident_bf_d, ident_f_d):
    p = c.p
    c.ev_ident = p.op("sync", lambda e, o=c.ident_bf, i=ident_bf_d: e.dma_start(out=o[:, :], in_=i[:, :]), dma=True)
    c.ev_identf = p.op("sync", lambda e, o=c.ident_f, i=ident_f_d: e.dma_start(out=o[:, :], in_=i[:, :]), dma=True)


def emit_load_x(c, x_d):
    p = c.p
    src = x_d.rearrange("(tt p) d -> p tt d", p=128)
    dst = c.xr[:, :].rearrange("p (tt d) -> p tt d", d=D)
    c.ev_x = p.op("sync", lambda e, o=dst, i=src: e.dma_start(out=o, in_=i), dma=True)


def emit_store_x(c, out_d, waits=None):
    p = c.p
    if waits is None:
        waits = p.now()
    dst = out_d.rearrange("(tt p) d -> p tt d", p=128)
    src = c.xr[:, :].rearrange("p (tt d) -> p tt d", d=D)
    ev = p.op("sync", lambda e, o=dst, i=src: e.dma_start(out=o, in_=i), waits=waits, dma=True)
    return ev


def emit_ln_stats(c, deps):
    p = c.p
    evs = []
    prev_hc = None
    for tt in range(NTT):
        xv = c.xr[:, tt * D:(tt + 1) * D]
        e1 = p.op("vector", lambda e, o=c.stat1[:, tt:tt + 1], i=xv: e.reduce_sum(o, i, axis=AX.X), waits=deps)
        e2a = p.op("scalar", lambda e, o=c.stageF, i=c.xr[:, tt * D: tt * D + 512], a=c.stat2[:, tt:tt + 1]:
                   e.activation(o[:, :], i, AF.Square, accum_out=a), waits=deps)
        e2b = p.op("scalar", lambda e, o=c.stageF, i=c.xr[:, tt * D + 512: (tt + 1) * D], a=c.halfcol:
                   e.activation(o[:, :], i, AF.Square, accum_out=a[:, :]), waits=deps + [prev_hc])
        e2c = p.op("vector", lambda e, o=c.stat2[:, tt:tt + 1]:
                   e.tensor_add(o, o, c.halfcol[:, :]), waits=[e2a, e2b])
        prev_hc = e2c
        evs += [e1, e2c]
    em = p.op("vector", lambda e: e.tensor_scalar_mul(c.mcols[:, :], c.stat1[:, :], 1.0 / D), waits=evs)
    ev = p.op("vector", lambda e: e.tensor_scalar_mul(c.rcols[:, :], c.stat2[:, :], 1.0 / D), waits=evs)
    e3 = p.op("vector", lambda e: e.tensor_mul(c.tmp16[:, :], c.mcols[:, :], c.mcols[:, :]), waits=[em])
    e4 = p.op("vector", lambda e: e.tensor_sub(c.rcols[:, :], c.rcols[:, :], c.tmp16[:, :]), waits=[ev, e3])
    e5 = p.op("vector", lambda e: e.tensor_scalar_add(c.rcols[:, :], c.rcols[:, :], LN_EPS), waits=[e4])
    e6 = p.op("vector", lambda e: e.reciprocal(c.tmp16[:, :], c.rcols[:, :]), waits=[e5])
    e7 = p.op("scalar", lambda e: e.activation(c.rcols[:, :], c.tmp16[:, :], AF.Sqrt), waits=[e6])
    return [em, e7]


def emit_ln_to_fm(c, deps, dst=None, dst_stride=T):
    p = c.p
    if dst is None:
        dst = c.fmA
    sevs = emit_ln_stats(c, deps)
    out_evs = []
    pts = [c.pT1, c.pT2]
    prev_tr = None
    for tt in range(NTT):
        xv = c.xr[:, tt * D:(tt + 1) * D]
        ea = p.op("vector", lambda e, o=c.ln_scr, i=xv, m=c.mcols[:, tt:tt + 1], r=c.rcols[:, tt:tt + 1]:
                  e.tensor_scalar(o[:, :], i, m, r, ALU.subtract, ALU.mult),
                  waits=sevs + deps + [prev_tr])
        pt = pts[tt % 2]
        tr = None
        for k in range(NKC):
            tr = p.op("tensor", lambda e, o=pt.t[:, k * 128:(k + 1) * 128], i=c.ln_scr[:, k * 128:(k + 1) * 128],
                      idn=c.ident_bf: e.transpose(o, i, idn[:, :]),
                      waits=[ea, c.ev_ident, pt.ev_r] if k == 0 else [ea])
        pt.ev_w = tr
        prev_tr = tr
        dv = dst[:, :].rearrange("p (k r) -> p k r", r=dst_stride)[:, :, tt * 128:(tt + 1) * 128]
        sv = pt.t[:, :].rearrange("p (k r) -> p k r", r=128)
        ee = p.op("vector", lambda e, o=dv, i=sv: e.tensor_copy(o, i), waits=[tr])
        pt.ev_r = c.confirm()
        out_evs.append(ee)
    return out_evs


def emit_fm_psum_to_resid(c, src_sb_f32, ot, tq, deps, scale=None):
    p = c.p
    evs = []
    pfs = [c.pF1, c.pF2]
    for j in range(4):
        pf = pfs[j % 2]
        tt = tq * 4 + j
        tr = p.op("tensor", lambda e, o=pf.t[:, 0:128], i=src_sb_f32[:, j * 128:(j + 1) * 128], idn=c.ident_f:
                  e.transpose(o, i, idn[:, :]), waits=deps + [c.ev_identf, pf.ev_r])
        pf.ev_w = tr
        xv = c.xr[:, tt * D + ot * 128: tt * D + (ot + 1) * 128]
        ee = p.op("vector", lambda e, o=xv, i=pf.t[:, 0:128], x=xv:
                  e.tensor_add(o, i, x), waits=[tr])
        pf.ev_r = c.confirm()
        evs.append(ee)
    return evs



def emit_l0attn(c, wq_d, wk_d, wv_d, wo_d, mask_d, ones_d, dbg=False):
    nc = c.nc
    p = c.p
    mask = c.sb("mask_sb", [128, 4 * 512], BF16)
    ones_col = c.sb("ones_sb", [128, 1], BF16)
    ev_mask = p.op("sync", lambda e: e.dma_start(out=mask[:, :], in_=mask_d[:, :]), dma=True)
    ev_ones = p.op("sync", lambda e: e.dma_start(out=ones_col[:, :], in_=ones_d[:, :]), dma=True)

    wbuf = [c.sb("wbuf0", [128, NKC * 128], BF16),
            c.sb("wbuf1", [128, NKC * 128], BF16)]
    wobuf = c.sb("wobuf", [128, NKC * D], BF16)
    qc = c.sb("qc", [128, T], BF16)
    kc = c.sb("kc", [128, T], BF16)
    vc = c.sb("vc", [128, NTT * 128], BF16)
    ptb = c.sb("ptb", [128, 16 * 512], BF16)
    att = c.sb("att", [128, NTT * 128], BF16)
    inv = c.sb("inv", [128, 1], F32)

    ln_evs = emit_ln_to_fm(c, [c.ev_x])
    if dbg:
        dfm = nc.dram_tensor("dbg_fm", [128, NKC * T], BF16, kind="ExternalOutput").ap()
        p.op("sync", lambda e: e.dma_start(out=dfm[:, :], in_=c.fmA[:, :]), waits=ln_evs, dma=True)

    qkv_seq = [(kc_i, pi) for kc_i in range(NKC) for pi in range(3)]
    qkv_evs = {}
    qkv_guardb = [None, None]
    qkv_next = [0]

    def issue_qkv():
        i = qkv_next[0]
        if i >= len(qkv_seq):
            return
        qkv_next[0] = i + 1
        kc_i, pi = qkv_seq[i]
        par = i % 2
        w_d = (wq_d, wk_d, wv_d)[pi]
        src = w_d[:, kc_i * 128:(kc_i + 1) * 128].rearrange("(k p) o -> p k o", p=128)
        qkv_evs[i] = p.op("gpsimd", lambda e, o=wbuf[par][:, :].rearrange("p (k o) -> p k o", o=128), i_=src:
                          e.dma_start(out=o, in_=i_), waits=[qkv_guardb[par]], dma=True)

    issue_qkv()
    issue_qkv()
    qkv_i = [0]
    lwo = p.op("gpsimd", lambda e, o=wobuf[:, :].rearrange("p (k o) -> p k o", o=D),
               i_=wo_d.rearrange("(k p) o -> p k o", p=128): e.dma_start(out=o, in_=i_), dma=True)

    wo_stage_deps = []
    attn_fm = c.sb("attn_fm", [128, NKC * T], BF16)

    for kc_i in range(NKC):
        prev_evq = []
        for (w_d, dstbuf, isv) in ((wq_d, qc, False), (wk_d, kc, False), (wv_d, vc, True)):
            wi = qkv_i[0]
            qkv_i[0] = wi + 1
            wpar = wi % 2
            lw = qkv_evs[wi]
            if not isv:
                for tq in range(NTQ):
                    pb = [c.pA, c.pB][tq % 2]
                    mws = ln_evs + [lw, pb.ev_r]
                    mm = None
                    for kk in range(NKC):
                        lh = wbuf[wpar][:, kk * 128:(kk + 1) * 128]
                        rh = c.fmA[:, kk * T + tq * 512: kk * T + (tq + 1) * 512]
                        mm = p.op("tensor", lambda e, o=pb.t[:, 0:512], l=lh, r=rh,
                                  s=(kk == 0), z=(kk == NKC - 1):
                                  e.matmul(o, l, r, start=s, stop=z),
                                  waits=mws if kk == 0 else [])
                    pb.ev_w = mm
                    qkv_guardb[wpar] = mm
                    ee = p.op("vector", lambda e, o=dstbuf[:, tq * 512:(tq + 1) * 512], i=pb.t[:, 0:512]:
                              e.tensor_copy(o, i), waits=[pb.ev_w])
                    pb.ev_r = ee
                    prev_evq.append(ee)
            else:
                for tt in range(NTT):
                    pb = [c.pA, c.pB][tt % 2]
                    mws = ln_evs + [lw, pb.ev_r]
                    mm = None
                    for kk in range(NKC):
                        lh = c.fmA[:, kk * T + tt * 128: kk * T + (tt + 1) * 128]
                        rh = wbuf[wpar][:, kk * 128:(kk + 1) * 128]
                        mm = p.op("tensor", lambda e, o=pb.t[:, 0:128], l=lh, r=rh,
                                  s=(kk == 0), z=(kk == NKC - 1):
                                  e.matmul(o, l, r, start=s, stop=z),
                                  waits=mws if kk == 0 else [])
                    pb.ev_w = mm
                    qkv_guardb[wpar] = mm
                    ee = p.op("vector", lambda e, o=dstbuf[:, tt * 128:(tt + 1) * 128], i=pb.t[:, 0:128]:
                              e.tensor_copy(o, i), waits=[pb.ev_w])
                    pb.ev_r = ee
                    prev_evq.append(ee)
            issue_qkv()
        qkv_done = list(prev_evq)
        if dbg and kc_i == 0:
            dq = nc.dram_tensor("dbg_q", [128, T], BF16, kind="ExternalOutput").ap()
            dk = nc.dram_tensor("dbg_k", [128, T], BF16, kind="ExternalOutput").ap()
            dv = nc.dram_tensor("dbg_v", [128, NTT * 128], BF16, kind="ExternalOutput").ap()
            p.op("sync", lambda e: e.dma_start(out=dq[:, :], in_=qc[:, :]), waits=qkv_done, dma=True)
            p.op("sync", lambda e: e.dma_start(out=dk[:, :], in_=kc[:, :]), waits=qkv_done, dma=True)
            p.op("sync", lambda e: e.dma_start(out=dv[:, :], in_=vc[:, :]), waits=qkv_done, dma=True)

        att_evs = []
        for hh in range(2):
            hbase = hh * 64
            for sq in range(NTQ):
                nkt = 4 * sq + 4
                exp_evs = []
                for kt in range(nkt):
                    pb = [c.pA, c.pB, c.pC, c.pD][kt % 4]
                    mm = p.op("tensor", lambda e, o=pb.t[:, 0:512],
                              l=kc[hbase:hbase + 64, kt * 128:(kt + 1) * 128],
                              r=qc[hbase:hbase + 64, sq * 512:(sq + 1) * 512]:
                              e.matmul(o, l, r, start=True, stop=True),
                              waits=qkv_done + [pb.ev_r])
                    pb.ev_w = mm
                    ex = p.op("scalar", lambda e, o=ptb[:, kt * 512:(kt + 1) * 512], i=pb.t[:, 0:512]:
                              e.activation(o, i, AF.Exp, scale=QSCALE), waits=[mm])
                    pb.ev_r = ex
                    exp_evs.append(ex)
                    if kt >= 4 * sq:
                        rel = kt - 4 * sq
                        me = p.op("vector", lambda e, o=ptb[:, kt * 512:(kt + 1) * 512],
                                  m=mask[:, rel * 512:(rel + 1) * 512]:
                                  e.tensor_mul(o, o, m), waits=[ex, ev_mask])
                        exp_evs.append(me)
                for si in range(4):
                    stile = sq * 4 + si
                    oevs = []
                    for kt in range(nkt):
                        lh = ptb[:, kt * 512 + si * 128: kt * 512 + (si + 1) * 128]
                        mm1 = p.op("tensor", lambda e, o=c.pC.t[:, 0:64], l=lh,
                                   r=vc[:, kt * 128 + hbase: kt * 128 + hbase + 64],
                                   s=(kt == 0), z=(kt == nkt - 1):
                                   e.matmul(o, l, r, start=s, stop=z),
                                   waits=exp_evs + [c.pC.ev_r] if kt == 0 else [])
                        mm2 = p.op("tensor", lambda e, o=c.pD.t[:, 0:1], l=lh,
                                   r=ones_col[:, 0:1],
                                   s=(kt == 0), z=(kt == nkt - 1):
                                   e.matmul(o, l, r, start=s, stop=z),
                                   waits=[ev_ones, c.pD.ev_r] if kt == 0 else [])
                        oevs += [mm1, mm2]
                    c.pC.ev_w = oevs[-2]
                    c.pD.ev_w = oevs[-1]
                    rc = p.op("vector", lambda e: e.reciprocal(inv[:, :], c.pD.t[:, 0:1]), waits=[c.pD.ev_w])
                    c.pD.ev_r = rc
                    ee = p.op("vector", lambda e, o=att[:, stile * 128 + hbase: stile * 128 + hbase + 64],
                              i=c.pC.t[:, 0:64]:
                              e.tensor_scalar(o, i, inv[:, :], None, ALU.mult), waits=[rc, c.pC.ev_w])
                    c.pC.ev_r = ee
                    att_evs.append(ee)
        if dbg and kc_i == 0:
            da = nc.dram_tensor("dbg_att", [128, NTT * 128], BF16, kind="ExternalOutput").ap()
            p.op("sync", lambda e: e.dma_start(out=da[:, :], in_=att[:, :]), waits=att_evs, dma=True)
        if dbg and kc_i == 7:
            da7 = nc.dram_tensor("dbg_att7", [128, NTT * 128], BF16, kind="ExternalOutput").ap()
            p.op("sync", lambda e: e.dma_start(out=da7[:, :], in_=att[:, :]), waits=att_evs, dma=True)
        pts = [c.pT1, c.pT2]
        for tt in range(NTT):
            pt = pts[tt % 2]
            tr = p.op("tensor", lambda e, o=pt.t[:, 0:128], i=att[:, tt * 128:(tt + 1) * 128], idn=c.ident_bf:
                      e.transpose(o, i, idn[:, :]), waits=att_evs + [c.ev_ident, pt.ev_r])
            pt.ev_w = tr
            ee = p.op("vector", lambda e, o=attn_fm[:, kc_i * T + tt * 128: kc_i * T + (tt + 1) * 128],
                      i=pt.t[:, 0:128]: e.tensor_copy(o, i), waits=[tr])
            pt.ev_r = ee
            wo_stage_deps.append(ee)

    if dbg:
        dafm = nc.dram_tensor("dbg_attnfm", [128, NKC * T], BF16, kind="ExternalOutput").ap()
        p.op("sync", lambda e: e.dma_start(out=dafm[:, :], in_=attn_fm[:, :]), waits=wo_stage_deps, dma=True)
    for tt in range(NTT):
        for oh in range(2):
            pb = [c.pA, c.pB][(tt * 2 + oh) % 2]
            mm = None
            for kk in range(NKC):
                lh = attn_fm[:, kk * T + tt * 128: kk * T + (tt + 1) * 128]
                rh = wobuf[:, kk * D + oh * 512: kk * D + (oh + 1) * 512]
                mm = p.op("tensor", lambda e, o=pb.t[:, 0:512], l=lh, r=rh,
                          s=(kk == 0), z=(kk == NKC - 1):
                          e.matmul(o, l, r, start=s, stop=z),
                          waits=wo_stage_deps + [lwo, pb.ev_r] if kk == 0 else [])
            pb.ev_w = mm
            xv = c.xr[:, tt * D + oh * 512: tt * D + (oh + 1) * 512]
            ec = p.op("vector", lambda e, o=xv, i=pb.t[:, 0:512]:
                      e.tensor_add(o, i, o), waits=[pb.ev_w])
            pb.ev_r = c.confirm()



def emit_l0ffn(c, wg_d, wu_d, wd_d, dbg=False):
    nc = c.nc
    p = c.p
    NHC = FFH // 128
    NHG = FFH // 512
    h_sb = c.sb("h_sb", [128, NHC * 512], BF16)
    g_grp = c.sb("g_grp", [128, 4 * 512], BF16)
    wd_sb = c.sb("wd_sb", [128, NHC * D], BF16)
    wgu_sb = [c.sb("wgu_sb0", [128, NKC * 512], BF16),
              c.sb("wgu_sb1", [128, NKC * 512], BF16)]

    ln_evs = emit_ln_to_fm(c, [c.ev_x])

    wdv = wd_d.rearrange("(k p) o -> p k o", p=128)
    lwd = p.op("gpsimd", lambda e, o=wd_sb[:, :].rearrange("p (k o) -> p k o", o=D), i=wdv:
               e.dma_start(out=o, in_=i), dma=True)

    g_guard = [None] * 4
    sil_ev = [None] * 4
    gu_seq = [(tq, hg, gi) for tq in range(NTQ) for hg in range(NHG) for gi in range(2)]
    gu_evs = {}
    gu_guardb = [None, None]
    gu_next = [0]

    def issue_gu():
        i = gu_next[0]
        if i >= len(gu_seq):
            return
        gu_next[0] = i + 1
        _tq, hg, gi = gu_seq[i]
        par = i % 2
        w_d = (wg_d, wu_d)[gi]
        src = w_d[:, hg * 512:(hg + 1) * 512].rearrange("(k p) h -> p k h", p=128)
        gu_evs[i] = p.op("gpsimd", lambda e, o=wgu_sb[par][:, :].rearrange("p (k h) -> p k h", h=512), i_=src:
                         e.dma_start(out=o, in_=i_), waits=[gu_guardb[par]], dma=True)

    issue_gu()
    issue_gu()
    gu_i = [0]
    for tq in range(NTQ):
        hevs = []
        for hg in range(NHG):
            for (w_d, is_gate) in ((wg_d, True), (wu_d, False)):
                gi_idx = gu_i[0]
                gu_i[0] = gi_idx + 1
                par = gi_idx % 2
                lw = gu_evs[gi_idx]
                for hti in range(4):
                    ht = hg * 4 + hti
                    pb = c.pA if is_gate else c.pB
                    mm = None
                    for kk in range(NKC):
                        lh = wgu_sb[par][:, kk * 512 + hti * 128: kk * 512 + (hti + 1) * 128]
                        rh = c.fmA[:, kk * T + tq * 512: kk * T + (tq + 1) * 512]
                        mm = p.op("tensor", lambda e, o=pb.t[:, 0:512], l=lh, r=rh,
                                  s=(kk == 0), z=(kk == NKC - 1):
                                  e.matmul(o, l, r, start=s, stop=z),
                                  waits=ln_evs + [lw, pb.ev_r] if kk == 0 else [])
                    pb.ev_w = mm
                    gu_guardb[par] = mm
                    if is_gate:
                        ee = p.op("scalar", lambda e, o=g_grp[:, hti * 512:(hti + 1) * 512], i=pb.t[:, 0:512]:
                                  e.activation(o, i, AF.Silu, scale=DEQ),
                                  waits=[pb.ev_w, g_guard[hti]])
                        sil_ev[hti] = ee
                    else:
                        ee = p.op("vector", lambda e, o=h_sb[:, ht * 512:(ht + 1) * 512],
                                  i=g_grp[:, hti * 512:(hti + 1) * 512], u=pb.t[:, 0:512]:
                                  e.tensor_mul(o, i, u), waits=[pb.ev_w, sil_ev[hti]])
                        g_guard[hti] = ee
                    pb.ev_r = ee
                    hevs.append(ee)
                issue_gu()
        for ttl in range(4):
            tt = tq * 4 + ttl
            for oh in range(2):
                pb = [c.pC, c.pD][(ttl * 2 + oh) % 2]
                mm = None
                for hk in range(NHC):
                    lh = h_sb[:, hk * 512 + ttl * 128: hk * 512 + (ttl + 1) * 128]
                    rh = wd_sb[:, hk * D + oh * 512: hk * D + (oh + 1) * 512]
                    mm = p.op("tensor", lambda e, o=pb.t[:, 0:512], l=lh, r=rh,
                              s=(hk == 0), z=(hk == NHC - 1):
                              e.matmul(o, l, r, start=s, stop=z),
                              waits=hevs + [lwd, pb.ev_r] if hk == 0 else [])
                pb.ev_w = mm
                xv = c.xr[:, tt * D + oh * 512: tt * D + (oh + 1) * 512]
                ec = p.op("vector", lambda e, o=xv, i=pb.t[:, 0:512]:
                          e.scalar_tensor_tensor(o, i, DEQ * DEQ, o, ALU.mult, ALU.add),
                          waits=[pb.ev_w])
                pb.ev_r = c.confirm()



def emit_l1attn(c, wxp1_d, wxp2_d, wdt_d, are_d, bdt_d, dd_d, sel_d, dbg=False):
    nc = c.nc
    p = c.p
    wxp1 = c.sb("wxp1", [128, NKC * 80], BF16)
    wxp2 = c.sb("wxp2", [128, NKC * 16], BF16)
    wdt = c.sb("wdt", [64, D], BF16)
    a_re = c.sb("a_re", [128, 128], F32)
    a_sc = c.sb("a_sc", [128, 128], F32)
    bdt = c.sb("bdt", [128, 8], F32)
    dcol = c.sb("dcol", [128, 8], F32)
    sel16 = c.sb("sel16_sb", [16, 16 * 128], BF16)
    dtl = c.sb("dtl", [64, T], BF16)
    b_bf = c.sb("b_bf", [16, T], BF16)
    c_bf = c.sb("c_bf", [16, T], BF16)
    dt_sb = c.sb("dt_sb", [128, NKC * T], BF16)
    u_sb = c.sb("u_sb", [128, NKC * 512], BF16)
    y_acc = c.sb("y_acc", [128, NKC * 512], F32)
    carry = c.sb("carry", [128, NKC * DST], F32)
    brep = c.sb("brep", [128, 512], BF16)
    crep = c.sb("crep", [128, 512], BF16)
    sA = c.sb("sA", [128, 512], F32)
    sB = c.sb("sB", [128, 512], F32)
    a_t = [c.sb("a_t0", [128, 512], BF16), c.sb("a_t1", [128, 512], BF16)]
    b_t = [c.sb("b_t0", [128, 512], BF16), c.sb("b_t1", [128, 512], BF16)]
    h_t = [c.sb("h_t0", [128, 512], BF16), c.sb("h_t1", [128, 512], BF16)]
    y_t = [c.sb("y_t0", [128, 512], BF16), c.sb("y_t1", [128, 512], BF16)]

    lw1 = p.op("gpsimd", lambda e, o=wxp1[:, :].rearrange("p (k o) -> p k o", o=80),
               i=wxp1_d.rearrange("(k p) o -> p k o", p=128): e.dma_start(out=o, in_=i), dma=True)
    lw2 = p.op("gpsimd", lambda e, o=wxp2[:, :].rearrange("p (k o) -> p k o", o=16),
               i=wxp2_d.rearrange("(k p) o -> p k o", p=128): e.dma_start(out=o, in_=i), dma=True)
    lw3 = p.op("gpsimd", lambda e: e.dma_start(out=wdt[:, :], in_=wdt_d[:, :]), dma=True)
    lw4 = p.op("sync", lambda e: e.dma_start(out=a_re[:, :], in_=are_d[:, :]), dma=True)
    lw5 = p.op("sync", lambda e: e.dma_start(out=bdt[:, :], in_=bdt_d[:, :]), dma=True)
    lw6 = p.op("sync", lambda e: e.dma_start(out=dcol[:, :], in_=dd_d[:, :]), dma=True)
    lw7 = p.op("sync", lambda e: e.dma_start(out=sel16[:, :], in_=sel_d[:, :]), dma=True)

    ea1 = p.op("scalar", lambda e: e.activation(a_sc[:, :], a_re[:, :], AF.Exp), waits=[lw4])
    ea2 = p.op("vector", lambda e: e.tensor_scalar_mul(a_sc[:, :], a_sc[:, :], -1.0), waits=[ea1])

    ln_evs = emit_ln_to_fm(c, [c.ev_x])

    xdbl_evs = []
    for tq in range(NTQ):
        p1 = c.pA
        p2 = c.pB
        mm1 = None
        mm2 = None
        for kk in range(NKC):
            lh = wxp1[:, kk * 80:(kk + 1) * 80]
            rh = c.fmA[:, kk * T + tq * 512: kk * T + (tq + 1) * 512]
            mm1 = p.op("tensor", lambda e, o=p1.t[0:80, 0:512], l=lh, r=rh,
                       s=(kk == 0), z=(kk == NKC - 1): e.matmul(o, l, r, start=s, stop=z),
                       waits=ln_evs + [lw1, p1.ev_r] if kk == 0 else [])
        for kk in range(NKC):
            lh = wxp2[:, kk * 16:(kk + 1) * 16]
            rh = c.fmA[:, kk * T + tq * 512: kk * T + (tq + 1) * 512]
            mm2 = p.op("tensor", lambda e, o=p2.t[0:16, 0:512], l=lh, r=rh,
                       s=(kk == 0), z=(kk == NKC - 1): e.matmul(o, l, r, start=s, stop=z),
                       waits=[lw2, p2.ev_r] if kk == 0 else [])
        p1.ev_w = mm1
        p2.ev_w = mm2
        e1 = p.op("vector", lambda e, o=dtl[:, tq * 512:(tq + 1) * 512], i=p1.t[0:64, 0:512]:
                  e.tensor_copy(o, i), waits=[mm1])
        e2 = p.op("vector", lambda e, o=b_bf[:, tq * 512:(tq + 1) * 512], i=p1.t[64:80, 0:512]:
                  e.tensor_copy(o, i), waits=[mm1])
        e3 = p.op("vector", lambda e, o=c_bf[:, tq * 512:(tq + 1) * 512], i=p2.t[0:16, 0:512]:
                  e.tensor_copy(o, i), waits=[mm2])
        p1.ev_r = e2
        p2.ev_r = e3
        xdbl_evs += [e1, e2, e3]

    dt_evs = []
    prev_ad = None
    for dc in range(NKC):
        for tq in range(NTQ):
            pb = [c.pA, c.pB][(dc * NTQ + tq) % 2]
            bcol = bdt[:, dc:dc + 1]
            mm = p.op("tensor", lambda e, o=pb.t[:, 0:512], l=wdt[:, dc * 128:(dc + 1) * 128],
                      r=dtl[0:64, tq * 512:(tq + 1) * 512]:
                      e.matmul(o, l, r, start=True, stop=True),
                      waits=xdbl_evs + [lw3, pb.ev_r])
            pb.ev_w = mm
            a1 = p.op("scalar", lambda e, o=sA, i=pb.t[:, 0:512], b=bcol:
                      e.activation(o[:, :], i, AF.Abs, bias=b), waits=[mm, lw5, prev_ad])
            a2 = p.op("scalar", lambda e, o=sB, i=sA: e.activation(o[:, :], i[:, :], AF.Exp, scale=-1.0),
                      waits=[a1])
            a3 = p.op("scalar", lambda e, o=sA, i=sB: e.activation(o[:, :], i[:, :], AF.Ln, bias=1.0),
                      waits=[a2])
            a4 = p.op("scalar", lambda e, o=sB, i=pb.t[:, 0:512], b=bcol:
                      e.activation(o[:, :], i, AF.Relu, bias=b), waits=[a3])
            pb.ev_r = a4
            ad = p.op("vector", lambda e, o=dt_sb[:, dc * T + tq * 512: dc * T + (tq + 1) * 512],
                      i=sA, j=sB: e.tensor_add(o, i[:, :], j[:, :]), waits=[a3, a4])
            dt_evs.append(ad)
            prev_ad = ad

    if dbg:
        ddtl = nc.dram_tensor("dbg_dtl", [64, T], BF16, kind="ExternalOutput").ap()
        dbb = nc.dram_tensor("dbg_b", [16, T], BF16, kind="ExternalOutput").ap()
        dcb = nc.dram_tensor("dbg_c", [16, T], BF16, kind="ExternalOutput").ap()
        ddt = nc.dram_tensor("dbg_dt", [128, NKC * T], BF16, kind="ExternalOutput").ap()
        p.op("sync", lambda e: e.dma_start(out=ddtl[:, :], in_=dtl[:, :]), waits=xdbl_evs, dma=True)
        p.op("sync", lambda e: e.dma_start(out=dbb[:, :], in_=b_bf[:, :]), waits=xdbl_evs, dma=True)
        p.op("sync", lambda e: e.dma_start(out=dcb[:, :], in_=c_bf[:, :]), waits=xdbl_evs, dma=True)
        p.op("sync", lambda e: e.dma_start(out=ddt[:, :], in_=dt_sb[:, :]), waits=dt_evs, dma=True)

    c.stage_guard = None
    y_guard = [None] * NKC
    fin_evs = []
    for tq in range(NTQ):
        u_evs = []
        for dc in range(NKC):
            ue = p.op("vector", lambda e, o=u_sb[:, dc * 512:(dc + 1) * 512],
                      i=dt_sb[:, dc * T + tq * 512: dc * T + (tq + 1) * 512],
                      j=c.fmA[:, dc * T + tq * 512: dc * T + (tq + 1) * 512]:
                      e.tensor_mul(o, i, j), waits=dt_evs + ln_evs + [y_guard[dc]])
            u_evs.append(ue)
        scan_prev = [None, None]
        pool_prev = [None, None]
        for n in range(DST):
            mmB = p.op("tensor", lambda e, o=c.pC.t[:, 0:512], l=sel16[:, n * 128:(n + 1) * 128],
                       r=b_bf[:, tq * 512:(tq + 1) * 512]:
                       e.matmul(o, l, r, start=True, stop=True),
                       waits=xdbl_evs + [lw7, c.pC.ev_r])
            c.pC.ev_w = mmB
            eb = p.op("vector", lambda e, o=brep, i=c.pC.t[:, 0:512]: e.tensor_copy(o[:, :], i),
                      waits=[mmB, scan_prev[0], scan_prev[1]])
            c.pC.ev_r = eb
            mmC = p.op("tensor", lambda e, o=c.pD.t[:, 0:512], l=sel16[:, n * 128:(n + 1) * 128],
                       r=c_bf[:, tq * 512:(tq + 1) * 512]:
                       e.matmul(o, l, r, start=True, stop=True),
                       waits=[c.pD.ev_r])
            c.pD.ev_w = mmC
            ec = p.op("vector", lambda e, o=crep, i=c.pD.t[:, 0:512]: e.tensor_copy(o[:, :], i),
                      waits=[mmC, pool_prev[0], pool_prev[1]])
            c.pD.ev_r = ec
            for dc in range(NKC):
                par = dc % 2
                av = p.op("scalar", lambda e, o=a_t[par], i=dt_sb[:, dc * T + tq * 512: dc * T + (tq + 1) * 512],
                          s=a_sc[:, dc * 16 + n: dc * 16 + n + 1]:
                          e.activation(o[:, :], i, AF.Exp, scale=s),
                          waits=[ea2, scan_prev[par]] + dt_evs)
                bv = p.op("vector", lambda e, o=b_t[par], i=u_sb[:, dc * 512:(dc + 1) * 512], j=brep:
                          e.tensor_mul(o[:, :], i, j[:, :]), waits=u_evs + [eb, scan_prev[par]])
                init = 0.0 if tq == 0 else carry[:, dc * DST + n: dc * DST + n + 1]
                hv = p.op("vector", lambda e, o=h_t[par], i=a_t[par], j=b_t[par], z=init:
                          e.tensor_tensor_scan(o[:, :], i[:, :], j[:, :], z, ALU.mult, ALU.add),
                          waits=[av, bv, pool_prev[par]])
                cv = p.op("vector", lambda e, o=carry[:, dc * DST + n: dc * DST + n + 1], i=h_t[par]:
                          e.tensor_copy(o, i[:, 511:512]), waits=[hv])
                yeng = getattr(build_l1attn, "Y_ENGINE", "gpsimd")
                yv = p.op(yeng, lambda e, o=y_t[par], i=h_t[par], j=crep:
                          e.tensor_mul(o[:, :], i[:, :], j[:, :]), waits=[hv, ec])
                if n == 0:
                    ya = p.op(yeng, lambda e, o=y_acc[:, dc * 512:(dc + 1) * 512], i=y_t[par]:
                              e.tensor_copy(o, i[:, :]), waits=[yv, y_guard[dc]])
                else:
                    ya = p.op(yeng, lambda e, o=y_acc[:, dc * 512:(dc + 1) * 512], i=y_t[par]:
                              e.tensor_add(o, o, i[:, :]), waits=[yv])
                scan_prev[par] = cv
                pool_prev[par] = ya
        if dbg and tq == 0:
            dya = nc.dram_tensor("dbg_yacc", [128, NKC * 512], F32, kind="ExternalOutput").ap()
            dcr = nc.dram_tensor("dbg_carry", [128, NKC * DST], F32, kind="ExternalOutput").ap()
            p.op("sync", lambda e: e.dma_start(out=dya[:, :], in_=y_acc[:, :]),
                 waits=[pool_prev[0], pool_prev[1]], dma=True)
            p.op("sync", lambda e: e.dma_start(out=dcr[:, :], in_=carry[:, :]),
                 waits=[scan_prev[0], scan_prev[1]], dma=True)
        fence_mode = getattr(build_l1attn, "FENCE_MODE", "fence")
        if fence_mode == "fence":
            p.fence()
        elif fence_mode == "sblock":
            p.wait("scalar", [("Ev", p.cnt["Ev"]), ("Eg", p.cnt["Eg"])])
        elif fence_mode == "tblock":
            p.wait("tensor", [("Ev", p.cnt["Ev"]), ("Eg", p.cnt["Eg"])])
        elif fence_mode == "gblock":
            p.wait("gpsimd", [("Ev", p.cnt["Ev"]), ("Es", p.cnt["Es"])])
        elif fence_mode == "slack":
            yeng0 = getattr(build_l1attn, "Y_ENGINE", "gpsimd")
            nslack = getattr(build_l1attn, "NSLACK", 4)
            sl = None
            for _ in range(nslack):
                sl = p.op(yeng0, lambda e: e.tensor_copy(sB[:, 0:512], sA[:, 0:512]),
                          waits=[pool_prev[0], pool_prev[1]])
            pool_prev = [sl, sl]
        elif fence_mode == "et":
            p.wait("vector", [("Et", p.cnt["Et"])])
        elif fence_mode == "es":
            p.wait("vector", [("Es", p.cnt["Es"])])
        elif fence_mode == "eg":
            p.wait("vector", [("Eg", p.cnt["Eg"])])
        if dbg and tq == 1:
            park = c.sb("park", [128, 2 * 512], F32)
            pk = p.op("vector", lambda e: e.tensor_copy(park[:, :], y_acc[:, 6 * 512:8 * 512]),
                      waits=[pool_prev[0], pool_prev[1]])
            dpk = nc.dram_tensor("dbg_park", [128, 2 * 512], F32, kind="ExternalOutput").ap()
            p.op("sync", lambda e: e.dma_start(out=dpk[:, :], in_=park[:, :]), waits=[pk], dma=True)
        for dc in range(NKC):
            fe = p.op("vector", lambda e, o=c.stageF,
                      i=c.fmA[:, dc * T + tq * 512: dc * T + (tq + 1) * 512],
                      s=dcol[:, dc:dc + 1], j=y_acc[:, dc * 512:(dc + 1) * 512]:
                      e.scalar_tensor_tensor(o[:, :], i, s, j, ALU.mult, ALU.add),
                      waits=[pool_prev[0], pool_prev[1], lw6, c.stage_guard])
            evs = emit_fm_psum_to_resid(c, c.stageF, dc, tq, [fe])
            c.stage_guard = evs[-1]
            y_guard[dc] = fe
            fin_evs += evs



def emit_l1ffn(c, w1_d, w2_d, b1_d, b2_d, gw_d, gb_d, temp_d, ones_row_d, sel8_d, dbg=False):
    nc = c.nc
    p = c.p
    NHC2 = EH // 128
    h_half = c.sb("h_half", [128, 16 * 1024], BF16)
    out_acc = c.sb("out_acc", [128, 8 * 1024], F32)
    wst = [c.sb("wst0", [128, 4096], BF16), c.sb("wst1", [128, 4096], BF16)]
    gw_sb = c.sb("gw_sb", [128, NKC * 8], BF16)
    gbr = c.sb("gbr", [128, 8], F32)
    onesr = c.sb("onesr", [1, 128], BF16)
    sel8 = c.sb("sel8_sb", [8, 8 * 128], BF16)
    b1sb = c.sb("b1sb", [8, 1024], F32)
    b1T = c.sb("b1T", [128, NHC2 * 8], F32)
    b2sb = c.sb("b2sb", [8, D], BF16)
    itemp = c.sb("itemp", [1, 1], F32)
    itcol = c.sb("itcol", [128, 1], F32)
    gates = c.sb("gates", [128, 8], F32)
    m1c = c.sb("m1c", [128, 1], F32)
    m2c = c.sb("m2c", [128, 1], F32)
    eq1 = c.sb("eq1", [128, 8], F32)
    eq2 = c.sb("eq2", [128, 8], F32)
    s1c = c.sb("s1c", [128, 1], F32)
    s2c = c.sb("s2c", [128, 1], F32)
    tmp8 = c.sb("tmp8", [128, 8], F32)
    w_tok = c.sb("w_tok", [128, NTT * 8], F32)
    w_fm = c.sb("w_fm", [8, T], BF16)
    wrep = c.sb("wrep", [128, 1024], BF16)
    g_t = [c.sb("g_t0", [128, 512], BF16), c.sb("g_t1", [128, 512], BF16)]

    lo = p.op("sync", lambda e: e.dma_start(out=onesr[:, :], in_=ones_row_d[:, :]), dma=True)
    ls8 = p.op("sync", lambda e: e.dma_start(out=sel8[:, :], in_=sel8_d[:, :]), dma=True)
    lgw = p.op("gpsimd", lambda e, o=gw_sb[:, :].rearrange("p (k e) -> p k e", e=8),
               i=gw_d.rearrange("(k p) e -> p k e", p=128): e.dma_start(out=o, in_=i), dma=True)
    gbf32 = c.sb("gbf32", [1, 8], F32)
    lgb = p.op("sync", lambda e: e.dma_start(out=gbf32[:, :], in_=gb_d[:, :]), dma=True)
    ltm = p.op("sync", lambda e: e.dma_start(out=itemp[:, :], in_=temp_d[:, :]), dma=True)
    lb2 = p.op("gpsimd", lambda e: e.dma_start(out=b2sb[:, :], in_=b2_d[:, :]), dma=True)
    gb_bf = c.sb("gb_bf", [1, 8], BF16)
    egb = p.op("vector", lambda e: e.tensor_copy(gb_bf[:, :], gbf32[:, :]), waits=[lgb])

    ln_evs = emit_ln_to_fm(c, [c.ev_x])
    p.fence()

    e1t = p.op("vector", lambda e: e.reciprocal(itemp[:, :], itemp[:, :]), waits=[ltm])
    it_bf = c.sb("it_bf", [1, 1], BF16)
    e2t = p.op("vector", lambda e: e.tensor_copy(it_bf[:, :], itemp[:, :]), waits=[e1t])
    mmt = p.op("tensor", lambda e: e.matmul(c.pF1.t[:, 0:1], onesr[:, :], it_bf[:, :], start=True, stop=True),
               waits=[e2t, lo, c.pF1.ev_r])
    c.pF1.ev_w = mmt
    e3t = p.op("vector", lambda e: e.tensor_copy(itcol[:, :], c.pF1.t[:, 0:1]), waits=[mmt])
    c.pF1.ev_r = c.confirm()
    mmg = p.op("tensor", lambda e: e.matmul(c.pF2.t[:, 0:8], onesr[:, :], gb_bf[:, :], start=True, stop=True),
               waits=[egb, lo, c.pF2.ev_r])
    c.pF2.ev_w = mmg
    e4t = p.op("vector", lambda e, s=itcol: e.tensor_scalar(gbr[:, :], c.pF2.t[:, 0:8], s[:, :], None, ALU.mult),
               waits=[mmg, e3t])
    c.pF2.ev_r = c.confirm()

    b1evs = []
    b1_rd_guard = None
    lb1 = None
    for hc in range(NHC2):
        if hc % 8 == 0:
            q = hc // 8
            lb1 = p.op("sync", lambda e, o=b1sb, i=b1_d[:, q * 1024:(q + 1) * 1024]:
                       e.dma_start(out=o[:, :], in_=i), waits=[b1_rd_guard], dma=True)
        pf = [c.pF1, c.pF2][hc % 2]
        tr = p.op("tensor", lambda e, o=pf.t[:, 0:8], i=b1sb[:, (hc % 8) * 128:(hc % 8 + 1) * 128]:
                  e.transpose(o, i, c.ident_f[0:8, 0:8]),
                  waits=[lb1, c.ev_identf, pf.ev_r])
        b1_rd_guard = tr
        pf.ev_w = tr
        ee = p.op("vector", lambda e, o=b1T[:, hc * 8:(hc + 1) * 8], i=pf.t[:, 0:8]:
                  e.tensor_copy(o, i), waits=[tr])
        pf.ev_r = c.confirm()
        b1evs.append(ee)

    p.fence()
    wtok_evs = []
    prev8 = None
    prev10 = None
    for tt in range(NTT):
        pb = [c.pC, c.pD][tt % 2]
        mm = None
        for kk in range(NKC):
            lh = c.fmA[:, kk * T + tt * 128: kk * T + (tt + 1) * 128]
            rh = gw_sb[:, kk * 8:(kk + 1) * 8]
            mm = p.op("tensor", lambda e, o=pb.t[:, 0:8], l=lh, r=rh, s=(kk == 0), z=(kk == NKC - 1):
                      e.matmul(o, l, r, start=s, stop=z),
                      waits=ln_evs + [lgw, pb.ev_r] if kk == 0 else [])
        pb.ev_w = mm
        ge = p.op("vector", lambda e, o=gates, i=pb.t[:, 0:8]:
                  e.scalar_tensor_tensor(o[:, :], i, itcol[:, :], gbr[:, :], ALU.mult, ALU.add),
                  waits=[mm, e4t, e3t] + ([c.dbg_gev] if dbg and getattr(c, "dbg_gev", None) else []))
        pb.ev_r = c.confirm()
        if dbg:
            if tt == 0:
                c.dbg_g = nc.dram_tensor("dbg_gates", [128, NTT * 8], F32, kind="ExternalOutput").ap()
                c.dbg_praw = nc.dram_tensor("dbg_praw", [128, NTT * 8], F32, kind="ExternalOutput").ap()
            dgr = p.op("vector", lambda e, o=c.tmp16[:, 0:8], i=pb.t[:, 0:8]: e.tensor_copy(o, i),
                       waits=[mm, getattr(c, "dbg_gev2", None)])
            c.dbg_gev = p.op("sync", lambda e, o=c.dbg_g[:, tt * 8:(tt + 1) * 8]: e.dma_start(out=o, in_=gates[:, :]),
                             waits=[ge], dma=True)
            c.dbg_gev2 = p.op("sync", lambda e, o=c.dbg_praw[:, tt * 8:(tt + 1) * 8]:
                              e.dma_start(out=o, in_=c.tmp16[:, 0:8]), waits=[dgr], dma=True)
        ee1 = p.op("vector", lambda e: e.reduce_max(m1c[:, :], gates[:, :], axis=AX.X), waits=[ge])
        ee2 = p.op("vector", lambda e: e.tensor_single_scalar(eq1[:, :], gates[:, :], m1c[:, :], ALU.is_ge),
                   waits=[ee1])
        ee3 = p.op("vector", lambda e: e.scalar_tensor_tensor(tmp8[:, :], eq1[:, :], -1e30, gates[:, :],
                                                             ALU.mult, ALU.add), waits=[ee2])
        ee4 = p.op("vector", lambda e: e.reduce_max(m2c[:, :], tmp8[:, :], axis=AX.X), waits=[ee3])
        ee5 = p.op("vector", lambda e: e.tensor_single_scalar(eq2[:, :], tmp8[:, :], m2c[:, :], ALU.is_ge),
                   waits=[ee4])
        ee6 = p.op("vector", lambda e: e.tensor_sub(c.halfcol[:, :], m1c[:, :], m2c[:, :]), waits=[ee4, prev8])
        ee7 = p.op("scalar", lambda e: e.activation(s1c[:, :], c.halfcol[:, :], AF.Sigmoid), waits=[ee6, prev10])
        ee8 = p.op("scalar", lambda e: e.activation(s2c[:, :], s1c[:, :], AF.Copy, scale=-1.0, bias=1.0),
                   waits=[ee7])
        ee9 = p.op("vector", lambda e: e.tensor_scalar(tmp8[:, :], eq1[:, :], s1c[:, :], None, ALU.mult),
                   waits=[ee7, ee5])
        ee10 = p.op("vector", lambda e, o=w_tok[:, tt * 8:(tt + 1) * 8]:
                    e.scalar_tensor_tensor(o, eq2[:, :], s2c[:, :], tmp8[:, :], ALU.mult, ALU.add),
                    waits=[ee8, ee9])
        prev8 = ee8
        prev10 = ee10
        wtok_evs.append(ee10)

    p.fence()
    if dbg:
        dwt = nc.dram_tensor("dbg_wtok", [128, NTT * 8], F32, kind="ExternalOutput").ap()
        p.op("sync", lambda e: e.dma_start(out=dwt[:, :], in_=w_tok[:, :]), waits=wtok_evs, dma=True)
        db1 = nc.dram_tensor("dbg_b1T", [128, NHC2 * 8], F32, kind="ExternalOutput").ap()
        p.op("sync", lambda e: e.dma_start(out=db1[:, :], in_=b1T[:, :]), waits=b1evs, dma=True)

    wfm_evs = []
    for tt in range(NTT):
        pf = [c.pF1, c.pF2][tt % 2]
        tr = p.op("tensor", lambda e, o=pf.t[0:8, 0:128], i=w_tok[:, tt * 8:(tt + 1) * 8]:
                  e.transpose(o, i, c.ident_f[:, :]),
                  waits=wtok_evs + [c.ev_identf, pf.ev_r])
        pf.ev_w = tr
        ee = p.op("vector", lambda e, o=w_fm[:, tt * 128:(tt + 1) * 128], i=pf.t[0:8, 0:128]:
                  e.tensor_copy(o, i), waits=[tr])
        pf.ev_r = ee
        wfm_evs.append(ee)

    p.fence()

    gate_done = ("Ev", p.cnt["Ev"])
    c.w1_guard = gate_done
    c.w2_guard = gate_done
    g_guard = [None, None]
    w_seq = []
    for _th in range(2):
        for ei in range(NE):
            for hh in range(2):
                for hg in range(4):
                    w_seq.append(("w1", ei, hh, hg))
                for grp in range(4):
                    w_seq.append(("w2", ei, hh, grp))
    w_evs = {}
    w_guardb = [gate_done, gate_done]
    w_next = [0]

    def issue_w():
        i = w_next[0]
        if i >= len(w_seq):
            return
        w_next[0] = i + 1
        kind, ei, hh, j = w_seq[i]
        par = i % 2
        if kind == "w1":
            hbase = hh * 2048 + j * 512
            src = w1_d[ei, :, hbase:hbase + 512].rearrange("(k p) h -> p k h", p=128)
            dst = wst[par][:, :].rearrange("p (k h) -> p k h", h=512)
        else:
            rbase = hh * 2048 + j * 512
            src = w2_d[ei, rbase:rbase + 512, :].rearrange("(k p) o -> p k o", p=128)
            dst = wst[par][:, :].rearrange("p (k o) -> p k o", o=D)
        w_evs[i] = p.op("gpsimd", lambda e, o=dst, i_=src:
                        e.dma_start(out=o, in_=i_), waits=[w_guardb[par]], dma=True)

    issue_w()
    issue_w()
    w_i = [0]

    for th in range(2):
        oacc_evs = []
        first_e = True
        for ei in range(NE):
            wr_evs = []
            for tsub in range(2):
                mmr = p.op("tensor", lambda e, o=c.pD.t[:, 0:512],
                           l=sel8[:, ei * 128:(ei + 1) * 128],
                           r=w_fm[:, th * 1024 + tsub * 512: th * 1024 + (tsub + 1) * 512],
                           s=True, z=True:
                           e.matmul(o, l, r, start=s, stop=z),
                           waits=wfm_evs + [ls8, c.pD.ev_r])
                c.pD.ev_w = mmr
                er = p.op("vector", lambda e, o=wrep[:, tsub * 512:(tsub + 1) * 512], i=c.pD.t[:, 0:512]:
                          e.tensor_copy(o, i), waits=[mmr])
                c.pD.ev_r = c.confirm()
                wr_evs.append(er)
            for hh in range(2):
                ph_evs = []
                for hg in range(4):
                    gi = w_i[0]
                    w_i[0] = gi + 1
                    par = gi % 2
                    lw1 = w_evs[gi]
                    for hti in range(4):
                        hc = hh * 16 + hg * 4 + hti
                        hloc = hg * 4 + hti
                        for tsub in range(2):
                            pb = [c.pA, c.pB][tsub]
                            mm = None
                            for kk in range(NKC):
                                lh = wst[par][:, kk * 512 + hti * 128: kk * 512 + (hti + 1) * 128]
                                rh = c.fmA[:, kk * T + th * 1024 + tsub * 512: kk * T + th * 1024 + (tsub + 1) * 512]
                                mm = p.op("tensor", lambda e, o=pb.t[:, 0:512], l=lh, r=rh,
                                          s=(kk == 0), z=(kk == NKC - 1):
                                          e.matmul(o, l, r, start=s, stop=z),
                                          waits=ln_evs + [lw1, pb.ev_r] if kk == 0 else [])
                            pb.ev_w = mm
                            w_guardb[par] = mm
                            gl = p.op("scalar", lambda e, o=g_t[tsub], i=pb.t[:, 0:512],
                                      b=b1T[:, hc * 8 + ei: hc * 8 + ei + 1]:
                                      e.activation(o[:, :], i, AF.Gelu, bias=b),
                                      waits=b1evs + [pb.ev_w, g_guard[tsub]])
                            wm = p.op("vector", lambda e, o=h_half[:, hloc * 1024 + tsub * 512: hloc * 1024 + (tsub + 1) * 512],
                                      i=g_t[tsub], w=wrep[:, tsub * 512:(tsub + 1) * 512]:
                                      e.tensor_mul(o, i[:, :], w[:, :]),
                                      waits=wr_evs + [gl])
                            g_guard[tsub] = wm
                            pb.ev_r = wm
                            ph_evs.append(wm)
                    issue_w()
                for grp in range(4):
                    gi = w_i[0]
                    w_i[0] = gi + 1
                    par = gi % 2
                    lw2 = w_evs[gi]
                    for ttl in range(8):
                        for oh in range(2):
                            pb = [c.pC, c.pD][(ttl * 2 + oh) % 2]
                            mm = None
                            for kk in range(4):
                                hloc = grp * 4 + kk
                                lh = h_half[:, hloc * 1024 + ttl * 128: hloc * 1024 + (ttl + 1) * 128]
                                rh = wst[par][:, kk * D + oh * 512: kk * D + (oh + 1) * 512]
                                mm = p.op("tensor", lambda e, o=pb.t[:, 0:512], l=lh, r=rh,
                                          s=(kk == 0), z=(kk == 3):
                                          e.matmul(o, l, r, start=s, stop=z),
                                          waits=ph_evs + [lw2, pb.ev_r] if kk == 0 else [])
                            pb.ev_w = mm
                            w_guardb[par] = mm
                            oslice = out_acc[:, ttl * 1024 + oh * 512: ttl * 1024 + (oh + 1) * 512]
                            if first_e and hh == 0 and grp == 0:
                                ea = p.op("vector", lambda e, o=oslice, i=pb.t[:, 0:512]:
                                          e.tensor_copy(o, i), waits=[mm])
                            else:
                                ea = p.op("vector", lambda e, o=oslice, i=pb.t[:, 0:512]:
                                          e.tensor_add(o, i, o), waits=[mm])
                            pb.ev_r = ea
                            oacc_evs.append(ea)
                    issue_w()
            first_e = False
        p.fence()
        for ttl in range(8):
            tt = th * 8 + ttl
            for oh in range(2):
                pb = [c.pA, c.pB][oh]
                mm = p.op("tensor", lambda e, o=pb.t[:, 0:512],
                          l=w_fm[:, tt * 128:(tt + 1) * 128],
                          r=b2sb[:, oh * 512:(oh + 1) * 512], s=True, z=True:
                          e.matmul(o, l, r, start=s, stop=z),
                          waits=wfm_evs + [lb2, pb.ev_r])
                pb.ev_w = mm
                xv = c.xr[:, tt * D + oh * 512: tt * D + (oh + 1) * 512]
                e1 = p.op("vector", lambda e, o=xv, i=out_acc[:, ttl * 1024 + oh * 512: ttl * 1024 + (oh + 1) * 512]:
                          e.tensor_add(o, i, o), waits=oacc_evs)
                e2 = p.op("vector", lambda e, o=xv, i=pb.t[:, 0:512]:
                          e.tensor_add(o, i, o), waits=[mm, e1])
                pb.ev_r = c.confirm()
        p.fence()



def _mk(nc_args=()):
    nc = bass.Bass("TRN2", target_bir_lowering=False)
    return nc


def _declare_common(nc):
    x_d = nc.dram_tensor("x", [T, D], F32, kind="ExternalInput").ap()
    identb_d = nc.dram_tensor("ident_bf", [128, 128], BF16, kind="ExternalInput").ap()
    identf_d = nc.dram_tensor("ident_f", [128, 128], F32, kind="ExternalInput").ap()
    out_d = nc.dram_tensor("xout", [T, D], F32, kind="ExternalOutput").ap()
    return x_d, identb_d, identf_d, out_d


def _decl_l0attn(nc):
    return (nc.dram_tensor("wqT", [D, D], F32, kind="ExternalInput").ap(),
            nc.dram_tensor("wkT", [D, D], F32, kind="ExternalInput").ap(),
            nc.dram_tensor("wvT", [D, D], F32, kind="ExternalInput").ap(),
            nc.dram_tensor("woT", [D, D], F32, kind="ExternalInput").ap(),
            nc.dram_tensor("mask4", [128, 4 * 512], BF16, kind="ExternalInput").ap(),
            nc.dram_tensor("ones_col", [128, 1], BF16, kind="ExternalInput").ap())


def _decl_l0ffn(nc):
    return (nc.dram_tensor("wgT", [D, FFH], I32, kind="ExternalInput").ap(),
            nc.dram_tensor("wuT", [D, FFH], I32, kind="ExternalInput").ap(),
            nc.dram_tensor("wdT", [FFH, D], I32, kind="ExternalInput").ap())


def _decl_l1attn(nc):
    return (nc.dram_tensor("wxp1T", [D, 80], F32, kind="ExternalInput").ap(),
            nc.dram_tensor("wxp2T", [D, 16], F32, kind="ExternalInput").ap(),
            nc.dram_tensor("wdtT", [DTR, D], F32, kind="ExternalInput").ap(),
            nc.dram_tensor("A_re", [128, 128], F32, kind="ExternalInput").ap(),
            nc.dram_tensor("b_dt_c", [128, 8], F32, kind="ExternalInput").ap(),
            nc.dram_tensor("D_c", [128, 8], F32, kind="ExternalInput").ap(),
            nc.dram_tensor("sel16", [16, 16 * 128], BF16, kind="ExternalInput").ap())


def _decl_l1ffn(nc):
    return (nc.dram_tensor("w1", [NE, D, EH], F32, kind="ExternalInput").ap(),
            nc.dram_tensor("w2", [NE, EH, D], F32, kind="ExternalInput").ap(),
            nc.dram_tensor("b1", [NE, EH], F32, kind="ExternalInput").ap(),
            nc.dram_tensor("b2", [NE, D], F32, kind="ExternalInput").ap(),
            nc.dram_tensor("gw", [D, NE], F32, kind="ExternalInput").ap(),
            nc.dram_tensor("gb", [1, NE], F32, kind="ExternalInput").ap(),
            nc.dram_tensor("temp", [1, 1], F32, kind="ExternalInput").ap(),
            nc.dram_tensor("ones_row", [1, 128], BF16, kind="ExternalInput").ap(),
            nc.dram_tensor("sel8", [8, 8 * 128], BF16, kind="ExternalInput").ap())


def _standalone(decl_fn, emit_fn, dbg=False):
    nc = bass.Bass("TRN2", target_bir_lowering=False)
    x_d, identb_d, identf_d, out_d = _declare_common(nc)
    drs = decl_fn(nc)
    with ExitStack() as stack:
        p = Plan(nc, stack)
        c = mk_ctx(nc, stack, p)
        alloc_common(c)
        emit_load_consts(c, identb_d, identf_d)
        emit_load_x(c, x_d)
        with ExitStack() as bst:
            c.stacks.append(bst)
            emit_fn(c, *drs, dbg=dbg)
            p.fence()
            c.stacks.pop()
        emit_store_x(c, out_d)
        p.wait("gpsimd", p.now())
        with nc.Block() as block:
            p.build(block)
    return nc


def build_l0attn(dbg=False):
    return _standalone(_decl_l0attn, emit_l0attn, dbg)


def build_l0ffn(dbg=False):
    return _standalone(_decl_l0ffn, emit_l0ffn, dbg)


def build_l1attn(dbg=False):
    return _standalone(_decl_l1attn, emit_l1attn, dbg)


def build_l1ffn(dbg=False):
    return _standalone(_decl_l1ffn, emit_l1ffn, dbg)


def build_fused():
    nc = bass.Bass("TRN2", target_bir_lowering=False)
    x_d, identb_d, identf_d, out_d = _declare_common(nc)
    dr0a = _decl_l0attn(nc)
    dr0f = _decl_l0ffn(nc)
    dr1a = _decl_l1attn(nc)
    dr1f = _decl_l1ffn(nc)
    with ExitStack() as stack:
        p = Plan(nc, stack)
        c = mk_ctx(nc, stack, p)
        alloc_common(c)
        emit_load_consts(c, identb_d, identf_d)
        emit_load_x(c, x_d)
        for emit_fn, drs in ((emit_l0attn, dr0a), (emit_l0ffn, dr0f),
                             (emit_l1attn, dr1a), (emit_l1ffn, dr1f)):
            with ExitStack() as bst:
                c.stacks.append(bst)
                emit_fn(c, *drs)
                p.fence()
                c.stacks.pop()
        emit_store_x(c, out_d)
        p.wait("gpsimd", p.now())
        with nc.Block() as block:
            p.build(block)
    return nc

def run_l1ffn(x_full, W1, b1, W2, b2, gate_w, gate_b, temperature):
    nc = build_l1ffn()
    ident_bf, ident_f, _, _ = host_consts()
    sel8 = np.zeros((8, 8 * 128), dtype=BF)
    for e in range(8):
        sel8[e, e * 128:(e + 1) * 128] = 1.0
    ones_row = np.ones((1, 128), dtype=BF)
    in_maps = []
    for i in range(8):
        in_maps.append(dict(x=np.ascontiguousarray(x_full[i]),
                            w1=np.asarray(W1), w2=np.asarray(W2),
                            b1=np.asarray(b1), b2=np.asarray(b2),
                            gw=np.asarray(gate_w), gb=np.asarray(gate_b).reshape(1, 8),
                            temp=np.asarray(temperature).reshape(1, 1),
                            ones_row=ones_row, sel8=sel8,
                            ident_bf=ident_bf, ident_f=ident_f))
    res = run_bass_kernel_spmd(nc, in_maps, core_ids=list(range(8)))
    return np.stack([r["xout"] for r in res.results])



def host_consts():
    ident_bf = np.eye(128, dtype=BF)
    ident_f = np.eye(128, dtype=np.float32)
    mask4 = np.zeros((128, 4 * 512), dtype=BF)
    for rel in range(4):
        r = np.arange(128)[:, None]
        j = np.arange(512)[None, :]
        mask4[:, rel * 512:(rel + 1) * 512] = (j >= r + rel * 128).astype(BF)
    ones_col = np.ones((128, 1), dtype=BF)
    return ident_bf, ident_f, mask4, ones_col


def run_l0attn(x_full, Wq, Wk, Wv, Wo):
    nc = build_l0attn()
    ident_bf, ident_f, mask4, ones_col = host_consts()
    wqT = np.ascontiguousarray(Wq.T)
    wkT = np.ascontiguousarray(Wk.T)
    wvT = np.ascontiguousarray(Wv.T)
    woT = np.ascontiguousarray(Wo.T)
    in_maps = []
    for i in range(8):
        in_maps.append(dict(x=np.ascontiguousarray(x_full[i]), wqT=wqT, wkT=wkT, wvT=wvT, woT=woT,
                            ident_bf=ident_bf, ident_f=ident_f, mask4=mask4, ones_col=ones_col))
    res = run_bass_kernel_spmd(nc, in_maps, core_ids=list(range(8)))
    return np.stack([r["xout"] for r in res.results])


def run_l0ffn(x_full, w_gate, w_up, w_down):
    nc = build_l0ffn()
    ident_bf, ident_f, _, _ = host_consts()
    wgT = np.ascontiguousarray(w_gate.T)
    wuT = np.ascontiguousarray(w_up.T)
    wdT = np.ascontiguousarray(w_down.T)
    in_maps = []
    for i in range(8):
        in_maps.append(dict(x=np.ascontiguousarray(x_full[i]), wgT=wgT, wuT=wuT, wdT=wdT,
                            ident_bf=ident_bf, ident_f=ident_f))
    res = run_bass_kernel_spmd(nc, in_maps, core_ids=list(range(8)))
    return np.stack([r["xout"] for r in res.results])


def _l1attn_host(A_log, Dp, W_xproj, W_dt, b_dt):
    wxpT = np.ascontiguousarray(np.asarray(W_xproj).T)
    sel16 = np.zeros((16, 16 * 128), dtype=BF)
    for n in range(16):
        sel16[n, n * 128:(n + 1) * 128] = 1.0
    return dict(
        wxp1T=np.ascontiguousarray(wxpT[:, 0:80]),
        wxp2T=np.ascontiguousarray(wxpT[:, 80:96]),
        wdtT=np.ascontiguousarray(np.asarray(W_dt).T),
        A_re=np.ascontiguousarray(np.asarray(A_log).reshape(8, 128, 16).transpose(1, 0, 2).reshape(128, 128)),
        b_dt_c=np.ascontiguousarray(np.asarray(b_dt).reshape(8, 128).T),
        D_c=np.ascontiguousarray(np.asarray(Dp).reshape(8, 128).T),
        sel16=sel16)


def run_l1attn(x_full, A_log, Dp, W_xproj, W_dt, b_dt):
    nc = build_l1attn()
    ident_bf, ident_f, _, _ = host_consts()
    hm = _l1attn_host(A_log, Dp, W_xproj, W_dt, b_dt)
    in_maps = []
    for i in range(8):
        in_maps.append(dict(x=np.ascontiguousarray(x_full[i]), ident_bf=ident_bf, ident_f=ident_f, **hm))
    res = run_bass_kernel_spmd(nc, in_maps, core_ids=list(range(8)))
    return np.stack([r["xout"] for r in res.results])


def _sel8_host():
    sel8 = np.zeros((8, 8 * 128), dtype=BF)
    for e in range(8):
        sel8[e, e * 128:(e + 1) * 128] = 1.0
    return sel8


def kernel(**inputs):
    g = lambda k: np.asarray(inputs[k])
    x = np.asarray(inputs["L0_attn_x"], dtype=np.float32)
    nc = build_fused()
    ident_bf, ident_f, mask4, ones_col = host_consts()
    hm1a = _l1attn_host(g("L1_attn_A_log"), g("L1_attn_D"), g("L1_attn_W_xproj"),
                        g("L1_attn_W_dt"), g("L1_attn_b_dt"))
    shared = dict(
        ident_bf=ident_bf, ident_f=ident_f, mask4=mask4, ones_col=ones_col,
        wqT=np.ascontiguousarray(g("L0_attn_Wq").T), wkT=np.ascontiguousarray(g("L0_attn_Wk").T),
        wvT=np.ascontiguousarray(g("L0_attn_Wv").T), woT=np.ascontiguousarray(g("L0_attn_Wo").T),
        wgT=np.ascontiguousarray(g("L0_ffn_w_gate").T), wuT=np.ascontiguousarray(g("L0_ffn_w_up").T),
        wdT=np.ascontiguousarray(g("L0_ffn_w_down").T),
        w1=g("L1_ffn_W1"), w2=g("L1_ffn_W2"), b1=g("L1_ffn_b1"), b2=g("L1_ffn_b2"),
        gw=g("L1_ffn_gate_w"), gb=g("L1_ffn_gate_b").reshape(1, 8),
        temp=g("L1_ffn_temperature").reshape(1, 1),
        ones_row=np.ones((1, 128), dtype=BF), sel8=_sel8_host(),
        **hm1a)
    in_maps = [dict(x=np.ascontiguousarray(x[i]), **shared) for i in range(8)]
    res = run_bass_kernel_spmd(nc, in_maps, core_ids=list(range(8)))
    return np.stack([r["xout"] for r in res.results])


# version 05
# speedup vs baseline: 2.6071x; 1.0095x over previous
import numpy as np
import ml_dtypes
from contextlib import ExitStack
import concourse.bass as bass
import concourse.mybir as mybir
from concourse.bass_utils import run_bass_kernel_spmd

F32 = mybir.dt.float32
BF16 = mybir.dt.bfloat16
I32 = mybir.dt.int32
AF = mybir.ActivationFunctionType
ALU = mybir.AluOpType
AX = mybir.AxisListType
BF = ml_dtypes.bfloat16

T, D = 2048, 1024
NTT = T // 128
NKC = D // 128
NTQ = T // 512
NH, DH = 16, 64
FFH = 3072
NE, EH = 8, 4096
DST, DTR = 16, 64
LN_EPS = 1e-5
QSCALE = 1.0 / 8.0
DEQ = 1.0 / 127.0

ENGINES = ("tensor", "vector", "scalar", "gpsimd", "sync")
ENG_EV = {"tensor": "Et", "vector": "Ev", "scalar": "Es", "gpsimd": "Eg", "sync": "Ey"}
DMA_SLOTS = {"gpsimd": 8, "sync": 6, "scalar": 2}
DMA_PREFIX = {"gpsimd": "Dg", "sync": "Dy", "scalar": "Da"}


class Plan:

    def __init__(self, nc, stack):
        self.nc = nc
        self.q = {e: [] for e in ENGINES}
        self.sems = {}
        self.cnt = {}
        self.rr = {e: 0 for e in DMA_SLOTS}
        for s in ENG_EV.values():
            self.sems[s] = stack.enter_context(nc.semaphore(s))
            self.cnt[s] = 0
        for e, n in DMA_SLOTS.items():
            for i in range(n):
                s = f"{DMA_PREFIX[e]}{i}"
                self.sems[s] = stack.enter_context(nc.semaphore(s))
                self.cnt[s] = 0

    def op(self, eng, fn, waits=(), dma=False):
        waits = [w for w in waits if w is not None]
        if dma:
            slot = self.rr[eng]
            self.rr[eng] = (slot + 1) % DMA_SLOTS[eng]
            sem = f"{DMA_PREFIX[eng]}{slot}"
            amt = 16
            if self.cnt[sem] > 0:
                waits = waits + [(sem, self.cnt[sem])]
        else:
            sem = ENG_EV[eng]
            amt = 1
            if self.cnt[sem] > 0 and eng != "tensor":
                waits = waits + [(sem, self.cnt[sem])]
        self.cnt[sem] += amt
        ev = (sem, self.cnt[sem])
        wmax = {}
        for s, v in waits:
            if v > wmax.get(s, -1):
                wmax[s] = v
        sems = self.sems

        def g(e, fn=fn, wmax=wmax, sem=sem, amt=amt):
            for s, v in wmax.items():
                e.wait_ge(sems[s], v)
            fn(e).then_inc(sems[sem], amt)
        self.q[eng].append(g)
        return ev

    def wait(self, eng, waits):
        wmax = {}
        for wv in waits:
            if wv is None:
                continue
            s, v = wv
            if v > wmax.get(s, -1):
                wmax[s] = v
        sems = self.sems

        def g(e, wmax=wmax):
            for s, v in wmax.items():
                e.wait_ge(sems[s], v)
        self.q[eng].append(g)

    def now(self):
        return [(s, v) for s, v in self.cnt.items() if v > 0]

    def fence(self):
        evs = self.now()
        for e in ENGINES:
            self.wait(e, evs)

    def build(self, block):
        for name in ENGINES:
            dec = getattr(block, name)
            fns = self.q[name]

            def body(e, fns=fns):
                for f in fns:
                    f(e)
            dec(body)


class PBuf:

    def __init__(self, t):
        self.t = t
        self.ev_w = None
        self.ev_r = None


class Ctx:
    pass


def mk_ctx(nc, stack, p):
    c = Ctx()
    c.nc = nc
    c.st = stack
    c.p = p
    c.stacks = [stack]

    def sb(name, shape, dt):
        return c.stacks[-1].enter_context(nc.sbuf_tensor(name, shape, dt))
    c.sb = sb

    def ps(name, shape, dt):
        return PBuf(stack.enter_context(nc.psum_tensor(name, shape, dt)))
    c.ps = ps

    c.dummy = c.sb("dummy_cf", [1, 16], F32)
    p.op("vector", lambda e: e.memset(c.dummy[:, :], 0.0))

    def confirm(eng="vector"):
        if eng == "scalar":
            return p.op(eng, lambda e: e.copy(c.dummy[0:1, 8:16], c.dummy[0:1, 0:8]))
        return p.op(eng, lambda e: e.tensor_copy(c.dummy[0:1, 8:16], c.dummy[0:1, 0:8]))
    c.confirm = confirm
    return c



def alloc_common(c, with_fmA=True):
    c.xr = c.sb("xr", [128, NTT * D], F32)
    if with_fmA:
        c.fmA = c.sb("fmA", [128, NKC * T], BF16)
    c.ident_bf = c.sb("identb_sb", [128, 128], BF16)
    c.ident_f = c.sb("identf_sb", [128, 128], F32)
    c.stat1 = c.sb("stat1", [128, NTT], F32)
    c.stat2 = c.sb("stat2", [128, NTT], F32)
    c.mcols = c.sb("mcols", [128, NTT], F32)
    c.rcols = c.sb("rcols", [128, NTT], F32)
    c.tmp16 = c.sb("tmp16", [128, NTT], F32)
    c.halfcol = c.sb("halfcol", [128, 1], F32)
    c.ln_scr = c.sb("ln_scr", [128, D], BF16)
    c.stageF = c.sb("stageF", [128, 512], F32)
    c.pA = c.ps("pA", [128, 512], F32)
    c.pB = c.ps("pB", [128, 512], F32)
    c.pC = c.ps("pC", [128, 512], F32)
    c.pD = c.ps("pD", [128, 512], F32)
    c.pT1 = c.ps("pT1", [128, 1024], BF16)
    c.pT2 = c.ps("pT2", [128, 1024], BF16)
    c.pF1 = c.ps("pF1", [128, 512], F32)
    c.pF2 = c.ps("pF2", [128, 512], F32)


def emit_load_consts(c, ident_bf_d, ident_f_d):
    p = c.p
    c.ev_ident = p.op("sync", lambda e, o=c.ident_bf, i=ident_bf_d: e.dma_start(out=o[:, :], in_=i[:, :]), dma=True)
    c.ev_identf = p.op("sync", lambda e, o=c.ident_f, i=ident_f_d: e.dma_start(out=o[:, :], in_=i[:, :]), dma=True)


def emit_load_x(c, x_d):
    p = c.p
    src = x_d.rearrange("(tt p) d -> p tt d", p=128)
    dst = c.xr[:, :].rearrange("p (tt d) -> p tt d", d=D)
    c.ev_x = p.op("sync", lambda e, o=dst, i=src: e.dma_start(out=o, in_=i), dma=True)


def emit_store_x(c, out_d, waits=None):
    p = c.p
    if waits is None:
        waits = p.now()
    dst = out_d.rearrange("(tt p) d -> p tt d", p=128)
    src = c.xr[:, :].rearrange("p (tt d) -> p tt d", d=D)
    ev = p.op("sync", lambda e, o=dst, i=src: e.dma_start(out=o, in_=i), waits=waits, dma=True)
    return ev


def emit_ln_stats(c, deps):
    p = c.p
    evs = []
    prev_hc = None
    for tt in range(NTT):
        xv = c.xr[:, tt * D:(tt + 1) * D]
        e1 = p.op("vector", lambda e, o=c.stat1[:, tt:tt + 1], i=xv: e.reduce_sum(o, i, axis=AX.X), waits=deps)
        e2a = p.op("scalar", lambda e, o=c.stageF, i=c.xr[:, tt * D: tt * D + 512], a=c.stat2[:, tt:tt + 1]:
                   e.activation(o[:, :], i, AF.Square, accum_out=a), waits=deps)
        e2b = p.op("scalar", lambda e, o=c.stageF, i=c.xr[:, tt * D + 512: (tt + 1) * D], a=c.halfcol:
                   e.activation(o[:, :], i, AF.Square, accum_out=a[:, :]), waits=deps + [prev_hc])
        e2c = p.op("vector", lambda e, o=c.stat2[:, tt:tt + 1]:
                   e.tensor_add(o, o, c.halfcol[:, :]), waits=[e2a, e2b])
        prev_hc = e2c
        evs += [e1, e2c]
    em = p.op("vector", lambda e: e.tensor_scalar_mul(c.mcols[:, :], c.stat1[:, :], 1.0 / D), waits=evs)
    ev = p.op("vector", lambda e: e.tensor_scalar_mul(c.rcols[:, :], c.stat2[:, :], 1.0 / D), waits=evs)
    e3 = p.op("vector", lambda e: e.tensor_mul(c.tmp16[:, :], c.mcols[:, :], c.mcols[:, :]), waits=[em])
    e4 = p.op("vector", lambda e: e.tensor_sub(c.rcols[:, :], c.rcols[:, :], c.tmp16[:, :]), waits=[ev, e3])
    e5 = p.op("vector", lambda e: e.tensor_scalar_add(c.rcols[:, :], c.rcols[:, :], LN_EPS), waits=[e4])
    e6 = p.op("vector", lambda e: e.reciprocal(c.tmp16[:, :], c.rcols[:, :]), waits=[e5])
    e7 = p.op("scalar", lambda e: e.activation(c.rcols[:, :], c.tmp16[:, :], AF.Sqrt), waits=[e6])
    return [em, e7]


def emit_ln_to_fm(c, deps, dst=None, dst_stride=T):
    p = c.p
    if dst is None:
        dst = c.fmA
    sevs = emit_ln_stats(c, deps)
    out_evs = []
    pts = [c.pT1, c.pT2]
    prev_tr = None
    for tt in range(NTT):
        xv = c.xr[:, tt * D:(tt + 1) * D]
        ea = p.op("vector", lambda e, o=c.ln_scr, i=xv, m=c.mcols[:, tt:tt + 1], r=c.rcols[:, tt:tt + 1]:
                  e.tensor_scalar(o[:, :], i, m, r, ALU.subtract, ALU.mult),
                  waits=sevs + deps + [prev_tr])
        pt = pts[tt % 2]
        tr = None
        for k in range(NKC):
            tr = p.op("tensor", lambda e, o=pt.t[:, k * 128:(k + 1) * 128], i=c.ln_scr[:, k * 128:(k + 1) * 128],
                      idn=c.ident_bf: e.transpose(o, i, idn[:, :]),
                      waits=[ea, c.ev_ident, pt.ev_r] if k == 0 else [ea])
        pt.ev_w = tr
        prev_tr = tr
        dv = dst[:, :].rearrange("p (k r) -> p k r", r=dst_stride)[:, :, tt * 128:(tt + 1) * 128]
        sv = pt.t[:, :].rearrange("p (k r) -> p k r", r=128)
        ee = p.op("vector", lambda e, o=dv, i=sv: e.tensor_copy(o, i), waits=[tr])
        pt.ev_r = c.confirm()
        out_evs.append(ee)
    return out_evs


def emit_fm_psum_to_resid(c, src_sb_f32, ot, tq, deps, scale=None):
    p = c.p
    evs = []
    pfs = [c.pF1, c.pF2]
    for j in range(4):
        pf = pfs[j % 2]
        tt = tq * 4 + j
        tr = p.op("tensor", lambda e, o=pf.t[:, 0:128], i=src_sb_f32[:, j * 128:(j + 1) * 128], idn=c.ident_f:
                  e.transpose(o, i, idn[:, :]), waits=deps + [c.ev_identf, pf.ev_r])
        pf.ev_w = tr
        xv = c.xr[:, tt * D + ot * 128: tt * D + (ot + 1) * 128]
        ee = p.op("vector", lambda e, o=xv, i=pf.t[:, 0:128], x=xv:
                  e.tensor_add(o, i, x), waits=[tr])
        pf.ev_r = c.confirm()
        evs.append(ee)
    return evs



def emit_l0attn(c, wq_d, wk_d, wv_d, wo_d, mask_d, ones_d, dbg=False):
    nc = c.nc
    p = c.p
    mask = c.sb("mask_sb", [128, 4 * 512], BF16)
    ones_col = c.sb("ones_sb", [128, 1], BF16)
    ev_mask = p.op("sync", lambda e: e.dma_start(out=mask[:, :], in_=mask_d[:, :]), dma=True)
    ev_ones = p.op("sync", lambda e: e.dma_start(out=ones_col[:, :], in_=ones_d[:, :]), dma=True)
    ev_vc1 = p.op("vector", lambda e: e.memset(vc[:, :], 1.0))

    wbuf = [c.sb("wbuf0", [128, NKC * 128], BF16),
            c.sb("wbuf1", [128, NKC * 128], BF16)]
    wobuf = c.sb("wobuf", [128, NKC * D], BF16)
    qc = c.sb("qc", [128, T], BF16)
    kc = c.sb("kc", [128, T], BF16)
    vc = c.sb("vc", [128, NTT * 130], BF16)
    ptb = c.sb("ptb", [128, 16 * 512], BF16)
    att = c.sb("att", [128, NTT * 128], BF16)
    inv = c.sb("inv", [128, 1], F32)

    ln_evs = emit_ln_to_fm(c, [c.ev_x])
    if dbg:
        dfm = nc.dram_tensor("dbg_fm", [128, NKC * T], BF16, kind="ExternalOutput").ap()
        p.op("sync", lambda e: e.dma_start(out=dfm[:, :], in_=c.fmA[:, :]), waits=ln_evs, dma=True)

    qkv_seq = [(kc_i, pi) for kc_i in range(NKC) for pi in range(3)]
    qkv_evs = {}
    qkv_guardb = [None, None]
    qkv_next = [0]

    def issue_qkv():
        i = qkv_next[0]
        if i >= len(qkv_seq):
            return
        qkv_next[0] = i + 1
        kc_i, pi = qkv_seq[i]
        par = i % 2
        w_d = (wq_d, wk_d, wv_d)[pi]
        src = w_d[:, kc_i * 128:(kc_i + 1) * 128].rearrange("(k p) o -> p k o", p=128)
        qkv_evs[i] = p.op("gpsimd", lambda e, o=wbuf[par][:, :].rearrange("p (k o) -> p k o", o=128), i_=src:
                          e.dma_start(out=o, in_=i_), waits=[qkv_guardb[par]], dma=True)

    issue_qkv()
    issue_qkv()
    qkv_i = [0]
    lwo = p.op("gpsimd", lambda e, o=wobuf[:, :].rearrange("p (k o) -> p k o", o=D),
               i_=wo_d.rearrange("(k p) o -> p k o", p=128): e.dma_start(out=o, in_=i_), dma=True)

    wo_stage_deps = []
    attn_fm = c.sb("attn_fm", [128, NKC * T], BF16)

    for kc_i in range(NKC):
        prev_evq = []
        for (w_d, dstbuf, isv) in ((wq_d, qc, False), (wk_d, kc, False), (wv_d, vc, True)):
            wi = qkv_i[0]
            qkv_i[0] = wi + 1
            wpar = wi % 2
            lw = qkv_evs[wi]
            if not isv:
                for tq in range(NTQ):
                    pb = [c.pA, c.pB][tq % 2]
                    mws = ln_evs + [lw, pb.ev_r]
                    mm = None
                    for kk in range(NKC):
                        lh = wbuf[wpar][:, kk * 128:(kk + 1) * 128]
                        rh = c.fmA[:, kk * T + tq * 512: kk * T + (tq + 1) * 512]
                        mm = p.op("tensor", lambda e, o=pb.t[:, 0:512], l=lh, r=rh,
                                  s=(kk == 0), z=(kk == NKC - 1):
                                  e.matmul(o, l, r, start=s, stop=z),
                                  waits=mws if kk == 0 else [])
                    pb.ev_w = mm
                    qkv_guardb[wpar] = mm
                    ee = p.op("vector", lambda e, o=dstbuf[:, tq * 512:(tq + 1) * 512], i=pb.t[:, 0:512]:
                              e.tensor_copy(o, i), waits=[pb.ev_w])
                    pb.ev_r = ee
                    prev_evq.append(ee)
            else:
                for tt in range(NTT):
                    pb = [c.pA, c.pB][tt % 2]
                    mws = ln_evs + [lw, pb.ev_r]
                    mm = None
                    for kk in range(NKC):
                        lh = c.fmA[:, kk * T + tt * 128: kk * T + (tt + 1) * 128]
                        rh = wbuf[wpar][:, kk * 128:(kk + 1) * 128]
                        mm = p.op("tensor", lambda e, o=pb.t[:, 0:128], l=lh, r=rh,
                                  s=(kk == 0), z=(kk == NKC - 1):
                                  e.matmul(o, l, r, start=s, stop=z),
                                  waits=mws if kk == 0 else [])
                    pb.ev_w = mm
                    qkv_guardb[wpar] = mm
                    dv = vc[:, :].rearrange("p (tt j x) -> p tt j x", j=2, x=65)[:, tt, :, 0:64]
                    sv = pb.t[:, 0:128].rearrange("p (j x) -> p j x", j=2)
                    ee = p.op("vector", lambda e, o=dv, i=sv:
                              e.tensor_copy(o, i), waits=[pb.ev_w, ev_vc1])
                    pb.ev_r = ee
                    prev_evq.append(ee)
            issue_qkv()
        qkv_done = list(prev_evq)
        if dbg and kc_i == 0:
            dq = nc.dram_tensor("dbg_q", [128, T], BF16, kind="ExternalOutput").ap()
            dk = nc.dram_tensor("dbg_k", [128, T], BF16, kind="ExternalOutput").ap()
            dv = nc.dram_tensor("dbg_v", [128, NTT * 128], BF16, kind="ExternalOutput").ap()
            p.op("sync", lambda e: e.dma_start(out=dq[:, :], in_=qc[:, :]), waits=qkv_done, dma=True)
            p.op("sync", lambda e: e.dma_start(out=dk[:, :], in_=kc[:, :]), waits=qkv_done, dma=True)
            p.op("sync", lambda e: e.dma_start(out=dv[:, :], in_=vc[:, :]), waits=qkv_done, dma=True)

        att_evs = []
        for hh in range(2):
            hbase = hh * 64
            for sq in range(NTQ):
                nkt = 4 * sq + 4
                exp_evs = []
                for kt in range(nkt):
                    pb = [c.pA, c.pB, c.pC, c.pD][kt % 4]
                    mm = p.op("tensor", lambda e, o=pb.t[:, 0:512],
                              l=kc[hbase:hbase + 64, kt * 128:(kt + 1) * 128],
                              r=qc[hbase:hbase + 64, sq * 512:(sq + 1) * 512]:
                              e.matmul(o, l, r, start=True, stop=True),
                              waits=qkv_done + [pb.ev_r])
                    pb.ev_w = mm
                    ex = p.op("scalar", lambda e, o=ptb[:, kt * 512:(kt + 1) * 512], i=pb.t[:, 0:512]:
                              e.activation(o, i, AF.Exp, scale=QSCALE), waits=[mm])
                    pb.ev_r = ex
                    exp_evs.append(ex)
                    if kt >= 4 * sq:
                        rel = kt - 4 * sq
                        me = p.op("vector", lambda e, o=ptb[:, kt * 512:(kt + 1) * 512],
                                  m=mask[:, rel * 512:(rel + 1) * 512]:
                                  e.tensor_mul(o, o, m), waits=[ex, ev_mask])
                        exp_evs.append(me)
                for si in range(4):
                    stile = sq * 4 + si
                    mm1 = None
                    for kt in range(nkt):
                        lh = ptb[:, kt * 512 + si * 128: kt * 512 + (si + 1) * 128]
                        mm1 = p.op("tensor", lambda e, o=c.pC.t[:, 0:65], l=lh,
                                   r=vc[:, kt * 130 + hh * 65: kt * 130 + hh * 65 + 65],
                                   s=(kt == 0), z=(kt == nkt - 1):
                                   e.matmul(o, l, r, start=s, stop=z),
                                   waits=exp_evs + [c.pC.ev_r] if kt == 0 else [])
                    c.pC.ev_w = mm1
                    rc = p.op("vector", lambda e: e.reciprocal(inv[:, :], c.pC.t[:, 64:65]), waits=[mm1])
                    ee = p.op("vector", lambda e, o=att[:, stile * 128 + hbase: stile * 128 + hbase + 64],
                              i=c.pC.t[:, 0:64]:
                              e.tensor_scalar(o, i, inv[:, :], None, ALU.mult), waits=[rc])
                    c.pC.ev_r = ee
                    att_evs.append(ee)
        if dbg and kc_i == 0:
            da = nc.dram_tensor("dbg_att", [128, NTT * 128], BF16, kind="ExternalOutput").ap()
            p.op("sync", lambda e: e.dma_start(out=da[:, :], in_=att[:, :]), waits=att_evs, dma=True)
        if dbg and kc_i == 7:
            da7 = nc.dram_tensor("dbg_att7", [128, NTT * 128], BF16, kind="ExternalOutput").ap()
            p.op("sync", lambda e: e.dma_start(out=da7[:, :], in_=att[:, :]), waits=att_evs, dma=True)
        pts = [c.pT1, c.pT2]
        for tt in range(NTT):
            pt = pts[tt % 2]
            tr = p.op("tensor", lambda e, o=pt.t[:, 0:128], i=att[:, tt * 128:(tt + 1) * 128], idn=c.ident_bf:
                      e.transpose(o, i, idn[:, :]), waits=att_evs + [c.ev_ident, pt.ev_r])
            pt.ev_w = tr
            ee = p.op("vector", lambda e, o=attn_fm[:, kc_i * T + tt * 128: kc_i * T + (tt + 1) * 128],
                      i=pt.t[:, 0:128]: e.tensor_copy(o, i), waits=[tr])
            pt.ev_r = ee
            wo_stage_deps.append(ee)

    if dbg:
        dafm = nc.dram_tensor("dbg_attnfm", [128, NKC * T], BF16, kind="ExternalOutput").ap()
        p.op("sync", lambda e: e.dma_start(out=dafm[:, :], in_=attn_fm[:, :]), waits=wo_stage_deps, dma=True)
    for tt in range(NTT):
        for oh in range(2):
            pb = [c.pA, c.pB][(tt * 2 + oh) % 2]
            mm = None
            for kk in range(NKC):
                lh = attn_fm[:, kk * T + tt * 128: kk * T + (tt + 1) * 128]
                rh = wobuf[:, kk * D + oh * 512: kk * D + (oh + 1) * 512]
                mm = p.op("tensor", lambda e, o=pb.t[:, 0:512], l=lh, r=rh,
                          s=(kk == 0), z=(kk == NKC - 1):
                          e.matmul(o, l, r, start=s, stop=z),
                          waits=wo_stage_deps + [lwo, pb.ev_r] if kk == 0 else [])
            pb.ev_w = mm
            xv = c.xr[:, tt * D + oh * 512: tt * D + (oh + 1) * 512]
            ec = p.op("vector", lambda e, o=xv, i=pb.t[:, 0:512]:
                      e.tensor_add(o, i, o), waits=[pb.ev_w])
            pb.ev_r = c.confirm()



def emit_l0ffn(c, wg_d, wu_d, wd_d, dbg=False):
    nc = c.nc
    p = c.p
    NHC = FFH // 128
    NHG = FFH // 512
    h_sb = c.sb("h_sb", [128, NHC * 512], BF16)
    g_grp = c.sb("g_grp", [128, 4 * 512], BF16)
    wd_sb = c.sb("wd_sb", [128, NHC * D], BF16)
    wgu_sb = [c.sb("wgu_sb0", [128, NKC * 512], BF16),
              c.sb("wgu_sb1", [128, NKC * 512], BF16)]

    ln_evs = emit_ln_to_fm(c, [c.ev_x])

    wdv = wd_d.rearrange("(k p) o -> p k o", p=128)
    lwd = p.op("gpsimd", lambda e, o=wd_sb[:, :].rearrange("p (k o) -> p k o", o=D), i=wdv:
               e.dma_start(out=o, in_=i), dma=True)

    g_guard = [None] * 4
    sil_ev = [None] * 4
    gu_seq = [(tq, hg, gi) for tq in range(NTQ) for hg in range(NHG) for gi in range(2)]
    gu_evs = {}
    gu_guardb = [None, None]
    gu_next = [0]

    def issue_gu():
        i = gu_next[0]
        if i >= len(gu_seq):
            return
        gu_next[0] = i + 1
        _tq, hg, gi = gu_seq[i]
        par = i % 2
        w_d = (wg_d, wu_d)[gi]
        src = w_d[:, hg * 512:(hg + 1) * 512].rearrange("(k p) h -> p k h", p=128)
        gu_evs[i] = p.op("gpsimd", lambda e, o=wgu_sb[par][:, :].rearrange("p (k h) -> p k h", h=512), i_=src:
                         e.dma_start(out=o, in_=i_), waits=[gu_guardb[par]], dma=True)

    issue_gu()
    issue_gu()
    gu_i = [0]
    for tq in range(NTQ):
        hevs = []
        for hg in range(NHG):
            for (w_d, is_gate) in ((wg_d, True), (wu_d, False)):
                gi_idx = gu_i[0]
                gu_i[0] = gi_idx + 1
                par = gi_idx % 2
                lw = gu_evs[gi_idx]
                for hti in range(4):
                    ht = hg * 4 + hti
                    pb = c.pA if is_gate else c.pB
                    mm = None
                    for kk in range(NKC):
                        lh = wgu_sb[par][:, kk * 512 + hti * 128: kk * 512 + (hti + 1) * 128]
                        rh = c.fmA[:, kk * T + tq * 512: kk * T + (tq + 1) * 512]
                        mm = p.op("tensor", lambda e, o=pb.t[:, 0:512], l=lh, r=rh,
                                  s=(kk == 0), z=(kk == NKC - 1):
                                  e.matmul(o, l, r, start=s, stop=z),
                                  waits=ln_evs + [lw, pb.ev_r] if kk == 0 else [])
                    pb.ev_w = mm
                    gu_guardb[par] = mm
                    if is_gate:
                        ee = p.op("scalar", lambda e, o=g_grp[:, hti * 512:(hti + 1) * 512], i=pb.t[:, 0:512]:
                                  e.activation(o, i, AF.Silu, scale=DEQ),
                                  waits=[pb.ev_w, g_guard[hti]])
                        sil_ev[hti] = ee
                    else:
                        ee = p.op("vector", lambda e, o=h_sb[:, ht * 512:(ht + 1) * 512],
                                  i=g_grp[:, hti * 512:(hti + 1) * 512], u=pb.t[:, 0:512]:
                                  e.tensor_mul(o, i, u), waits=[pb.ev_w, sil_ev[hti]])
                        g_guard[hti] = ee
                    pb.ev_r = ee
                    hevs.append(ee)
                issue_gu()
        for ttl in range(4):
            tt = tq * 4 + ttl
            for oh in range(2):
                pb = [c.pC, c.pD][(ttl * 2 + oh) % 2]
                mm = None
                for hk in range(NHC):
                    lh = h_sb[:, hk * 512 + ttl * 128: hk * 512 + (ttl + 1) * 128]
                    rh = wd_sb[:, hk * D + oh * 512: hk * D + (oh + 1) * 512]
                    mm = p.op("tensor", lambda e, o=pb.t[:, 0:512], l=lh, r=rh,
                              s=(hk == 0), z=(hk == NHC - 1):
                              e.matmul(o, l, r, start=s, stop=z),
                              waits=hevs + [lwd, pb.ev_r] if hk == 0 else [])
                pb.ev_w = mm
                xv = c.xr[:, tt * D + oh * 512: tt * D + (oh + 1) * 512]
                ec = p.op("vector", lambda e, o=xv, i=pb.t[:, 0:512]:
                          e.scalar_tensor_tensor(o, i, DEQ * DEQ, o, ALU.mult, ALU.add),
                          waits=[pb.ev_w])
                pb.ev_r = c.confirm()



def emit_l1attn(c, wxp1_d, wxp2_d, wdt_d, are_d, bdt_d, dd_d, sel_d, dbg=False):
    nc = c.nc
    p = c.p
    wxp1 = c.sb("wxp1", [128, NKC * 80], BF16)
    wxp2 = c.sb("wxp2", [128, NKC * 16], BF16)
    wdt = c.sb("wdt", [64, D], BF16)
    a_re = c.sb("a_re", [128, 128], F32)
    a_sc = c.sb("a_sc", [128, 128], F32)
    bdt = c.sb("bdt", [128, 8], F32)
    dcol = c.sb("dcol", [128, 8], F32)
    sel16 = c.sb("sel16_sb", [16, 16 * 128], BF16)
    dtl = c.sb("dtl", [64, T], BF16)
    b_bf = c.sb("b_bf", [16, T], BF16)
    c_bf = c.sb("c_bf", [16, T], BF16)
    dt_sb = c.sb("dt_sb", [128, NKC * T], BF16)
    u_sb = c.sb("u_sb", [128, NKC * 512], BF16)
    y_acc = c.sb("y_acc", [128, NKC * 512], F32)
    carry = c.sb("carry", [128, NKC * DST], F32)
    brep = c.sb("brep", [128, 512], BF16)
    crep = c.sb("crep", [128, 512], BF16)
    sA = c.sb("sA", [128, 512], F32)
    sB = c.sb("sB", [128, 512], F32)
    a_t = [c.sb("a_t0", [128, 512], BF16), c.sb("a_t1", [128, 512], BF16)]
    b_t = [c.sb("b_t0", [128, 512], BF16), c.sb("b_t1", [128, 512], BF16)]
    h_t = [c.sb("h_t0", [128, 512], BF16), c.sb("h_t1", [128, 512], BF16)]
    y_t = [c.sb("y_t0", [128, 512], BF16), c.sb("y_t1", [128, 512], BF16)]

    lw1 = p.op("gpsimd", lambda e, o=wxp1[:, :].rearrange("p (k o) -> p k o", o=80),
               i=wxp1_d.rearrange("(k p) o -> p k o", p=128): e.dma_start(out=o, in_=i), dma=True)
    lw2 = p.op("gpsimd", lambda e, o=wxp2[:, :].rearrange("p (k o) -> p k o", o=16),
               i=wxp2_d.rearrange("(k p) o -> p k o", p=128): e.dma_start(out=o, in_=i), dma=True)
    lw3 = p.op("gpsimd", lambda e: e.dma_start(out=wdt[:, :], in_=wdt_d[:, :]), dma=True)
    lw4 = p.op("sync", lambda e: e.dma_start(out=a_re[:, :], in_=are_d[:, :]), dma=True)
    lw5 = p.op("sync", lambda e: e.dma_start(out=bdt[:, :], in_=bdt_d[:, :]), dma=True)
    lw6 = p.op("sync", lambda e: e.dma_start(out=dcol[:, :], in_=dd_d[:, :]), dma=True)
    lw7 = p.op("sync", lambda e: e.dma_start(out=sel16[:, :], in_=sel_d[:, :]), dma=True)

    ea1 = p.op("scalar", lambda e: e.activation(a_sc[:, :], a_re[:, :], AF.Exp), waits=[lw4])
    ea2 = p.op("vector", lambda e: e.tensor_scalar_mul(a_sc[:, :], a_sc[:, :], -1.0), waits=[ea1])

    ln_evs = emit_ln_to_fm(c, [c.ev_x])

    xdbl_evs = []
    for tq in range(NTQ):
        p1 = c.pA
        p2 = c.pB
        mm1 = None
        mm2 = None
        for kk in range(NKC):
            lh = wxp1[:, kk * 80:(kk + 1) * 80]
            rh = c.fmA[:, kk * T + tq * 512: kk * T + (tq + 1) * 512]
            mm1 = p.op("tensor", lambda e, o=p1.t[0:80, 0:512], l=lh, r=rh,
                       s=(kk == 0), z=(kk == NKC - 1): e.matmul(o, l, r, start=s, stop=z),
                       waits=ln_evs + [lw1, p1.ev_r] if kk == 0 else [])
        for kk in range(NKC):
            lh = wxp2[:, kk * 16:(kk + 1) * 16]
            rh = c.fmA[:, kk * T + tq * 512: kk * T + (tq + 1) * 512]
            mm2 = p.op("tensor", lambda e, o=p2.t[0:16, 0:512], l=lh, r=rh,
                       s=(kk == 0), z=(kk == NKC - 1): e.matmul(o, l, r, start=s, stop=z),
                       waits=[lw2, p2.ev_r] if kk == 0 else [])
        p1.ev_w = mm1
        p2.ev_w = mm2
        e1 = p.op("vector", lambda e, o=dtl[:, tq * 512:(tq + 1) * 512], i=p1.t[0:64, 0:512]:
                  e.tensor_copy(o, i), waits=[mm1])
        e2 = p.op("vector", lambda e, o=b_bf[:, tq * 512:(tq + 1) * 512], i=p1.t[64:80, 0:512]:
                  e.tensor_copy(o, i), waits=[mm1])
        e3 = p.op("vector", lambda e, o=c_bf[:, tq * 512:(tq + 1) * 512], i=p2.t[0:16, 0:512]:
                  e.tensor_copy(o, i), waits=[mm2])
        p1.ev_r = e2
        p2.ev_r = e3
        xdbl_evs += [e1, e2, e3]

    dt_evs = []
    prev_ad = None
    for dc in range(NKC):
        for tq in range(NTQ):
            pb = [c.pA, c.pB][(dc * NTQ + tq) % 2]
            bcol = bdt[:, dc:dc + 1]
            mm = p.op("tensor", lambda e, o=pb.t[:, 0:512], l=wdt[:, dc * 128:(dc + 1) * 128],
                      r=dtl[0:64, tq * 512:(tq + 1) * 512]:
                      e.matmul(o, l, r, start=True, stop=True),
                      waits=xdbl_evs + [lw3, pb.ev_r])
            pb.ev_w = mm
            a1 = p.op("scalar", lambda e, o=sA, i=pb.t[:, 0:512], b=bcol:
                      e.activation(o[:, :], i, AF.Abs, bias=b), waits=[mm, lw5, prev_ad])
            a2 = p.op("scalar", lambda e, o=sB, i=sA: e.activation(o[:, :], i[:, :], AF.Exp, scale=-1.0),
                      waits=[a1])
            a3 = p.op("scalar", lambda e, o=sA, i=sB: e.activation(o[:, :], i[:, :], AF.Ln, bias=1.0),
                      waits=[a2])
            a4 = p.op("scalar", lambda e, o=sB, i=pb.t[:, 0:512], b=bcol:
                      e.activation(o[:, :], i, AF.Relu, bias=b), waits=[a3])
            pb.ev_r = a4
            ad = p.op("vector", lambda e, o=dt_sb[:, dc * T + tq * 512: dc * T + (tq + 1) * 512],
                      i=sA, j=sB: e.tensor_add(o, i[:, :], j[:, :]), waits=[a3, a4])
            dt_evs.append(ad)
            prev_ad = ad

    if dbg:
        ddtl = nc.dram_tensor("dbg_dtl", [64, T], BF16, kind="ExternalOutput").ap()
        dbb = nc.dram_tensor("dbg_b", [16, T], BF16, kind="ExternalOutput").ap()
        dcb = nc.dram_tensor("dbg_c", [16, T], BF16, kind="ExternalOutput").ap()
        ddt = nc.dram_tensor("dbg_dt", [128, NKC * T], BF16, kind="ExternalOutput").ap()
        p.op("sync", lambda e: e.dma_start(out=ddtl[:, :], in_=dtl[:, :]), waits=xdbl_evs, dma=True)
        p.op("sync", lambda e: e.dma_start(out=dbb[:, :], in_=b_bf[:, :]), waits=xdbl_evs, dma=True)
        p.op("sync", lambda e: e.dma_start(out=dcb[:, :], in_=c_bf[:, :]), waits=xdbl_evs, dma=True)
        p.op("sync", lambda e: e.dma_start(out=ddt[:, :], in_=dt_sb[:, :]), waits=dt_evs, dma=True)

    c.stage_guard = None
    y_guard = [None] * NKC
    fin_evs = []
    for tq in range(NTQ):
        u_evs = []
        for dc in range(NKC):
            ue = p.op("vector", lambda e, o=u_sb[:, dc * 512:(dc + 1) * 512],
                      i=dt_sb[:, dc * T + tq * 512: dc * T + (tq + 1) * 512],
                      j=c.fmA[:, dc * T + tq * 512: dc * T + (tq + 1) * 512]:
                      e.tensor_mul(o, i, j), waits=dt_evs + ln_evs + [y_guard[dc]])
            u_evs.append(ue)
        scan_prev = [None, None]
        pool_prev = [None, None]
        for n in range(DST):
            mmB = p.op("tensor", lambda e, o=c.pC.t[:, 0:512], l=sel16[:, n * 128:(n + 1) * 128],
                       r=b_bf[:, tq * 512:(tq + 1) * 512]:
                       e.matmul(o, l, r, start=True, stop=True),
                       waits=xdbl_evs + [lw7, c.pC.ev_r])
            c.pC.ev_w = mmB
            eb = p.op("vector", lambda e, o=brep, i=c.pC.t[:, 0:512]: e.tensor_copy(o[:, :], i),
                      waits=[mmB, scan_prev[0], scan_prev[1]])
            c.pC.ev_r = eb
            mmC = p.op("tensor", lambda e, o=c.pD.t[:, 0:512], l=sel16[:, n * 128:(n + 1) * 128],
                       r=c_bf[:, tq * 512:(tq + 1) * 512]:
                       e.matmul(o, l, r, start=True, stop=True),
                       waits=[c.pD.ev_r])
            c.pD.ev_w = mmC
            ec = p.op("vector", lambda e, o=crep, i=c.pD.t[:, 0:512]: e.tensor_copy(o[:, :], i),
                      waits=[mmC, pool_prev[0], pool_prev[1]])
            c.pD.ev_r = ec
            for dc in range(NKC):
                par = dc % 2
                av = p.op("scalar", lambda e, o=a_t[par], i=dt_sb[:, dc * T + tq * 512: dc * T + (tq + 1) * 512],
                          s=a_sc[:, dc * 16 + n: dc * 16 + n + 1]:
                          e.activation(o[:, :], i, AF.Exp, scale=s),
                          waits=[ea2, scan_prev[par]] + dt_evs)
                bv = p.op("vector", lambda e, o=b_t[par], i=u_sb[:, dc * 512:(dc + 1) * 512], j=brep:
                          e.tensor_mul(o[:, :], i, j[:, :]), waits=u_evs + [eb, scan_prev[par]])
                init = 0.0 if tq == 0 else carry[:, dc * DST + n: dc * DST + n + 1]
                hv = p.op("vector", lambda e, o=h_t[par], i=a_t[par], j=b_t[par], z=init:
                          e.tensor_tensor_scan(o[:, :], i[:, :], j[:, :], z, ALU.mult, ALU.add),
                          waits=[av, bv, pool_prev[par]])
                cv = p.op("vector", lambda e, o=carry[:, dc * DST + n: dc * DST + n + 1], i=h_t[par]:
                          e.tensor_copy(o, i[:, 511:512]), waits=[hv])
                yeng = getattr(build_l1attn, "Y_ENGINE", "gpsimd")
                yv = p.op(yeng, lambda e, o=y_t[par], i=h_t[par], j=crep:
                          e.tensor_mul(o[:, :], i[:, :], j[:, :]), waits=[hv, ec])
                if n == 0:
                    ya = p.op(yeng, lambda e, o=y_acc[:, dc * 512:(dc + 1) * 512], i=y_t[par]:
                              e.tensor_copy(o, i[:, :]), waits=[yv, y_guard[dc]])
                else:
                    ya = p.op(yeng, lambda e, o=y_acc[:, dc * 512:(dc + 1) * 512], i=y_t[par]:
                              e.tensor_add(o, o, i[:, :]), waits=[yv])
                scan_prev[par] = cv
                pool_prev[par] = ya
        if dbg and tq == 0:
            dya = nc.dram_tensor("dbg_yacc", [128, NKC * 512], F32, kind="ExternalOutput").ap()
            dcr = nc.dram_tensor("dbg_carry", [128, NKC * DST], F32, kind="ExternalOutput").ap()
            p.op("sync", lambda e: e.dma_start(out=dya[:, :], in_=y_acc[:, :]),
                 waits=[pool_prev[0], pool_prev[1]], dma=True)
            p.op("sync", lambda e: e.dma_start(out=dcr[:, :], in_=carry[:, :]),
                 waits=[scan_prev[0], scan_prev[1]], dma=True)
        fence_mode = getattr(build_l1attn, "FENCE_MODE", "fence")
        if fence_mode == "fence":
            p.fence()
        elif fence_mode == "sblock":
            p.wait("scalar", [("Ev", p.cnt["Ev"]), ("Eg", p.cnt["Eg"])])
        elif fence_mode == "tblock":
            p.wait("tensor", [("Ev", p.cnt["Ev"]), ("Eg", p.cnt["Eg"])])
        elif fence_mode == "gblock":
            p.wait("gpsimd", [("Ev", p.cnt["Ev"]), ("Es", p.cnt["Es"])])
        elif fence_mode == "slack":
            yeng0 = getattr(build_l1attn, "Y_ENGINE", "gpsimd")
            nslack = getattr(build_l1attn, "NSLACK", 4)
            sl = None
            for _ in range(nslack):
                sl = p.op(yeng0, lambda e: e.tensor_copy(sB[:, 0:512], sA[:, 0:512]),
                          waits=[pool_prev[0], pool_prev[1]])
            pool_prev = [sl, sl]
        elif fence_mode == "et":
            p.wait("vector", [("Et", p.cnt["Et"])])
        elif fence_mode == "es":
            p.wait("vector", [("Es", p.cnt["Es"])])
        elif fence_mode == "eg":
            p.wait("vector", [("Eg", p.cnt["Eg"])])
        if dbg and tq == 1:
            park = c.sb("park", [128, 2 * 512], F32)
            pk = p.op("vector", lambda e: e.tensor_copy(park[:, :], y_acc[:, 6 * 512:8 * 512]),
                      waits=[pool_prev[0], pool_prev[1]])
            dpk = nc.dram_tensor("dbg_park", [128, 2 * 512], F32, kind="ExternalOutput").ap()
            p.op("sync", lambda e: e.dma_start(out=dpk[:, :], in_=park[:, :]), waits=[pk], dma=True)
        for dc in range(NKC):
            fe = p.op("vector", lambda e, o=c.stageF,
                      i=c.fmA[:, dc * T + tq * 512: dc * T + (tq + 1) * 512],
                      s=dcol[:, dc:dc + 1], j=y_acc[:, dc * 512:(dc + 1) * 512]:
                      e.scalar_tensor_tensor(o[:, :], i, s, j, ALU.mult, ALU.add),
                      waits=[pool_prev[0], pool_prev[1], lw6, c.stage_guard])
            evs = emit_fm_psum_to_resid(c, c.stageF, dc, tq, [fe])
            c.stage_guard = evs[-1]
            y_guard[dc] = fe
            fin_evs += evs



def emit_l1ffn(c, w1_d, w2_d, b1_d, b2_d, gw_d, gb_d, temp_d, ones_row_d, sel8_d, dbg=False):
    nc = c.nc
    p = c.p
    NHC2 = EH // 128
    h_half = c.sb("h_half", [128, 16 * 1024], BF16)
    out_acc = c.sb("out_acc", [128, 8 * 1024], F32)
    wst = [c.sb("wst0", [128, 4096], BF16), c.sb("wst1", [128, 4096], BF16)]
    gw_sb = c.sb("gw_sb", [128, NKC * 8], BF16)
    gbr = c.sb("gbr", [128, 8], F32)
    onesr = c.sb("onesr", [1, 128], BF16)
    sel8 = c.sb("sel8_sb", [8, 8 * 128], BF16)
    b1sb = c.sb("b1sb", [8, 1024], F32)
    b1T = c.sb("b1T", [128, NHC2 * 8], F32)
    b2sb = c.sb("b2sb", [8, D], BF16)
    itemp = c.sb("itemp", [1, 1], F32)
    itcol = c.sb("itcol", [128, 1], F32)
    gates = c.sb("gates", [128, 8], F32)
    m1c = c.sb("m1c", [128, 1], F32)
    m2c = c.sb("m2c", [128, 1], F32)
    eq1 = c.sb("eq1", [128, 8], F32)
    eq2 = c.sb("eq2", [128, 8], F32)
    s1c = c.sb("s1c", [128, 1], F32)
    s2c = c.sb("s2c", [128, 1], F32)
    tmp8 = c.sb("tmp8", [128, 8], F32)
    w_tok = c.sb("w_tok", [128, NTT * 8], F32)
    w_fm = c.sb("w_fm", [8, T], BF16)
    wrep = c.sb("wrep", [128, 1024], BF16)
    g_t = [c.sb("g_t0", [128, 512], BF16), c.sb("g_t1", [128, 512], BF16)]

    lo = p.op("sync", lambda e: e.dma_start(out=onesr[:, :], in_=ones_row_d[:, :]), dma=True)
    ls8 = p.op("sync", lambda e: e.dma_start(out=sel8[:, :], in_=sel8_d[:, :]), dma=True)
    lgw = p.op("gpsimd", lambda e, o=gw_sb[:, :].rearrange("p (k e) -> p k e", e=8),
               i=gw_d.rearrange("(k p) e -> p k e", p=128): e.dma_start(out=o, in_=i), dma=True)
    gbf32 = c.sb("gbf32", [1, 8], F32)
    lgb = p.op("sync", lambda e: e.dma_start(out=gbf32[:, :], in_=gb_d[:, :]), dma=True)
    ltm = p.op("sync", lambda e: e.dma_start(out=itemp[:, :], in_=temp_d[:, :]), dma=True)
    lb2 = p.op("gpsimd", lambda e: e.dma_start(out=b2sb[:, :], in_=b2_d[:, :]), dma=True)
    gb_bf = c.sb("gb_bf", [1, 8], BF16)
    egb = p.op("vector", lambda e: e.tensor_copy(gb_bf[:, :], gbf32[:, :]), waits=[lgb])

    ln_evs = emit_ln_to_fm(c, [c.ev_x])
    p.fence()

    e1t = p.op("vector", lambda e: e.reciprocal(itemp[:, :], itemp[:, :]), waits=[ltm])
    it_bf = c.sb("it_bf", [1, 1], BF16)
    e2t = p.op("vector", lambda e: e.tensor_copy(it_bf[:, :], itemp[:, :]), waits=[e1t])
    mmt = p.op("tensor", lambda e: e.matmul(c.pF1.t[:, 0:1], onesr[:, :], it_bf[:, :], start=True, stop=True),
               waits=[e2t, lo, c.pF1.ev_r])
    c.pF1.ev_w = mmt
    e3t = p.op("vector", lambda e: e.tensor_copy(itcol[:, :], c.pF1.t[:, 0:1]), waits=[mmt])
    c.pF1.ev_r = c.confirm()
    mmg = p.op("tensor", lambda e: e.matmul(c.pF2.t[:, 0:8], onesr[:, :], gb_bf[:, :], start=True, stop=True),
               waits=[egb, lo, c.pF2.ev_r])
    c.pF2.ev_w = mmg
    e4t = p.op("vector", lambda e, s=itcol: e.tensor_scalar(gbr[:, :], c.pF2.t[:, 0:8], s[:, :], None, ALU.mult),
               waits=[mmg, e3t])
    c.pF2.ev_r = c.confirm()

    b1evs = []
    b1_rd_guard = None
    lb1 = None
    for hc in range(NHC2):
        if hc % 8 == 0:
            q = hc // 8
            lb1 = p.op("sync", lambda e, o=b1sb, i=b1_d[:, q * 1024:(q + 1) * 1024]:
                       e.dma_start(out=o[:, :], in_=i), waits=[b1_rd_guard], dma=True)
        pf = [c.pF1, c.pF2][hc % 2]
        tr = p.op("tensor", lambda e, o=pf.t[:, 0:8], i=b1sb[:, (hc % 8) * 128:(hc % 8 + 1) * 128]:
                  e.transpose(o, i, c.ident_f[0:8, 0:8]),
                  waits=[lb1, c.ev_identf, pf.ev_r])
        b1_rd_guard = tr
        pf.ev_w = tr
        ee = p.op("vector", lambda e, o=b1T[:, hc * 8:(hc + 1) * 8], i=pf.t[:, 0:8]:
                  e.tensor_copy(o, i), waits=[tr])
        pf.ev_r = c.confirm()
        b1evs.append(ee)

    p.fence()
    wtok_evs = []
    prev8 = None
    prev10 = None
    for tt in range(NTT):
        pb = [c.pC, c.pD][tt % 2]
        mm = None
        for kk in range(NKC):
            lh = c.fmA[:, kk * T + tt * 128: kk * T + (tt + 1) * 128]
            rh = gw_sb[:, kk * 8:(kk + 1) * 8]
            mm = p.op("tensor", lambda e, o=pb.t[:, 0:8], l=lh, r=rh, s=(kk == 0), z=(kk == NKC - 1):
                      e.matmul(o, l, r, start=s, stop=z),
                      waits=ln_evs + [lgw, pb.ev_r] if kk == 0 else [])
        pb.ev_w = mm
        ge = p.op("vector", lambda e, o=gates, i=pb.t[:, 0:8]:
                  e.scalar_tensor_tensor(o[:, :], i, itcol[:, :], gbr[:, :], ALU.mult, ALU.add),
                  waits=[mm, e4t, e3t] + ([c.dbg_gev] if dbg and getattr(c, "dbg_gev", None) else []))
        pb.ev_r = c.confirm()
        if dbg:
            if tt == 0:
                c.dbg_g = nc.dram_tensor("dbg_gates", [128, NTT * 8], F32, kind="ExternalOutput").ap()
                c.dbg_praw = nc.dram_tensor("dbg_praw", [128, NTT * 8], F32, kind="ExternalOutput").ap()
            dgr = p.op("vector", lambda e, o=c.tmp16[:, 0:8], i=pb.t[:, 0:8]: e.tensor_copy(o, i),
                       waits=[mm, getattr(c, "dbg_gev2", None)])
            c.dbg_gev = p.op("sync", lambda e, o=c.dbg_g[:, tt * 8:(tt + 1) * 8]: e.dma_start(out=o, in_=gates[:, :]),
                             waits=[ge], dma=True)
            c.dbg_gev2 = p.op("sync", lambda e, o=c.dbg_praw[:, tt * 8:(tt + 1) * 8]:
                              e.dma_start(out=o, in_=c.tmp16[:, 0:8]), waits=[dgr], dma=True)
        ee1 = p.op("vector", lambda e: e.reduce_max(m1c[:, :], gates[:, :], axis=AX.X), waits=[ge])
        ee2 = p.op("vector", lambda e: e.tensor_single_scalar(eq1[:, :], gates[:, :], m1c[:, :], ALU.is_ge),
                   waits=[ee1])
        ee3 = p.op("vector", lambda e: e.scalar_tensor_tensor(tmp8[:, :], eq1[:, :], -1e30, gates[:, :],
                                                             ALU.mult, ALU.add), waits=[ee2])
        ee4 = p.op("vector", lambda e: e.reduce_max(m2c[:, :], tmp8[:, :], axis=AX.X), waits=[ee3])
        ee5 = p.op("vector", lambda e: e.tensor_single_scalar(eq2[:, :], tmp8[:, :], m2c[:, :], ALU.is_ge),
                   waits=[ee4])
        ee6 = p.op("vector", lambda e: e.tensor_sub(c.halfcol[:, :], m1c[:, :], m2c[:, :]), waits=[ee4, prev8])
        ee7 = p.op("scalar", lambda e: e.activation(s1c[:, :], c.halfcol[:, :], AF.Sigmoid), waits=[ee6, prev10])
        ee8 = p.op("scalar", lambda e: e.activation(s2c[:, :], s1c[:, :], AF.Copy, scale=-1.0, bias=1.0),
                   waits=[ee7])
        ee9 = p.op("vector", lambda e: e.tensor_scalar(tmp8[:, :], eq1[:, :], s1c[:, :], None, ALU.mult),
                   waits=[ee7, ee5])
        ee10 = p.op("vector", lambda e, o=w_tok[:, tt * 8:(tt + 1) * 8]:
                    e.scalar_tensor_tensor(o, eq2[:, :], s2c[:, :], tmp8[:, :], ALU.mult, ALU.add),
                    waits=[ee8, ee9])
        prev8 = ee8
        prev10 = ee10
        wtok_evs.append(ee10)

    p.fence()
    if dbg:
        dwt = nc.dram_tensor("dbg_wtok", [128, NTT * 8], F32, kind="ExternalOutput").ap()
        p.op("sync", lambda e: e.dma_start(out=dwt[:, :], in_=w_tok[:, :]), waits=wtok_evs, dma=True)
        db1 = nc.dram_tensor("dbg_b1T", [128, NHC2 * 8], F32, kind="ExternalOutput").ap()
        p.op("sync", lambda e: e.dma_start(out=db1[:, :], in_=b1T[:, :]), waits=b1evs, dma=True)

    wfm_evs = []
    for tt in range(NTT):
        pf = [c.pF1, c.pF2][tt % 2]
        tr = p.op("tensor", lambda e, o=pf.t[0:8, 0:128], i=w_tok[:, tt * 8:(tt + 1) * 8]:
                  e.transpose(o, i, c.ident_f[:, :]),
                  waits=wtok_evs + [c.ev_identf, pf.ev_r])
        pf.ev_w = tr
        ee = p.op("vector", lambda e, o=w_fm[:, tt * 128:(tt + 1) * 128], i=pf.t[0:8, 0:128]:
                  e.tensor_copy(o, i), waits=[tr])
        pf.ev_r = ee
        wfm_evs.append(ee)

    p.fence()

    gate_done = ("Ev", p.cnt["Ev"])
    c.w1_guard = gate_done
    c.w2_guard = gate_done
    g_guard = [None, None]
    w_seq = []
    for _th in range(2):
        for ei in range(NE):
            for hh in range(2):
                for hg in range(4):
                    w_seq.append(("w1", ei, hh, hg))
                for grp in range(4):
                    w_seq.append(("w2", ei, hh, grp))
    w_evs = {}
    w_guardb = [gate_done, gate_done]
    w_next = [0]

    def issue_w():
        i = w_next[0]
        if i >= len(w_seq):
            return
        w_next[0] = i + 1
        kind, ei, hh, j = w_seq[i]
        par = i % 2
        if kind == "w1":
            hbase = hh * 2048 + j * 512
            src = w1_d[ei, :, hbase:hbase + 512].rearrange("(k p) h -> p k h", p=128)
            dst = wst[par][:, :].rearrange("p (k h) -> p k h", h=512)
        else:
            rbase = hh * 2048 + j * 512
            src = w2_d[ei, rbase:rbase + 512, :].rearrange("(k p) o -> p k o", p=128)
            dst = wst[par][:, :].rearrange("p (k o) -> p k o", o=D)
        w_evs[i] = p.op("gpsimd", lambda e, o=dst, i_=src:
                        e.dma_start(out=o, in_=i_), waits=[w_guardb[par]], dma=True)

    issue_w()
    issue_w()
    w_i = [0]

    for th in range(2):
        oacc_evs = []
        first_e = True
        for ei in range(NE):
            wr_evs = []
            for tsub in range(2):
                mmr = p.op("tensor", lambda e, o=c.pD.t[:, 0:512],
                           l=sel8[:, ei * 128:(ei + 1) * 128],
                           r=w_fm[:, th * 1024 + tsub * 512: th * 1024 + (tsub + 1) * 512],
                           s=True, z=True:
                           e.matmul(o, l, r, start=s, stop=z),
                           waits=wfm_evs + [ls8, c.pD.ev_r])
                c.pD.ev_w = mmr
                er = p.op("vector", lambda e, o=wrep[:, tsub * 512:(tsub + 1) * 512], i=c.pD.t[:, 0:512]:
                          e.tensor_copy(o, i), waits=[mmr])
                c.pD.ev_r = c.confirm()
                wr_evs.append(er)
            for hh in range(2):
                ph_evs = []
                for hg in range(4):
                    gi = w_i[0]
                    w_i[0] = gi + 1
                    par = gi % 2
                    lw1 = w_evs[gi]
                    for hti in range(4):
                        hc = hh * 16 + hg * 4 + hti
                        hloc = hg * 4 + hti
                        for tsub in range(2):
                            pb = [c.pA, c.pB][tsub]
                            mm = None
                            for kk in range(NKC):
                                lh = wst[par][:, kk * 512 + hti * 128: kk * 512 + (hti + 1) * 128]
                                rh = c.fmA[:, kk * T + th * 1024 + tsub * 512: kk * T + th * 1024 + (tsub + 1) * 512]
                                mm = p.op("tensor", lambda e, o=pb.t[:, 0:512], l=lh, r=rh,
                                          s=(kk == 0), z=(kk == NKC - 1):
                                          e.matmul(o, l, r, start=s, stop=z),
                                          waits=ln_evs + [lw1, pb.ev_r] if kk == 0 else [])
                            pb.ev_w = mm
                            w_guardb[par] = mm
                            gl = p.op("scalar", lambda e, o=g_t[tsub], i=pb.t[:, 0:512],
                                      b=b1T[:, hc * 8 + ei: hc * 8 + ei + 1]:
                                      e.activation(o[:, :], i, AF.Gelu, bias=b),
                                      waits=b1evs + [pb.ev_w, g_guard[tsub]])
                            wm = p.op("vector", lambda e, o=h_half[:, hloc * 1024 + tsub * 512: hloc * 1024 + (tsub + 1) * 512],
                                      i=g_t[tsub], w=wrep[:, tsub * 512:(tsub + 1) * 512]:
                                      e.tensor_mul(o, i[:, :], w[:, :]),
                                      waits=wr_evs + [gl])
                            g_guard[tsub] = wm
                            pb.ev_r = wm
                            ph_evs.append(wm)
                    issue_w()
                for grp in range(4):
                    gi = w_i[0]
                    w_i[0] = gi + 1
                    par = gi % 2
                    lw2 = w_evs[gi]
                    for ttl in range(8):
                        for oh in range(2):
                            pb = [c.pC, c.pD][(ttl * 2 + oh) % 2]
                            mm = None
                            for kk in range(4):
                                hloc = grp * 4 + kk
                                lh = h_half[:, hloc * 1024 + ttl * 128: hloc * 1024 + (ttl + 1) * 128]
                                rh = wst[par][:, kk * D + oh * 512: kk * D + (oh + 1) * 512]
                                mm = p.op("tensor", lambda e, o=pb.t[:, 0:512], l=lh, r=rh,
                                          s=(kk == 0), z=(kk == 3):
                                          e.matmul(o, l, r, start=s, stop=z),
                                          waits=ph_evs + [lw2, pb.ev_r] if kk == 0 else [])
                            pb.ev_w = mm
                            w_guardb[par] = mm
                            oslice = out_acc[:, ttl * 1024 + oh * 512: ttl * 1024 + (oh + 1) * 512]
                            if first_e and hh == 0 and grp == 0:
                                ea = p.op("vector", lambda e, o=oslice, i=pb.t[:, 0:512]:
                                          e.tensor_copy(o, i), waits=[mm])
                            else:
                                ea = p.op("vector", lambda e, o=oslice, i=pb.t[:, 0:512]:
                                          e.tensor_add(o, i, o), waits=[mm])
                            pb.ev_r = ea
                            oacc_evs.append(ea)
                    issue_w()
            first_e = False
        p.fence()
        for ttl in range(8):
            tt = th * 8 + ttl
            for oh in range(2):
                pb = [c.pA, c.pB][oh]
                mm = p.op("tensor", lambda e, o=pb.t[:, 0:512],
                          l=w_fm[:, tt * 128:(tt + 1) * 128],
                          r=b2sb[:, oh * 512:(oh + 1) * 512], s=True, z=True:
                          e.matmul(o, l, r, start=s, stop=z),
                          waits=wfm_evs + [lb2, pb.ev_r])
                pb.ev_w = mm
                xv = c.xr[:, tt * D + oh * 512: tt * D + (oh + 1) * 512]
                e1 = p.op("vector", lambda e, o=xv, i=out_acc[:, ttl * 1024 + oh * 512: ttl * 1024 + (oh + 1) * 512]:
                          e.tensor_add(o, i, o), waits=oacc_evs)
                e2 = p.op("vector", lambda e, o=xv, i=pb.t[:, 0:512]:
                          e.tensor_add(o, i, o), waits=[mm, e1])
                pb.ev_r = c.confirm()
        p.fence()



def _mk(nc_args=()):
    nc = bass.Bass("TRN2", target_bir_lowering=False)
    return nc


def _declare_common(nc):
    x_d = nc.dram_tensor("x", [T, D], F32, kind="ExternalInput").ap()
    identb_d = nc.dram_tensor("ident_bf", [128, 128], BF16, kind="ExternalInput").ap()
    identf_d = nc.dram_tensor("ident_f", [128, 128], F32, kind="ExternalInput").ap()
    out_d = nc.dram_tensor("xout", [T, D], F32, kind="ExternalOutput").ap()
    return x_d, identb_d, identf_d, out_d


def _decl_l0attn(nc):
    return (nc.dram_tensor("wqT", [D, D], F32, kind="ExternalInput").ap(),
            nc.dram_tensor("wkT", [D, D], F32, kind="ExternalInput").ap(),
            nc.dram_tensor("wvT", [D, D], F32, kind="ExternalInput").ap(),
            nc.dram_tensor("woT", [D, D], F32, kind="ExternalInput").ap(),
            nc.dram_tensor("mask4", [128, 4 * 512], BF16, kind="ExternalInput").ap(),
            nc.dram_tensor("ones_col", [128, 1], BF16, kind="ExternalInput").ap())


def _decl_l0ffn(nc):
    return (nc.dram_tensor("wgT", [D, FFH], I32, kind="ExternalInput").ap(),
            nc.dram_tensor("wuT", [D, FFH], I32, kind="ExternalInput").ap(),
            nc.dram_tensor("wdT", [FFH, D], I32, kind="ExternalInput").ap())


def _decl_l1attn(nc):
    return (nc.dram_tensor("wxp1T", [D, 80], F32, kind="ExternalInput").ap(),
            nc.dram_tensor("wxp2T", [D, 16], F32, kind="ExternalInput").ap(),
            nc.dram_tensor("wdtT", [DTR, D], F32, kind="ExternalInput").ap(),
            nc.dram_tensor("A_re", [128, 128], F32, kind="ExternalInput").ap(),
            nc.dram_tensor("b_dt_c", [128, 8], F32, kind="ExternalInput").ap(),
            nc.dram_tensor("D_c", [128, 8], F32, kind="ExternalInput").ap(),
            nc.dram_tensor("sel16", [16, 16 * 128], BF16, kind="ExternalInput").ap())


def _decl_l1ffn(nc):
    return (nc.dram_tensor("w1", [NE, D, EH], F32, kind="ExternalInput").ap(),
            nc.dram_tensor("w2", [NE, EH, D], F32, kind="ExternalInput").ap(),
            nc.dram_tensor("b1", [NE, EH], F32, kind="ExternalInput").ap(),
            nc.dram_tensor("b2", [NE, D], F32, kind="ExternalInput").ap(),
            nc.dram_tensor("gw", [D, NE], F32, kind="ExternalInput").ap(),
            nc.dram_tensor("gb", [1, NE], F32, kind="ExternalInput").ap(),
            nc.dram_tensor("temp", [1, 1], F32, kind="ExternalInput").ap(),
            nc.dram_tensor("ones_row", [1, 128], BF16, kind="ExternalInput").ap(),
            nc.dram_tensor("sel8", [8, 8 * 128], BF16, kind="ExternalInput").ap())


def _standalone(decl_fn, emit_fn, dbg=False):
    nc = bass.Bass("TRN2", target_bir_lowering=False)
    x_d, identb_d, identf_d, out_d = _declare_common(nc)
    drs = decl_fn(nc)
    with ExitStack() as stack:
        p = Plan(nc, stack)
        c = mk_ctx(nc, stack, p)
        alloc_common(c)
        emit_load_consts(c, identb_d, identf_d)
        emit_load_x(c, x_d)
        with ExitStack() as bst:
            c.stacks.append(bst)
            emit_fn(c, *drs, dbg=dbg)
            p.fence()
            c.stacks.pop()
        emit_store_x(c, out_d)
        p.wait("gpsimd", p.now())
        with nc.Block() as block:
            p.build(block)
    return nc


def build_l0attn(dbg=False):
    return _standalone(_decl_l0attn, emit_l0attn, dbg)


def build_l0ffn(dbg=False):
    return _standalone(_decl_l0ffn, emit_l0ffn, dbg)


def build_l1attn(dbg=False):
    return _standalone(_decl_l1attn, emit_l1attn, dbg)


def build_l1ffn(dbg=False):
    return _standalone(_decl_l1ffn, emit_l1ffn, dbg)


def build_fused():
    nc = bass.Bass("TRN2", target_bir_lowering=False)
    x_d, identb_d, identf_d, out_d = _declare_common(nc)
    dr0a = _decl_l0attn(nc)
    dr0f = _decl_l0ffn(nc)
    dr1a = _decl_l1attn(nc)
    dr1f = _decl_l1ffn(nc)
    with ExitStack() as stack:
        p = Plan(nc, stack)
        c = mk_ctx(nc, stack, p)
        alloc_common(c)
        emit_load_consts(c, identb_d, identf_d)
        emit_load_x(c, x_d)
        for emit_fn, drs in ((emit_l0attn, dr0a), (emit_l0ffn, dr0f),
                             (emit_l1attn, dr1a), (emit_l1ffn, dr1f)):
            with ExitStack() as bst:
                c.stacks.append(bst)
                emit_fn(c, *drs)
                p.fence()
                c.stacks.pop()
        emit_store_x(c, out_d)
        p.wait("gpsimd", p.now())
        with nc.Block() as block:
            p.build(block)
    return nc

def run_l1ffn(x_full, W1, b1, W2, b2, gate_w, gate_b, temperature):
    nc = build_l1ffn()
    ident_bf, ident_f, _, _ = host_consts()
    sel8 = np.zeros((8, 8 * 128), dtype=BF)
    for e in range(8):
        sel8[e, e * 128:(e + 1) * 128] = 1.0
    ones_row = np.ones((1, 128), dtype=BF)
    in_maps = []
    for i in range(8):
        in_maps.append(dict(x=np.ascontiguousarray(x_full[i]),
                            w1=np.asarray(W1), w2=np.asarray(W2),
                            b1=np.asarray(b1), b2=np.asarray(b2),
                            gw=np.asarray(gate_w), gb=np.asarray(gate_b).reshape(1, 8),
                            temp=np.asarray(temperature).reshape(1, 1),
                            ones_row=ones_row, sel8=sel8,
                            ident_bf=ident_bf, ident_f=ident_f))
    res = run_bass_kernel_spmd(nc, in_maps, core_ids=list(range(8)))
    return np.stack([r["xout"] for r in res.results])



def host_consts():
    ident_bf = np.eye(128, dtype=BF)
    ident_f = np.eye(128, dtype=np.float32)
    mask4 = np.zeros((128, 4 * 512), dtype=BF)
    for rel in range(4):
        r = np.arange(128)[:, None]
        j = np.arange(512)[None, :]
        mask4[:, rel * 512:(rel + 1) * 512] = (j >= r + rel * 128).astype(BF)
    ones_col = np.ones((128, 1), dtype=BF)
    return ident_bf, ident_f, mask4, ones_col


def run_l0attn(x_full, Wq, Wk, Wv, Wo):
    nc = build_l0attn()
    ident_bf, ident_f, mask4, ones_col = host_consts()
    wqT = np.ascontiguousarray(Wq.T)
    wkT = np.ascontiguousarray(Wk.T)
    wvT = np.ascontiguousarray(Wv.T)
    woT = np.ascontiguousarray(Wo.T)
    in_maps = []
    for i in range(8):
        in_maps.append(dict(x=np.ascontiguousarray(x_full[i]), wqT=wqT, wkT=wkT, wvT=wvT, woT=woT,
                            ident_bf=ident_bf, ident_f=ident_f, mask4=mask4, ones_col=ones_col))
    res = run_bass_kernel_spmd(nc, in_maps, core_ids=list(range(8)))
    return np.stack([r["xout"] for r in res.results])


def run_l0ffn(x_full, w_gate, w_up, w_down):
    nc = build_l0ffn()
    ident_bf, ident_f, _, _ = host_consts()
    wgT = np.ascontiguousarray(w_gate.T)
    wuT = np.ascontiguousarray(w_up.T)
    wdT = np.ascontiguousarray(w_down.T)
    in_maps = []
    for i in range(8):
        in_maps.append(dict(x=np.ascontiguousarray(x_full[i]), wgT=wgT, wuT=wuT, wdT=wdT,
                            ident_bf=ident_bf, ident_f=ident_f))
    res = run_bass_kernel_spmd(nc, in_maps, core_ids=list(range(8)))
    return np.stack([r["xout"] for r in res.results])


def _l1attn_host(A_log, Dp, W_xproj, W_dt, b_dt):
    wxpT = np.ascontiguousarray(np.asarray(W_xproj).T)
    sel16 = np.zeros((16, 16 * 128), dtype=BF)
    for n in range(16):
        sel16[n, n * 128:(n + 1) * 128] = 1.0
    return dict(
        wxp1T=np.ascontiguousarray(wxpT[:, 0:80]),
        wxp2T=np.ascontiguousarray(wxpT[:, 80:96]),
        wdtT=np.ascontiguousarray(np.asarray(W_dt).T),
        A_re=np.ascontiguousarray(np.asarray(A_log).reshape(8, 128, 16).transpose(1, 0, 2).reshape(128, 128)),
        b_dt_c=np.ascontiguousarray(np.asarray(b_dt).reshape(8, 128).T),
        D_c=np.ascontiguousarray(np.asarray(Dp).reshape(8, 128).T),
        sel16=sel16)


def run_l1attn(x_full, A_log, Dp, W_xproj, W_dt, b_dt):
    nc = build_l1attn()
    ident_bf, ident_f, _, _ = host_consts()
    hm = _l1attn_host(A_log, Dp, W_xproj, W_dt, b_dt)
    in_maps = []
    for i in range(8):
        in_maps.append(dict(x=np.ascontiguousarray(x_full[i]), ident_bf=ident_bf, ident_f=ident_f, **hm))
    res = run_bass_kernel_spmd(nc, in_maps, core_ids=list(range(8)))
    return np.stack([r["xout"] for r in res.results])


def _sel8_host():
    sel8 = np.zeros((8, 8 * 128), dtype=BF)
    for e in range(8):
        sel8[e, e * 128:(e + 1) * 128] = 1.0
    return sel8


def kernel(**inputs):
    g = lambda k: np.asarray(inputs[k])
    x = np.asarray(inputs["L0_attn_x"], dtype=np.float32)
    nc = build_fused()
    ident_bf, ident_f, mask4, ones_col = host_consts()
    hm1a = _l1attn_host(g("L1_attn_A_log"), g("L1_attn_D"), g("L1_attn_W_xproj"),
                        g("L1_attn_W_dt"), g("L1_attn_b_dt"))
    shared = dict(
        ident_bf=ident_bf, ident_f=ident_f, mask4=mask4, ones_col=ones_col,
        wqT=np.ascontiguousarray(g("L0_attn_Wq").T), wkT=np.ascontiguousarray(g("L0_attn_Wk").T),
        wvT=np.ascontiguousarray(g("L0_attn_Wv").T), woT=np.ascontiguousarray(g("L0_attn_Wo").T),
        wgT=np.ascontiguousarray(g("L0_ffn_w_gate").T), wuT=np.ascontiguousarray(g("L0_ffn_w_up").T),
        wdT=np.ascontiguousarray(g("L0_ffn_w_down").T),
        w1=g("L1_ffn_W1"), w2=g("L1_ffn_W2"), b1=g("L1_ffn_b1"), b2=g("L1_ffn_b2"),
        gw=g("L1_ffn_gate_w"), gb=g("L1_ffn_gate_b").reshape(1, 8),
        temp=g("L1_ffn_temperature").reshape(1, 1),
        ones_row=np.ones((1, 128), dtype=BF), sel8=_sel8_host(),
        **hm1a)
    in_maps = [dict(x=np.ascontiguousarray(x[i]), **shared) for i in range(8)]
    res = run_bass_kernel_spmd(nc, in_maps, core_ids=list(range(8)))
    return np.stack([r["xout"] for r in res.results])


# version 06
# speedup vs baseline: 2.6263x; 1.0074x over previous
import numpy as np
import ml_dtypes
from contextlib import ExitStack
import concourse.bass as bass
import concourse.mybir as mybir
from concourse.bass_utils import run_bass_kernel_spmd

F32 = mybir.dt.float32
BF16 = mybir.dt.bfloat16
I32 = mybir.dt.int32
AF = mybir.ActivationFunctionType
ALU = mybir.AluOpType
AX = mybir.AxisListType
BF = ml_dtypes.bfloat16

T, D = 2048, 1024
NTT = T // 128
NKC = D // 128
NTQ = T // 512
NH, DH = 16, 64
FFH = 3072
NE, EH = 8, 4096
DST, DTR = 16, 64
LN_EPS = 1e-5
QSCALE = 1.0 / 8.0
DEQ = 1.0 / 127.0

ENGINES = ("tensor", "vector", "scalar", "gpsimd", "sync")
ENG_EV = {"tensor": "Et", "vector": "Ev", "scalar": "Es", "gpsimd": "Eg", "sync": "Ey"}
DMA_SLOTS = {"gpsimd": 8, "sync": 6, "scalar": 2}
DMA_PREFIX = {"gpsimd": "Dg", "sync": "Dy", "scalar": "Da"}


class Plan:

    def __init__(self, nc, stack):
        self.nc = nc
        self.q = {e: [] for e in ENGINES}
        self.sems = {}
        self.cnt = {}
        self.rr = {e: 0 for e in DMA_SLOTS}
        for s in ENG_EV.values():
            self.sems[s] = stack.enter_context(nc.semaphore(s))
            self.cnt[s] = 0
        for e, n in DMA_SLOTS.items():
            for i in range(n):
                s = f"{DMA_PREFIX[e]}{i}"
                self.sems[s] = stack.enter_context(nc.semaphore(s))
                self.cnt[s] = 0

    def op(self, eng, fn, waits=(), dma=False):
        waits = [w for w in waits if w is not None]
        if dma:
            slot = self.rr[eng]
            self.rr[eng] = (slot + 1) % DMA_SLOTS[eng]
            sem = f"{DMA_PREFIX[eng]}{slot}"
            amt = 16
            if self.cnt[sem] > 0:
                waits = waits + [(sem, self.cnt[sem])]
        else:
            sem = ENG_EV[eng]
            amt = 1
            if self.cnt[sem] > 0 and eng != "tensor":
                waits = waits + [(sem, self.cnt[sem])]
        self.cnt[sem] += amt
        ev = (sem, self.cnt[sem])
        wmax = {}
        for s, v in waits:
            if v > wmax.get(s, -1):
                wmax[s] = v
        sems = self.sems

        def g(e, fn=fn, wmax=wmax, sem=sem, amt=amt):
            for s, v in wmax.items():
                e.wait_ge(sems[s], v)
            fn(e).then_inc(sems[sem], amt)
        self.q[eng].append(g)
        return ev

    def wait(self, eng, waits):
        wmax = {}
        for wv in waits:
            if wv is None:
                continue
            s, v = wv
            if v > wmax.get(s, -1):
                wmax[s] = v
        sems = self.sems

        def g(e, wmax=wmax):
            for s, v in wmax.items():
                e.wait_ge(sems[s], v)
        self.q[eng].append(g)

    def now(self):
        return [(s, v) for s, v in self.cnt.items() if v > 0]

    def fence(self):
        evs = self.now()
        for e in ENGINES:
            self.wait(e, evs)

    def build(self, block):
        for name in ENGINES:
            dec = getattr(block, name)
            fns = self.q[name]

            def body(e, fns=fns):
                for f in fns:
                    f(e)
            dec(body)


class PBuf:

    def __init__(self, t):
        self.t = t
        self.ev_w = None
        self.ev_r = None


class Ctx:
    pass


def mk_ctx(nc, stack, p):
    c = Ctx()
    c.nc = nc
    c.st = stack
    c.p = p
    c.stacks = [stack]

    def sb(name, shape, dt):
        return c.stacks[-1].enter_context(nc.sbuf_tensor(name, shape, dt))
    c.sb = sb

    def ps(name, shape, dt):
        return PBuf(stack.enter_context(nc.psum_tensor(name, shape, dt)))
    c.ps = ps

    c.dummy = c.sb("dummy_cf", [1, 16], F32)
    p.op("vector", lambda e: e.memset(c.dummy[:, :], 0.0))

    def confirm(eng="vector"):
        if eng == "scalar":
            return p.op(eng, lambda e: e.copy(c.dummy[0:1, 8:16], c.dummy[0:1, 0:8]))
        return p.op(eng, lambda e: e.tensor_copy(c.dummy[0:1, 8:16], c.dummy[0:1, 0:8]))
    c.confirm = confirm
    return c



def alloc_common(c, with_fmA=True):
    c.xr = c.sb("xr", [128, NTT * D], F32)
    if with_fmA:
        c.fmA = c.sb("fmA", [128, NKC * T], BF16)
    c.ident_bf = c.sb("identb_sb", [128, 128], BF16)
    c.ident_f = c.sb("identf_sb", [128, 128], F32)
    c.stat1 = c.sb("stat1", [128, NTT], F32)
    c.stat2 = c.sb("stat2", [128, NTT], F32)
    c.mcols = c.sb("mcols", [128, NTT], F32)
    c.rcols = c.sb("rcols", [128, NTT], F32)
    c.tmp16 = c.sb("tmp16", [128, NTT], F32)
    c.halfcol = c.sb("halfcol", [128, 1], F32)
    c.ln_scr = c.sb("ln_scr", [128, D], BF16)
    c.stageF = c.sb("stageF", [128, 512], F32)
    c.pA = c.ps("pA", [128, 512], F32)
    c.pB = c.ps("pB", [128, 512], F32)
    c.pC = c.ps("pC", [128, 512], F32)
    c.pD = c.ps("pD", [128, 512], F32)
    c.pT1 = c.ps("pT1", [128, 1024], BF16)
    c.pT2 = c.ps("pT2", [128, 1024], BF16)
    c.pF1 = c.ps("pF1", [128, 512], F32)
    c.pF2 = c.ps("pF2", [128, 512], F32)


def emit_load_consts(c, ident_bf_d, ident_f_d):
    p = c.p
    c.ev_ident = p.op("sync", lambda e, o=c.ident_bf, i=ident_bf_d: e.dma_start(out=o[:, :], in_=i[:, :]), dma=True)
    c.ev_identf = p.op("sync", lambda e, o=c.ident_f, i=ident_f_d: e.dma_start(out=o[:, :], in_=i[:, :]), dma=True)


def emit_load_x(c, x_d):
    p = c.p
    src = x_d.rearrange("(tt p) d -> p tt d", p=128)
    dst = c.xr[:, :].rearrange("p (tt d) -> p tt d", d=D)
    c.ev_x = p.op("sync", lambda e, o=dst, i=src: e.dma_start(out=o, in_=i), dma=True)


def emit_store_x(c, out_d, waits=None):
    p = c.p
    if waits is None:
        waits = p.now()
    dst = out_d.rearrange("(tt p) d -> p tt d", p=128)
    src = c.xr[:, :].rearrange("p (tt d) -> p tt d", d=D)
    ev = p.op("sync", lambda e, o=dst, i=src: e.dma_start(out=o, in_=i), waits=waits, dma=True)
    return ev


def emit_ln_stats(c, deps):
    p = c.p
    evs = []
    prev_hc = None
    for tt in range(NTT):
        xv = c.xr[:, tt * D:(tt + 1) * D]
        e1 = p.op("vector", lambda e, o=c.stat1[:, tt:tt + 1], i=xv: e.reduce_sum(o, i, axis=AX.X), waits=deps)
        e2a = p.op("scalar", lambda e, o=c.stageF, i=c.xr[:, tt * D: tt * D + 512], a=c.stat2[:, tt:tt + 1]:
                   e.activation(o[:, :], i, AF.Square, accum_out=a), waits=deps)
        e2b = p.op("scalar", lambda e, o=c.stageF, i=c.xr[:, tt * D + 512: (tt + 1) * D], a=c.halfcol:
                   e.activation(o[:, :], i, AF.Square, accum_out=a[:, :]), waits=deps + [prev_hc])
        e2c = p.op("vector", lambda e, o=c.stat2[:, tt:tt + 1]:
                   e.tensor_add(o, o, c.halfcol[:, :]), waits=[e2a, e2b])
        prev_hc = e2c
        evs += [e1, e2c]
    em = p.op("vector", lambda e: e.tensor_scalar_mul(c.mcols[:, :], c.stat1[:, :], 1.0 / D), waits=evs)
    ev = p.op("vector", lambda e: e.tensor_scalar_mul(c.rcols[:, :], c.stat2[:, :], 1.0 / D), waits=evs)
    e3 = p.op("vector", lambda e: e.tensor_mul(c.tmp16[:, :], c.mcols[:, :], c.mcols[:, :]), waits=[em])
    e4 = p.op("vector", lambda e: e.tensor_sub(c.rcols[:, :], c.rcols[:, :], c.tmp16[:, :]), waits=[ev, e3])
    e5 = p.op("vector", lambda e: e.tensor_scalar_add(c.rcols[:, :], c.rcols[:, :], LN_EPS), waits=[e4])
    e6 = p.op("vector", lambda e: e.reciprocal(c.tmp16[:, :], c.rcols[:, :]), waits=[e5])
    e7 = p.op("scalar", lambda e: e.activation(c.rcols[:, :], c.tmp16[:, :], AF.Sqrt), waits=[e6])
    return [em, e7]


def emit_ln_to_fm(c, deps, dst=None, dst_stride=T):
    p = c.p
    if dst is None:
        dst = c.fmA
    sevs = emit_ln_stats(c, deps)
    out_evs = []
    pts = [c.pT1, c.pT2]
    prev_tr = None
    for tt in range(NTT):
        xv = c.xr[:, tt * D:(tt + 1) * D]
        ea = p.op("vector", lambda e, o=c.ln_scr, i=xv, m=c.mcols[:, tt:tt + 1], r=c.rcols[:, tt:tt + 1]:
                  e.tensor_scalar(o[:, :], i, m, r, ALU.subtract, ALU.mult),
                  waits=sevs + deps + [prev_tr])
        pt = pts[tt % 2]
        tr = None
        for k in range(NKC):
            tr = p.op("tensor", lambda e, o=pt.t[:, k * 128:(k + 1) * 128], i=c.ln_scr[:, k * 128:(k + 1) * 128],
                      idn=c.ident_bf: e.transpose(o, i, idn[:, :]),
                      waits=[ea, c.ev_ident, pt.ev_r] if k == 0 else [ea])
        pt.ev_w = tr
        prev_tr = tr
        dv = dst[:, :].rearrange("p (k r) -> p k r", r=dst_stride)[:, :, tt * 128:(tt + 1) * 128]
        sv = pt.t[:, :].rearrange("p (k r) -> p k r", r=128)
        ee = p.op("vector", lambda e, o=dv, i=sv: e.tensor_copy(o, i), waits=[tr])
        pt.ev_r = c.confirm()
        out_evs.append(ee)
    return out_evs


def emit_fm_psum_to_resid(c, src_sb_f32, ot, tq, deps, scale=None):
    p = c.p
    evs = []
    pfs = [c.pF1, c.pF2]
    for j in range(4):
        pf = pfs[j % 2]
        tt = tq * 4 + j
        tr = p.op("tensor", lambda e, o=pf.t[:, 0:128], i=src_sb_f32[:, j * 128:(j + 1) * 128], idn=c.ident_f:
                  e.transpose(o, i, idn[:, :]), waits=deps + [c.ev_identf, pf.ev_r])
        pf.ev_w = tr
        xv = c.xr[:, tt * D + ot * 128: tt * D + (ot + 1) * 128]
        ee = p.op("vector", lambda e, o=xv, i=pf.t[:, 0:128], x=xv:
                  e.tensor_add(o, i, x), waits=[tr])
        pf.ev_r = c.confirm()
        evs.append(ee)
    return evs



def emit_l0attn(c, wq_d, wk_d, wv_d, wo_d, mask_d, ones_d, dbg=False):
    nc = c.nc
    p = c.p
    mask = c.sb("mask_sb", [128, 4 * 512], BF16)
    ones_col = c.sb("ones_sb", [128, 1], BF16)
    ev_mask = p.op("sync", lambda e: e.dma_start(out=mask[:, :], in_=mask_d[:, :]), dma=True)
    ev_ones = p.op("sync", lambda e: e.dma_start(out=ones_col[:, :], in_=ones_d[:, :]), dma=True)
    ev_vc1 = p.op("vector", lambda e: e.memset(vc[:, :], 1.0))

    wbuf = [c.sb("wbuf0", [128, NKC * 128], BF16),
            c.sb("wbuf1", [128, NKC * 128], BF16)]
    wobuf = c.sb("wobuf", [128, NKC * D], BF16)
    qc = c.sb("qc", [128, T], BF16)
    kc = c.sb("kc", [128, T], BF16)
    vc = c.sb("vc", [128, NTT * 130], BF16)
    ptb = c.sb("ptb", [128, 16 * 512], BF16)
    att = c.sb("att", [128, NTT * 128], BF16)
    inv = c.sb("inv", [128, 1], F32)

    ln_evs = emit_ln_to_fm(c, [c.ev_x])
    if dbg:
        dfm = nc.dram_tensor("dbg_fm", [128, NKC * T], BF16, kind="ExternalOutput").ap()
        p.op("sync", lambda e: e.dma_start(out=dfm[:, :], in_=c.fmA[:, :]), waits=ln_evs, dma=True)

    qkv_seq = [(kc_i, pi) for kc_i in range(NKC) for pi in range(3)]
    qkv_evs = {}
    qkv_guardb = [None, None]
    qkv_next = [0]

    def issue_qkv():
        i = qkv_next[0]
        if i >= len(qkv_seq):
            return
        qkv_next[0] = i + 1
        kc_i, pi = qkv_seq[i]
        par = i % 2
        w_d = (wq_d, wk_d, wv_d)[pi]
        src = w_d[:, kc_i * 128:(kc_i + 1) * 128].rearrange("(k p) o -> p k o", p=128)
        qkv_evs[i] = p.op("gpsimd", lambda e, o=wbuf[par][:, :].rearrange("p (k o) -> p k o", o=128), i_=src:
                          e.dma_start(out=o, in_=i_), waits=[qkv_guardb[par]], dma=True)

    issue_qkv()
    issue_qkv()
    qkv_i = [0]
    lwo = p.op("gpsimd", lambda e, o=wobuf[:, :].rearrange("p (k o) -> p k o", o=D),
               i_=wo_d.rearrange("(k p) o -> p k o", p=128): e.dma_start(out=o, in_=i_), dma=True)

    wo_stage_deps = []
    attn_fm = c.sb("attn_fm", [128, NKC * T], BF16)

    for kc_i in range(NKC):
        prev_evq = []
        for (w_d, dstbuf, isv) in ((wq_d, qc, False), (wk_d, kc, False), (wv_d, vc, True)):
            wi = qkv_i[0]
            qkv_i[0] = wi + 1
            wpar = wi % 2
            lw = qkv_evs[wi]
            if not isv:
                for tq in range(NTQ):
                    pb = [c.pA, c.pB][tq % 2]
                    mws = ln_evs + [lw, pb.ev_r]
                    mm = None
                    for kk in range(NKC):
                        lh = wbuf[wpar][:, kk * 128:(kk + 1) * 128]
                        rh = c.fmA[:, kk * T + tq * 512: kk * T + (tq + 1) * 512]
                        mm = p.op("tensor", lambda e, o=pb.t[:, 0:512], l=lh, r=rh,
                                  s=(kk == 0), z=(kk == NKC - 1):
                                  e.matmul(o, l, r, start=s, stop=z),
                                  waits=mws if kk == 0 else [])
                    pb.ev_w = mm
                    qkv_guardb[wpar] = mm
                    ee = p.op("vector", lambda e, o=dstbuf[:, tq * 512:(tq + 1) * 512], i=pb.t[:, 0:512]:
                              e.tensor_copy(o, i), waits=[pb.ev_w])
                    pb.ev_r = ee
                    prev_evq.append(ee)
            else:
                for tt in range(NTT):
                    pb = [c.pA, c.pB][tt % 2]
                    mws = ln_evs + [lw, pb.ev_r]
                    mm = None
                    for kk in range(NKC):
                        lh = c.fmA[:, kk * T + tt * 128: kk * T + (tt + 1) * 128]
                        rh = wbuf[wpar][:, kk * 128:(kk + 1) * 128]
                        mm = p.op("tensor", lambda e, o=pb.t[:, 0:128], l=lh, r=rh,
                                  s=(kk == 0), z=(kk == NKC - 1):
                                  e.matmul(o, l, r, start=s, stop=z),
                                  waits=mws if kk == 0 else [])
                    pb.ev_w = mm
                    qkv_guardb[wpar] = mm
                    dv = vc[:, :].rearrange("p (tt j x) -> p tt j x", j=2, x=65)[:, tt, :, 0:64]
                    sv = pb.t[:, 0:128].rearrange("p (j x) -> p j x", j=2)
                    ee = p.op("vector", lambda e, o=dv, i=sv:
                              e.tensor_copy(o, i), waits=[pb.ev_w, ev_vc1])
                    pb.ev_r = ee
                    prev_evq.append(ee)
            issue_qkv()
        qkv_done = list(prev_evq)
        if dbg and kc_i == 0:
            dq = nc.dram_tensor("dbg_q", [128, T], BF16, kind="ExternalOutput").ap()
            dk = nc.dram_tensor("dbg_k", [128, T], BF16, kind="ExternalOutput").ap()
            dv = nc.dram_tensor("dbg_v", [128, NTT * 128], BF16, kind="ExternalOutput").ap()
            p.op("sync", lambda e: e.dma_start(out=dq[:, :], in_=qc[:, :]), waits=qkv_done, dma=True)
            p.op("sync", lambda e: e.dma_start(out=dk[:, :], in_=kc[:, :]), waits=qkv_done, dma=True)
            p.op("sync", lambda e: e.dma_start(out=dv[:, :], in_=vc[:, :]), waits=qkv_done, dma=True)

        att_evs = []
        for hh in range(2):
            hbase = hh * 64
            for sq in range(NTQ):
                nkt = 4 * sq + 4
                exp_evs = []
                for kt in range(nkt):
                    pb = [c.pA, c.pB, c.pC, c.pD][kt % 4]
                    mm = p.op("tensor", lambda e, o=pb.t[:, 0:512],
                              l=kc[hbase:hbase + 64, kt * 128:(kt + 1) * 128],
                              r=qc[hbase:hbase + 64, sq * 512:(sq + 1) * 512]:
                              e.matmul(o, l, r, start=True, stop=True),
                              waits=qkv_done + [pb.ev_r])
                    pb.ev_w = mm
                    ex = p.op("scalar", lambda e, o=ptb[:, kt * 512:(kt + 1) * 512], i=pb.t[:, 0:512]:
                              e.activation(o, i, AF.Exp, scale=QSCALE), waits=[mm])
                    pb.ev_r = ex
                    exp_evs.append(ex)
                    if kt >= 4 * sq:
                        rel = kt - 4 * sq
                        me = p.op("vector", lambda e, o=ptb[:, kt * 512:(kt + 1) * 512],
                                  m=mask[:, rel * 512:(rel + 1) * 512]:
                                  e.tensor_mul(o, o, m), waits=[ex, ev_mask])
                        exp_evs.append(me)
                for si in range(4):
                    stile = sq * 4 + si
                    mm1 = None
                    for kt in range(nkt):
                        lh = ptb[:, kt * 512 + si * 128: kt * 512 + (si + 1) * 128]
                        mm1 = p.op("tensor", lambda e, o=c.pC.t[:, 0:65], l=lh,
                                   r=vc[:, kt * 130 + hh * 65: kt * 130 + hh * 65 + 65],
                                   s=(kt == 0), z=(kt == nkt - 1):
                                   e.matmul(o, l, r, start=s, stop=z),
                                   waits=exp_evs + [c.pC.ev_r] if kt == 0 else [])
                    c.pC.ev_w = mm1
                    rc = p.op("vector", lambda e: e.reciprocal(inv[:, :], c.pC.t[:, 64:65]), waits=[mm1])
                    ee = p.op("vector", lambda e, o=att[:, stile * 128 + hbase: stile * 128 + hbase + 64],
                              i=c.pC.t[:, 0:64]:
                              e.tensor_scalar(o, i, inv[:, :], None, ALU.mult), waits=[rc])
                    c.pC.ev_r = ee
                    att_evs.append(ee)
        if dbg and kc_i == 0:
            da = nc.dram_tensor("dbg_att", [128, NTT * 128], BF16, kind="ExternalOutput").ap()
            p.op("sync", lambda e: e.dma_start(out=da[:, :], in_=att[:, :]), waits=att_evs, dma=True)
        if dbg and kc_i == 7:
            da7 = nc.dram_tensor("dbg_att7", [128, NTT * 128], BF16, kind="ExternalOutput").ap()
            p.op("sync", lambda e: e.dma_start(out=da7[:, :], in_=att[:, :]), waits=att_evs, dma=True)
        pts = [c.pT1, c.pT2]
        for tt in range(NTT):
            pt = pts[tt % 2]
            tr = p.op("tensor", lambda e, o=pt.t[:, 0:128], i=att[:, tt * 128:(tt + 1) * 128], idn=c.ident_bf:
                      e.transpose(o, i, idn[:, :]), waits=att_evs + [c.ev_ident, pt.ev_r])
            pt.ev_w = tr
            ee = p.op("vector", lambda e, o=attn_fm[:, kc_i * T + tt * 128: kc_i * T + (tt + 1) * 128],
                      i=pt.t[:, 0:128]: e.tensor_copy(o, i), waits=[tr])
            pt.ev_r = ee
            wo_stage_deps.append(ee)

    if dbg:
        dafm = nc.dram_tensor("dbg_attnfm", [128, NKC * T], BF16, kind="ExternalOutput").ap()
        p.op("sync", lambda e: e.dma_start(out=dafm[:, :], in_=attn_fm[:, :]), waits=wo_stage_deps, dma=True)
    for tt in range(NTT):
        for oh in range(2):
            pb = [c.pA, c.pB][(tt * 2 + oh) % 2]
            mm = None
            for kk in range(NKC):
                lh = attn_fm[:, kk * T + tt * 128: kk * T + (tt + 1) * 128]
                rh = wobuf[:, kk * D + oh * 512: kk * D + (oh + 1) * 512]
                mm = p.op("tensor", lambda e, o=pb.t[:, 0:512], l=lh, r=rh,
                          s=(kk == 0), z=(kk == NKC - 1):
                          e.matmul(o, l, r, start=s, stop=z),
                          waits=wo_stage_deps + [lwo, pb.ev_r] if kk == 0 else [])
            pb.ev_w = mm
            xv = c.xr[:, tt * D + oh * 512: tt * D + (oh + 1) * 512]
            ec = p.op("vector", lambda e, o=xv, i=pb.t[:, 0:512]:
                      e.tensor_add(o, i, o), waits=[pb.ev_w])
            pb.ev_r = c.confirm()



def emit_l0ffn(c, wg_d, wu_d, wd_d, dbg=False):
    nc = c.nc
    p = c.p
    NHC = FFH // 128
    NHG = FFH // 512
    h_sb = c.sb("h_sb", [128, NHC * 512], BF16)
    g_grp = c.sb("g_grp", [128, 4 * 512], BF16)
    wd_sb = c.sb("wd_sb", [128, NHC * D], BF16)
    wgu_sb = [c.sb("wgu_sb0", [128, NKC * 512], BF16),
              c.sb("wgu_sb1", [128, NKC * 512], BF16)]

    ln_evs = emit_ln_to_fm(c, [c.ev_x])

    wdv = wd_d.rearrange("(k p) o -> p k o", p=128)
    lwd = p.op("gpsimd", lambda e, o=wd_sb[:, :].rearrange("p (k o) -> p k o", o=D), i=wdv:
               e.dma_start(out=o, in_=i), dma=True)

    g_guard = [None] * 4
    sil_ev = [None] * 4
    gu_seq = [(tq, hg, gi) for tq in range(NTQ) for hg in range(NHG) for gi in range(2)]
    gu_evs = {}
    gu_guardb = [None, None]
    gu_next = [0]

    def issue_gu():
        i = gu_next[0]
        if i >= len(gu_seq):
            return
        gu_next[0] = i + 1
        _tq, hg, gi = gu_seq[i]
        par = i % 2
        w_d = (wg_d, wu_d)[gi]
        src = w_d[:, hg * 512:(hg + 1) * 512].rearrange("(k p) h -> p k h", p=128)
        gu_evs[i] = p.op("gpsimd", lambda e, o=wgu_sb[par][:, :].rearrange("p (k h) -> p k h", h=512), i_=src:
                         e.dma_start(out=o, in_=i_), waits=[gu_guardb[par]], dma=True)

    issue_gu()
    issue_gu()
    gu_i = [0]
    for tq in range(NTQ):
        hevs = []
        for hg in range(NHG):
            for (w_d, is_gate) in ((wg_d, True), (wu_d, False)):
                gi_idx = gu_i[0]
                gu_i[0] = gi_idx + 1
                par = gi_idx % 2
                lw = gu_evs[gi_idx]
                for hti in range(4):
                    ht = hg * 4 + hti
                    pb = c.pA if is_gate else c.pB
                    mm = None
                    for kk in range(NKC):
                        lh = wgu_sb[par][:, kk * 512 + hti * 128: kk * 512 + (hti + 1) * 128]
                        rh = c.fmA[:, kk * T + tq * 512: kk * T + (tq + 1) * 512]
                        mm = p.op("tensor", lambda e, o=pb.t[:, 0:512], l=lh, r=rh,
                                  s=(kk == 0), z=(kk == NKC - 1):
                                  e.matmul(o, l, r, start=s, stop=z),
                                  waits=ln_evs + [lw, pb.ev_r] if kk == 0 else [])
                    pb.ev_w = mm
                    gu_guardb[par] = mm
                    if is_gate:
                        ee = p.op("scalar", lambda e, o=g_grp[:, hti * 512:(hti + 1) * 512], i=pb.t[:, 0:512]:
                                  e.activation(o, i, AF.Silu, scale=DEQ),
                                  waits=[pb.ev_w, g_guard[hti]])
                        sil_ev[hti] = ee
                    else:
                        ee = p.op("vector", lambda e, o=h_sb[:, ht * 512:(ht + 1) * 512],
                                  i=g_grp[:, hti * 512:(hti + 1) * 512], u=pb.t[:, 0:512]:
                                  e.tensor_mul(o, i, u), waits=[pb.ev_w, sil_ev[hti]])
                        g_guard[hti] = ee
                    pb.ev_r = ee
                    hevs.append(ee)
                issue_gu()
        for ttl in range(4):
            tt = tq * 4 + ttl
            for oh in range(2):
                pb = [c.pC, c.pD][(ttl * 2 + oh) % 2]
                mm = None
                for hk in range(NHC):
                    lh = h_sb[:, hk * 512 + ttl * 128: hk * 512 + (ttl + 1) * 128]
                    rh = wd_sb[:, hk * D + oh * 512: hk * D + (oh + 1) * 512]
                    mm = p.op("tensor", lambda e, o=pb.t[:, 0:512], l=lh, r=rh,
                              s=(hk == 0), z=(hk == NHC - 1):
                              e.matmul(o, l, r, start=s, stop=z),
                              waits=hevs + [lwd, pb.ev_r] if hk == 0 else [])
                pb.ev_w = mm
                xv = c.xr[:, tt * D + oh * 512: tt * D + (oh + 1) * 512]
                ec = p.op("vector", lambda e, o=xv, i=pb.t[:, 0:512]:
                          e.scalar_tensor_tensor(o, i, DEQ * DEQ, o, ALU.mult, ALU.add),
                          waits=[pb.ev_w])
                pb.ev_r = c.confirm()



def emit_l1attn(c, wxp1_d, wxp2_d, wdt_d, are_d, bdt_d, dd_d, sel_d, dbg=False):
    nc = c.nc
    p = c.p
    wxp1 = c.sb("wxp1", [128, NKC * 80], BF16)
    wxp2 = c.sb("wxp2", [128, NKC * 16], BF16)
    wdt = c.sb("wdt", [64, D], BF16)
    a_re = c.sb("a_re", [128, 128], F32)
    a_sc = c.sb("a_sc", [128, 128], F32)
    bdt = c.sb("bdt", [128, 8], F32)
    dcol = c.sb("dcol", [128, 8], F32)
    sel16 = c.sb("sel16_sb", [16, 16 * 128], BF16)
    dtl = c.sb("dtl", [64, T], BF16)
    b_bf = c.sb("b_bf", [16, T], BF16)
    c_bf = c.sb("c_bf", [16, T], BF16)
    dt_sb = c.sb("dt_sb", [128, NKC * T], BF16)
    u_sb = c.sb("u_sb", [128, NKC * 512], BF16)
    y_acc = c.sb("y_acc", [128, NKC * 512], F32)
    carry = c.sb("carry", [128, NKC * DST], F32)
    brep = c.sb("brep", [128, 512], BF16)
    crep = c.sb("crep", [128, 512], BF16)
    sA = c.sb("sA", [128, 512], F32)
    sB = c.sb("sB", [128, 512], F32)
    NPAR = 3
    a_t = [c.sb(f"a_t{i}", [128, 512], BF16) for i in range(NPAR)]
    b_t = [c.sb(f"b_t{i}", [128, 512], BF16) for i in range(NPAR)]
    h_t = [c.sb(f"h_t{i}", [128, 512], BF16) for i in range(NPAR)]
    y_t = [c.sb(f"y_t{i}", [128, 512], BF16) for i in range(NPAR)]

    lw1 = p.op("gpsimd", lambda e, o=wxp1[:, :].rearrange("p (k o) -> p k o", o=80),
               i=wxp1_d.rearrange("(k p) o -> p k o", p=128): e.dma_start(out=o, in_=i), dma=True)
    lw2 = p.op("gpsimd", lambda e, o=wxp2[:, :].rearrange("p (k o) -> p k o", o=16),
               i=wxp2_d.rearrange("(k p) o -> p k o", p=128): e.dma_start(out=o, in_=i), dma=True)
    lw3 = p.op("gpsimd", lambda e: e.dma_start(out=wdt[:, :], in_=wdt_d[:, :]), dma=True)
    lw4 = p.op("sync", lambda e: e.dma_start(out=a_re[:, :], in_=are_d[:, :]), dma=True)
    lw5 = p.op("sync", lambda e: e.dma_start(out=bdt[:, :], in_=bdt_d[:, :]), dma=True)
    lw6 = p.op("sync", lambda e: e.dma_start(out=dcol[:, :], in_=dd_d[:, :]), dma=True)
    lw7 = p.op("sync", lambda e: e.dma_start(out=sel16[:, :], in_=sel_d[:, :]), dma=True)

    ea1 = p.op("scalar", lambda e: e.activation(a_sc[:, :], a_re[:, :], AF.Exp), waits=[lw4])
    ea2 = p.op("vector", lambda e: e.tensor_scalar_mul(a_sc[:, :], a_sc[:, :], -1.0), waits=[ea1])

    ln_evs = emit_ln_to_fm(c, [c.ev_x])

    xdbl_evs = []
    for tq in range(NTQ):
        p1 = c.pA
        p2 = c.pB
        mm1 = None
        mm2 = None
        for kk in range(NKC):
            lh = wxp1[:, kk * 80:(kk + 1) * 80]
            rh = c.fmA[:, kk * T + tq * 512: kk * T + (tq + 1) * 512]
            mm1 = p.op("tensor", lambda e, o=p1.t[0:80, 0:512], l=lh, r=rh,
                       s=(kk == 0), z=(kk == NKC - 1): e.matmul(o, l, r, start=s, stop=z),
                       waits=ln_evs + [lw1, p1.ev_r] if kk == 0 else [])
        for kk in range(NKC):
            lh = wxp2[:, kk * 16:(kk + 1) * 16]
            rh = c.fmA[:, kk * T + tq * 512: kk * T + (tq + 1) * 512]
            mm2 = p.op("tensor", lambda e, o=p2.t[0:16, 0:512], l=lh, r=rh,
                       s=(kk == 0), z=(kk == NKC - 1): e.matmul(o, l, r, start=s, stop=z),
                       waits=[lw2, p2.ev_r] if kk == 0 else [])
        p1.ev_w = mm1
        p2.ev_w = mm2
        e1 = p.op("vector", lambda e, o=dtl[:, tq * 512:(tq + 1) * 512], i=p1.t[0:64, 0:512]:
                  e.tensor_copy(o, i), waits=[mm1])
        e2 = p.op("vector", lambda e, o=b_bf[:, tq * 512:(tq + 1) * 512], i=p1.t[64:80, 0:512]:
                  e.tensor_copy(o, i), waits=[mm1])
        e3 = p.op("vector", lambda e, o=c_bf[:, tq * 512:(tq + 1) * 512], i=p2.t[0:16, 0:512]:
                  e.tensor_copy(o, i), waits=[mm2])
        p1.ev_r = e2
        p2.ev_r = e3
        xdbl_evs += [e1, e2, e3]

    dt_evs = []
    prev_ad = None
    for dc in range(NKC):
        for tq in range(NTQ):
            pb = [c.pA, c.pB][(dc * NTQ + tq) % 2]
            bcol = bdt[:, dc:dc + 1]
            mm = p.op("tensor", lambda e, o=pb.t[:, 0:512], l=wdt[:, dc * 128:(dc + 1) * 128],
                      r=dtl[0:64, tq * 512:(tq + 1) * 512]:
                      e.matmul(o, l, r, start=True, stop=True),
                      waits=xdbl_evs + [lw3, pb.ev_r])
            pb.ev_w = mm
            a1 = p.op("scalar", lambda e, o=sA, i=pb.t[:, 0:512], b=bcol:
                      e.activation(o[:, :], i, AF.Abs, bias=b), waits=[mm, lw5, prev_ad])
            a2 = p.op("scalar", lambda e, o=sB, i=sA: e.activation(o[:, :], i[:, :], AF.Exp, scale=-1.0),
                      waits=[a1])
            a3 = p.op("scalar", lambda e, o=sA, i=sB: e.activation(o[:, :], i[:, :], AF.Ln, bias=1.0),
                      waits=[a2])
            a4 = p.op("scalar", lambda e, o=sB, i=pb.t[:, 0:512], b=bcol:
                      e.activation(o[:, :], i, AF.Relu, bias=b), waits=[a3])
            pb.ev_r = a4
            ad = p.op("vector", lambda e, o=dt_sb[:, dc * T + tq * 512: dc * T + (tq + 1) * 512],
                      i=sA, j=sB: e.tensor_add(o, i[:, :], j[:, :]), waits=[a3, a4])
            dt_evs.append(ad)
            prev_ad = ad

    if dbg:
        ddtl = nc.dram_tensor("dbg_dtl", [64, T], BF16, kind="ExternalOutput").ap()
        dbb = nc.dram_tensor("dbg_b", [16, T], BF16, kind="ExternalOutput").ap()
        dcb = nc.dram_tensor("dbg_c", [16, T], BF16, kind="ExternalOutput").ap()
        ddt = nc.dram_tensor("dbg_dt", [128, NKC * T], BF16, kind="ExternalOutput").ap()
        p.op("sync", lambda e: e.dma_start(out=ddtl[:, :], in_=dtl[:, :]), waits=xdbl_evs, dma=True)
        p.op("sync", lambda e: e.dma_start(out=dbb[:, :], in_=b_bf[:, :]), waits=xdbl_evs, dma=True)
        p.op("sync", lambda e: e.dma_start(out=dcb[:, :], in_=c_bf[:, :]), waits=xdbl_evs, dma=True)
        p.op("sync", lambda e: e.dma_start(out=ddt[:, :], in_=dt_sb[:, :]), waits=dt_evs, dma=True)

    c.stage_guard = None
    y_guard = [None] * NKC
    fin_evs = []
    scan_prev = [None] * NPAR
    pool_prev = [None] * NPAR
    for tq in range(NTQ):
        u_evs = []
        for dc in range(NKC):
            ue = p.op("vector", lambda e, o=u_sb[:, dc * 512:(dc + 1) * 512],
                      i=dt_sb[:, dc * T + tq * 512: dc * T + (tq + 1) * 512],
                      j=c.fmA[:, dc * T + tq * 512: dc * T + (tq + 1) * 512]:
                      e.tensor_mul(o, i, j), waits=dt_evs + ln_evs + [y_guard[dc]])
            u_evs.append(ue)
        for n in range(DST):
            mmB = p.op("tensor", lambda e, o=c.pC.t[:, 0:512], l=sel16[:, n * 128:(n + 1) * 128],
                       r=b_bf[:, tq * 512:(tq + 1) * 512]:
                       e.matmul(o, l, r, start=True, stop=True),
                       waits=xdbl_evs + [lw7, c.pC.ev_r])
            c.pC.ev_w = mmB
            eb = p.op("vector", lambda e, o=brep, i=c.pC.t[:, 0:512]: e.tensor_copy(o[:, :], i),
                      waits=[mmB] + scan_prev)
            c.pC.ev_r = eb
            mmC = p.op("tensor", lambda e, o=c.pD.t[:, 0:512], l=sel16[:, n * 128:(n + 1) * 128],
                       r=c_bf[:, tq * 512:(tq + 1) * 512]:
                       e.matmul(o, l, r, start=True, stop=True),
                       waits=[c.pD.ev_r])
            c.pD.ev_w = mmC
            ec = p.op("vector", lambda e, o=crep, i=c.pD.t[:, 0:512]: e.tensor_copy(o[:, :], i),
                      waits=[mmC] + pool_prev)
            c.pD.ev_r = ec
            for dc in range(NKC):
                par = dc % NPAR
                av = p.op("scalar", lambda e, o=a_t[par], i=dt_sb[:, dc * T + tq * 512: dc * T + (tq + 1) * 512],
                          s=a_sc[:, dc * 16 + n: dc * 16 + n + 1]:
                          e.activation(o[:, :], i, AF.Exp, scale=s),
                          waits=[ea2, scan_prev[par]] + dt_evs)
                bv = p.op("vector", lambda e, o=b_t[par], i=u_sb[:, dc * 512:(dc + 1) * 512], j=brep:
                          e.tensor_mul(o[:, :], i, j[:, :]), waits=u_evs + [eb, scan_prev[par]])
                init = 0.0 if tq == 0 else carry[:, dc * DST + n: dc * DST + n + 1]
                hv = p.op("vector", lambda e, o=h_t[par], i=a_t[par], j=b_t[par], z=init:
                          e.tensor_tensor_scan(o[:, :], i[:, :], j[:, :], z, ALU.mult, ALU.add),
                          waits=[av, bv, pool_prev[par]])
                cv = p.op("vector", lambda e, o=carry[:, dc * DST + n: dc * DST + n + 1], i=h_t[par]:
                          e.tensor_copy(o, i[:, 511:512]), waits=[hv])
                yeng = getattr(build_l1attn, "Y_ENGINE", "gpsimd")
                yv = p.op(yeng, lambda e, o=y_t[par], i=h_t[par], j=crep:
                          e.tensor_mul(o[:, :], i[:, :], j[:, :]), waits=[hv, ec])
                if n == 0:
                    ya = p.op(yeng, lambda e, o=y_acc[:, dc * 512:(dc + 1) * 512], i=y_t[par]:
                              e.tensor_copy(o, i[:, :]), waits=[yv, y_guard[dc]])
                else:
                    ya = p.op(yeng, lambda e, o=y_acc[:, dc * 512:(dc + 1) * 512], i=y_t[par]:
                              e.tensor_add(o, o, i[:, :]), waits=[yv])
                scan_prev[par] = cv
                pool_prev[par] = ya
        if dbg and tq == 0:
            dya = nc.dram_tensor("dbg_yacc", [128, NKC * 512], F32, kind="ExternalOutput").ap()
            dcr = nc.dram_tensor("dbg_carry", [128, NKC * DST], F32, kind="ExternalOutput").ap()
            p.op("sync", lambda e: e.dma_start(out=dya[:, :], in_=y_acc[:, :]),
                 waits=[pool_prev[0], pool_prev[1]], dma=True)
            p.op("sync", lambda e: e.dma_start(out=dcr[:, :], in_=carry[:, :]),
                 waits=[scan_prev[0], scan_prev[1]], dma=True)
        fence_mode = getattr(build_l1attn, "FENCE_MODE", "none")
        if fence_mode == "fence":
            p.fence()
        elif fence_mode == "sblock":
            p.wait("scalar", [("Ev", p.cnt["Ev"]), ("Eg", p.cnt["Eg"])])
        elif fence_mode == "tblock":
            p.wait("tensor", [("Ev", p.cnt["Ev"]), ("Eg", p.cnt["Eg"])])
        elif fence_mode == "gblock":
            p.wait("gpsimd", [("Ev", p.cnt["Ev"]), ("Es", p.cnt["Es"])])
        elif fence_mode == "slack":
            yeng0 = getattr(build_l1attn, "Y_ENGINE", "gpsimd")
            nslack = getattr(build_l1attn, "NSLACK", 4)
            sl = None
            for _ in range(nslack):
                sl = p.op(yeng0, lambda e: e.tensor_copy(sB[:, 0:512], sA[:, 0:512]),
                          waits=pool_prev)
            pool_prev = [sl, sl]
        elif fence_mode == "et":
            p.wait("vector", [("Et", p.cnt["Et"])])
        elif fence_mode == "es":
            p.wait("vector", [("Es", p.cnt["Es"])])
        elif fence_mode == "eg":
            p.wait("vector", [("Eg", p.cnt["Eg"])])
        if dbg and tq == 1:
            park = c.sb("park", [128, 2 * 512], F32)
            pk = p.op("vector", lambda e: e.tensor_copy(park[:, :], y_acc[:, 6 * 512:8 * 512]),
                      waits=[pool_prev[0], pool_prev[1]])
            dpk = nc.dram_tensor("dbg_park", [128, 2 * 512], F32, kind="ExternalOutput").ap()
            p.op("sync", lambda e: e.dma_start(out=dpk[:, :], in_=park[:, :]), waits=[pk], dma=True)
        for dc in range(NKC):
            fe = p.op("vector", lambda e, o=c.stageF,
                      i=c.fmA[:, dc * T + tq * 512: dc * T + (tq + 1) * 512],
                      s=dcol[:, dc:dc + 1], j=y_acc[:, dc * 512:(dc + 1) * 512]:
                      e.scalar_tensor_tensor(o[:, :], i, s, j, ALU.mult, ALU.add),
                      waits=pool_prev + [lw6, c.stage_guard])
            evs = emit_fm_psum_to_resid(c, c.stageF, dc, tq, [fe])
            c.stage_guard = evs[-1]
            y_guard[dc] = fe
            fin_evs += evs



def emit_l1ffn(c, w1_d, w2_d, b1_d, b2_d, gw_d, gb_d, temp_d, ones_row_d, sel8_d, dbg=False):
    nc = c.nc
    p = c.p
    NHC2 = EH // 128
    h_half = c.sb("h_half", [128, 16 * 1024], BF16)
    out_acc = c.sb("out_acc", [128, 8 * 1024], F32)
    wst = [c.sb("wst0", [128, 4096], BF16), c.sb("wst1", [128, 4096], BF16)]
    gw_sb = c.sb("gw_sb", [128, NKC * 8], BF16)
    gbr = c.sb("gbr", [128, 8], F32)
    onesr = c.sb("onesr", [1, 128], BF16)
    sel8 = c.sb("sel8_sb", [8, 8 * 128], BF16)
    b1sb = c.sb("b1sb", [8, 1024], F32)
    b1T = c.sb("b1T", [128, NHC2 * 8], F32)
    b2sb = c.sb("b2sb", [8, D], BF16)
    itemp = c.sb("itemp", [1, 1], F32)
    itcol = c.sb("itcol", [128, 1], F32)
    gates = c.sb("gates", [128, 8], F32)
    m1c = c.sb("m1c", [128, 1], F32)
    m2c = c.sb("m2c", [128, 1], F32)
    eq1 = c.sb("eq1", [128, 8], F32)
    eq2 = c.sb("eq2", [128, 8], F32)
    s1c = c.sb("s1c", [128, 1], F32)
    s2c = c.sb("s2c", [128, 1], F32)
    tmp8 = c.sb("tmp8", [128, 8], F32)
    w_tok = c.sb("w_tok", [128, NTT * 8], F32)
    w_fm = c.sb("w_fm", [8, T], BF16)
    wrep = c.sb("wrep", [128, 1024], BF16)
    g_t = [c.sb("g_t0", [128, 512], BF16), c.sb("g_t1", [128, 512], BF16)]

    lo = p.op("sync", lambda e: e.dma_start(out=onesr[:, :], in_=ones_row_d[:, :]), dma=True)
    ls8 = p.op("sync", lambda e: e.dma_start(out=sel8[:, :], in_=sel8_d[:, :]), dma=True)
    lgw = p.op("gpsimd", lambda e, o=gw_sb[:, :].rearrange("p (k e) -> p k e", e=8),
               i=gw_d.rearrange("(k p) e -> p k e", p=128): e.dma_start(out=o, in_=i), dma=True)
    gbf32 = c.sb("gbf32", [1, 8], F32)
    lgb = p.op("sync", lambda e: e.dma_start(out=gbf32[:, :], in_=gb_d[:, :]), dma=True)
    ltm = p.op("sync", lambda e: e.dma_start(out=itemp[:, :], in_=temp_d[:, :]), dma=True)
    lb2 = p.op("gpsimd", lambda e: e.dma_start(out=b2sb[:, :], in_=b2_d[:, :]), dma=True)
    gb_bf = c.sb("gb_bf", [1, 8], BF16)
    egb = p.op("vector", lambda e: e.tensor_copy(gb_bf[:, :], gbf32[:, :]), waits=[lgb])

    ln_evs = emit_ln_to_fm(c, [c.ev_x])
    p.fence()

    e1t = p.op("vector", lambda e: e.reciprocal(itemp[:, :], itemp[:, :]), waits=[ltm])
    it_bf = c.sb("it_bf", [1, 1], BF16)
    e2t = p.op("vector", lambda e: e.tensor_copy(it_bf[:, :], itemp[:, :]), waits=[e1t])
    mmt = p.op("tensor", lambda e: e.matmul(c.pF1.t[:, 0:1], onesr[:, :], it_bf[:, :], start=True, stop=True),
               waits=[e2t, lo, c.pF1.ev_r])
    c.pF1.ev_w = mmt
    e3t = p.op("vector", lambda e: e.tensor_copy(itcol[:, :], c.pF1.t[:, 0:1]), waits=[mmt])
    c.pF1.ev_r = c.confirm()
    mmg = p.op("tensor", lambda e: e.matmul(c.pF2.t[:, 0:8], onesr[:, :], gb_bf[:, :], start=True, stop=True),
               waits=[egb, lo, c.pF2.ev_r])
    c.pF2.ev_w = mmg
    e4t = p.op("vector", lambda e, s=itcol: e.tensor_scalar(gbr[:, :], c.pF2.t[:, 0:8], s[:, :], None, ALU.mult),
               waits=[mmg, e3t])
    c.pF2.ev_r = c.confirm()

    b1evs = []
    b1_rd_guard = None
    lb1 = None
    for hc in range(NHC2):
        if hc % 8 == 0:
            q = hc // 8
            lb1 = p.op("sync", lambda e, o=b1sb, i=b1_d[:, q * 1024:(q + 1) * 1024]:
                       e.dma_start(out=o[:, :], in_=i), waits=[b1_rd_guard], dma=True)
        pf = [c.pF1, c.pF2][hc % 2]
        tr = p.op("tensor", lambda e, o=pf.t[:, 0:8], i=b1sb[:, (hc % 8) * 128:(hc % 8 + 1) * 128]:
                  e.transpose(o, i, c.ident_f[0:8, 0:8]),
                  waits=[lb1, c.ev_identf, pf.ev_r])
        b1_rd_guard = tr
        pf.ev_w = tr
        ee = p.op("vector", lambda e, o=b1T[:, hc * 8:(hc + 1) * 8], i=pf.t[:, 0:8]:
                  e.tensor_copy(o, i), waits=[tr])
        pf.ev_r = c.confirm()
        b1evs.append(ee)

    p.fence()
    wtok_evs = []
    prev8 = None
    prev10 = None
    for tt in range(NTT):
        pb = [c.pC, c.pD][tt % 2]
        mm = None
        for kk in range(NKC):
            lh = c.fmA[:, kk * T + tt * 128: kk * T + (tt + 1) * 128]
            rh = gw_sb[:, kk * 8:(kk + 1) * 8]
            mm = p.op("tensor", lambda e, o=pb.t[:, 0:8], l=lh, r=rh, s=(kk == 0), z=(kk == NKC - 1):
                      e.matmul(o, l, r, start=s, stop=z),
                      waits=ln_evs + [lgw, pb.ev_r] if kk == 0 else [])
        pb.ev_w = mm
        ge = p.op("vector", lambda e, o=gates, i=pb.t[:, 0:8]:
                  e.scalar_tensor_tensor(o[:, :], i, itcol[:, :], gbr[:, :], ALU.mult, ALU.add),
                  waits=[mm, e4t, e3t] + ([c.dbg_gev] if dbg and getattr(c, "dbg_gev", None) else []))
        pb.ev_r = c.confirm()
        if dbg:
            if tt == 0:
                c.dbg_g = nc.dram_tensor("dbg_gates", [128, NTT * 8], F32, kind="ExternalOutput").ap()
                c.dbg_praw = nc.dram_tensor("dbg_praw", [128, NTT * 8], F32, kind="ExternalOutput").ap()
            dgr = p.op("vector", lambda e, o=c.tmp16[:, 0:8], i=pb.t[:, 0:8]: e.tensor_copy(o, i),
                       waits=[mm, getattr(c, "dbg_gev2", None)])
            c.dbg_gev = p.op("sync", lambda e, o=c.dbg_g[:, tt * 8:(tt + 1) * 8]: e.dma_start(out=o, in_=gates[:, :]),
                             waits=[ge], dma=True)
            c.dbg_gev2 = p.op("sync", lambda e, o=c.dbg_praw[:, tt * 8:(tt + 1) * 8]:
                              e.dma_start(out=o, in_=c.tmp16[:, 0:8]), waits=[dgr], dma=True)
        ee1 = p.op("vector", lambda e: e.reduce_max(m1c[:, :], gates[:, :], axis=AX.X), waits=[ge])
        ee2 = p.op("vector", lambda e: e.tensor_single_scalar(eq1[:, :], gates[:, :], m1c[:, :], ALU.is_ge),
                   waits=[ee1])
        ee3 = p.op("vector", lambda e: e.scalar_tensor_tensor(tmp8[:, :], eq1[:, :], -1e30, gates[:, :],
                                                             ALU.mult, ALU.add), waits=[ee2])
        ee4 = p.op("vector", lambda e: e.reduce_max(m2c[:, :], tmp8[:, :], axis=AX.X), waits=[ee3])
        ee5 = p.op("vector", lambda e: e.tensor_single_scalar(eq2[:, :], tmp8[:, :], m2c[:, :], ALU.is_ge),
                   waits=[ee4])
        ee6 = p.op("vector", lambda e: e.tensor_sub(c.halfcol[:, :], m1c[:, :], m2c[:, :]), waits=[ee4, prev8])
        ee7 = p.op("scalar", lambda e: e.activation(s1c[:, :], c.halfcol[:, :], AF.Sigmoid), waits=[ee6, prev10])
        ee8 = p.op("scalar", lambda e: e.activation(s2c[:, :], s1c[:, :], AF.Copy, scale=-1.0, bias=1.0),
                   waits=[ee7])
        ee9 = p.op("vector", lambda e: e.tensor_scalar(tmp8[:, :], eq1[:, :], s1c[:, :], None, ALU.mult),
                   waits=[ee7, ee5])
        ee10 = p.op("vector", lambda e, o=w_tok[:, tt * 8:(tt + 1) * 8]:
                    e.scalar_tensor_tensor(o, eq2[:, :], s2c[:, :], tmp8[:, :], ALU.mult, ALU.add),
                    waits=[ee8, ee9])
        prev8 = ee8
        prev10 = ee10
        wtok_evs.append(ee10)

    p.fence()
    if dbg:
        dwt = nc.dram_tensor("dbg_wtok", [128, NTT * 8], F32, kind="ExternalOutput").ap()
        p.op("sync", lambda e: e.dma_start(out=dwt[:, :], in_=w_tok[:, :]), waits=wtok_evs, dma=True)
        db1 = nc.dram_tensor("dbg_b1T", [128, NHC2 * 8], F32, kind="ExternalOutput").ap()
        p.op("sync", lambda e: e.dma_start(out=db1[:, :], in_=b1T[:, :]), waits=b1evs, dma=True)

    wfm_evs = []
    for tt in range(NTT):
        pf = [c.pF1, c.pF2][tt % 2]
        tr = p.op("tensor", lambda e, o=pf.t[0:8, 0:128], i=w_tok[:, tt * 8:(tt + 1) * 8]:
                  e.transpose(o, i, c.ident_f[:, :]),
                  waits=wtok_evs + [c.ev_identf, pf.ev_r])
        pf.ev_w = tr
        ee = p.op("vector", lambda e, o=w_fm[:, tt * 128:(tt + 1) * 128], i=pf.t[0:8, 0:128]:
                  e.tensor_copy(o, i), waits=[tr])
        pf.ev_r = ee
        wfm_evs.append(ee)

    p.fence()

    gate_done = ("Ev", p.cnt["Ev"])
    c.w1_guard = gate_done
    c.w2_guard = gate_done
    g_guard = [None, None]
    w_seq = []
    for _th in range(2):
        for ei in range(NE):
            for hh in range(2):
                for hg in range(4):
                    w_seq.append(("w1", ei, hh, hg))
                for grp in range(4):
                    w_seq.append(("w2", ei, hh, grp))
    w_evs = {}
    w_guardb = [gate_done, gate_done]
    w_next = [0]

    def issue_w():
        i = w_next[0]
        if i >= len(w_seq):
            return
        w_next[0] = i + 1
        kind, ei, hh, j = w_seq[i]
        par = i % 2
        if kind == "w1":
            hbase = hh * 2048 + j * 512
            src = w1_d[ei, :, hbase:hbase + 512].rearrange("(k p) h -> p k h", p=128)
            dst = wst[par][:, :].rearrange("p (k h) -> p k h", h=512)
        else:
            rbase = hh * 2048 + j * 512
            src = w2_d[ei, rbase:rbase + 512, :].rearrange("(k p) o -> p k o", p=128)
            dst = wst[par][:, :].rearrange("p (k o) -> p k o", o=D)
        w_evs[i] = p.op("gpsimd", lambda e, o=dst, i_=src:
                        e.dma_start(out=o, in_=i_), waits=[w_guardb[par]], dma=True)

    issue_w()
    issue_w()
    w_i = [0]

    for th in range(2):
        oacc_evs = []
        first_e = True
        for ei in range(NE):
            wr_evs = []
            for tsub in range(2):
                mmr = p.op("tensor", lambda e, o=c.pD.t[:, 0:512],
                           l=sel8[:, ei * 128:(ei + 1) * 128],
                           r=w_fm[:, th * 1024 + tsub * 512: th * 1024 + (tsub + 1) * 512],
                           s=True, z=True:
                           e.matmul(o, l, r, start=s, stop=z),
                           waits=wfm_evs + [ls8, c.pD.ev_r])
                c.pD.ev_w = mmr
                er = p.op("vector", lambda e, o=wrep[:, tsub * 512:(tsub + 1) * 512], i=c.pD.t[:, 0:512]:
                          e.tensor_copy(o, i), waits=[mmr])
                c.pD.ev_r = c.confirm()
                wr_evs.append(er)
            for hh in range(2):
                ph_evs = []
                for hg in range(4):
                    gi = w_i[0]
                    w_i[0] = gi + 1
                    par = gi % 2
                    lw1 = w_evs[gi]
                    for hti in range(4):
                        hc = hh * 16 + hg * 4 + hti
                        hloc = hg * 4 + hti
                        for tsub in range(2):
                            pb = [c.pA, c.pB][tsub]
                            mm = None
                            for kk in range(NKC):
                                lh = wst[par][:, kk * 512 + hti * 128: kk * 512 + (hti + 1) * 128]
                                rh = c.fmA[:, kk * T + th * 1024 + tsub * 512: kk * T + th * 1024 + (tsub + 1) * 512]
                                mm = p.op("tensor", lambda e, o=pb.t[:, 0:512], l=lh, r=rh,
                                          s=(kk == 0), z=(kk == NKC - 1):
                                          e.matmul(o, l, r, start=s, stop=z),
                                          waits=ln_evs + [lw1, pb.ev_r] if kk == 0 else [])
                            pb.ev_w = mm
                            w_guardb[par] = mm
                            gl = p.op("scalar", lambda e, o=g_t[tsub], i=pb.t[:, 0:512],
                                      b=b1T[:, hc * 8 + ei: hc * 8 + ei + 1]:
                                      e.activation(o[:, :], i, AF.Gelu, bias=b),
                                      waits=b1evs + [pb.ev_w, g_guard[tsub]])
                            wm = p.op("vector", lambda e, o=h_half[:, hloc * 1024 + tsub * 512: hloc * 1024 + (tsub + 1) * 512],
                                      i=g_t[tsub], w=wrep[:, tsub * 512:(tsub + 1) * 512]:
                                      e.tensor_mul(o, i[:, :], w[:, :]),
                                      waits=wr_evs + [gl])
                            g_guard[tsub] = wm
                            pb.ev_r = wm
                            ph_evs.append(wm)
                    issue_w()
                for grp in range(4):
                    gi = w_i[0]
                    w_i[0] = gi + 1
                    par = gi % 2
                    lw2 = w_evs[gi]
                    for ttl in range(8):
                        for oh in range(2):
                            pb = [c.pC, c.pD][(ttl * 2 + oh) % 2]
                            mm = None
                            for kk in range(4):
                                hloc = grp * 4 + kk
                                lh = h_half[:, hloc * 1024 + ttl * 128: hloc * 1024 + (ttl + 1) * 128]
                                rh = wst[par][:, kk * D + oh * 512: kk * D + (oh + 1) * 512]
                                mm = p.op("tensor", lambda e, o=pb.t[:, 0:512], l=lh, r=rh,
                                          s=(kk == 0), z=(kk == 3):
                                          e.matmul(o, l, r, start=s, stop=z),
                                          waits=ph_evs + [lw2, pb.ev_r] if kk == 0 else [])
                            pb.ev_w = mm
                            w_guardb[par] = mm
                            oslice = out_acc[:, ttl * 1024 + oh * 512: ttl * 1024 + (oh + 1) * 512]
                            if first_e and hh == 0 and grp == 0:
                                ea = p.op("vector", lambda e, o=oslice, i=pb.t[:, 0:512]:
                                          e.tensor_copy(o, i), waits=[mm])
                            else:
                                ea = p.op("vector", lambda e, o=oslice, i=pb.t[:, 0:512]:
                                          e.tensor_add(o, i, o), waits=[mm])
                            pb.ev_r = ea
                            oacc_evs.append(ea)
                    issue_w()
            first_e = False
        p.fence()
        for ttl in range(8):
            tt = th * 8 + ttl
            for oh in range(2):
                pb = [c.pA, c.pB][oh]
                mm = p.op("tensor", lambda e, o=pb.t[:, 0:512],
                          l=w_fm[:, tt * 128:(tt + 1) * 128],
                          r=b2sb[:, oh * 512:(oh + 1) * 512], s=True, z=True:
                          e.matmul(o, l, r, start=s, stop=z),
                          waits=wfm_evs + [lb2, pb.ev_r])
                pb.ev_w = mm
                xv = c.xr[:, tt * D + oh * 512: tt * D + (oh + 1) * 512]
                e1 = p.op("vector", lambda e, o=xv, i=out_acc[:, ttl * 1024 + oh * 512: ttl * 1024 + (oh + 1) * 512]:
                          e.tensor_add(o, i, o), waits=oacc_evs)
                e2 = p.op("vector", lambda e, o=xv, i=pb.t[:, 0:512]:
                          e.tensor_add(o, i, o), waits=[mm, e1])
                pb.ev_r = c.confirm()
        p.fence()



def _mk(nc_args=()):
    nc = bass.Bass("TRN2", target_bir_lowering=False)
    return nc


def _declare_common(nc):
    x_d = nc.dram_tensor("x", [T, D], F32, kind="ExternalInput").ap()
    identb_d = nc.dram_tensor("ident_bf", [128, 128], BF16, kind="ExternalInput").ap()
    identf_d = nc.dram_tensor("ident_f", [128, 128], F32, kind="ExternalInput").ap()
    out_d = nc.dram_tensor("xout", [T, D], F32, kind="ExternalOutput").ap()
    return x_d, identb_d, identf_d, out_d


def _decl_l0attn(nc):
    return (nc.dram_tensor("wqT", [D, D], F32, kind="ExternalInput").ap(),
            nc.dram_tensor("wkT", [D, D], F32, kind="ExternalInput").ap(),
            nc.dram_tensor("wvT", [D, D], F32, kind="ExternalInput").ap(),
            nc.dram_tensor("woT", [D, D], F32, kind="ExternalInput").ap(),
            nc.dram_tensor("mask4", [128, 4 * 512], BF16, kind="ExternalInput").ap(),
            nc.dram_tensor("ones_col", [128, 1], BF16, kind="ExternalInput").ap())


def _decl_l0ffn(nc):
    return (nc.dram_tensor("wgT", [D, FFH], I32, kind="ExternalInput").ap(),
            nc.dram_tensor("wuT", [D, FFH], I32, kind="ExternalInput").ap(),
            nc.dram_tensor("wdT", [FFH, D], I32, kind="ExternalInput").ap())


def _decl_l1attn(nc):
    return (nc.dram_tensor("wxp1T", [D, 80], F32, kind="ExternalInput").ap(),
            nc.dram_tensor("wxp2T", [D, 16], F32, kind="ExternalInput").ap(),
            nc.dram_tensor("wdtT", [DTR, D], F32, kind="ExternalInput").ap(),
            nc.dram_tensor("A_re", [128, 128], F32, kind="ExternalInput").ap(),
            nc.dram_tensor("b_dt_c", [128, 8], F32, kind="ExternalInput").ap(),
            nc.dram_tensor("D_c", [128, 8], F32, kind="ExternalInput").ap(),
            nc.dram_tensor("sel16", [16, 16 * 128], BF16, kind="ExternalInput").ap())


def _decl_l1ffn(nc):
    return (nc.dram_tensor("w1", [NE, D, EH], F32, kind="ExternalInput").ap(),
            nc.dram_tensor("w2", [NE, EH, D], F32, kind="ExternalInput").ap(),
            nc.dram_tensor("b1", [NE, EH], F32, kind="ExternalInput").ap(),
            nc.dram_tensor("b2", [NE, D], F32, kind="ExternalInput").ap(),
            nc.dram_tensor("gw", [D, NE], F32, kind="ExternalInput").ap(),
            nc.dram_tensor("gb", [1, NE], F32, kind="ExternalInput").ap(),
            nc.dram_tensor("temp", [1, 1], F32, kind="ExternalInput").ap(),
            nc.dram_tensor("ones_row", [1, 128], BF16, kind="ExternalInput").ap(),
            nc.dram_tensor("sel8", [8, 8 * 128], BF16, kind="ExternalInput").ap())


def _standalone(decl_fn, emit_fn, dbg=False):
    nc = bass.Bass("TRN2", target_bir_lowering=False)
    x_d, identb_d, identf_d, out_d = _declare_common(nc)
    drs = decl_fn(nc)
    with ExitStack() as stack:
        p = Plan(nc, stack)
        c = mk_ctx(nc, stack, p)
        alloc_common(c)
        emit_load_consts(c, identb_d, identf_d)
        emit_load_x(c, x_d)
        with ExitStack() as bst:
            c.stacks.append(bst)
            emit_fn(c, *drs, dbg=dbg)
            p.fence()
            c.stacks.pop()
        emit_store_x(c, out_d)
        p.wait("gpsimd", p.now())
        with nc.Block() as block:
            p.build(block)
    return nc


def build_l0attn(dbg=False):
    return _standalone(_decl_l0attn, emit_l0attn, dbg)


def build_l0ffn(dbg=False):
    return _standalone(_decl_l0ffn, emit_l0ffn, dbg)


def build_l1attn(dbg=False):
    return _standalone(_decl_l1attn, emit_l1attn, dbg)


def build_l1ffn(dbg=False):
    return _standalone(_decl_l1ffn, emit_l1ffn, dbg)


def build_fused():
    nc = bass.Bass("TRN2", target_bir_lowering=False)
    x_d, identb_d, identf_d, out_d = _declare_common(nc)
    dr0a = _decl_l0attn(nc)
    dr0f = _decl_l0ffn(nc)
    dr1a = _decl_l1attn(nc)
    dr1f = _decl_l1ffn(nc)
    with ExitStack() as stack:
        p = Plan(nc, stack)
        c = mk_ctx(nc, stack, p)
        alloc_common(c)
        emit_load_consts(c, identb_d, identf_d)
        emit_load_x(c, x_d)
        for emit_fn, drs in ((emit_l0attn, dr0a), (emit_l0ffn, dr0f),
                             (emit_l1attn, dr1a), (emit_l1ffn, dr1f)):
            with ExitStack() as bst:
                c.stacks.append(bst)
                emit_fn(c, *drs)
                p.fence()
                c.stacks.pop()
        emit_store_x(c, out_d)
        p.wait("gpsimd", p.now())
        with nc.Block() as block:
            p.build(block)
    return nc

def run_l1ffn(x_full, W1, b1, W2, b2, gate_w, gate_b, temperature):
    nc = build_l1ffn()
    ident_bf, ident_f, _, _ = host_consts()
    sel8 = np.zeros((8, 8 * 128), dtype=BF)
    for e in range(8):
        sel8[e, e * 128:(e + 1) * 128] = 1.0
    ones_row = np.ones((1, 128), dtype=BF)
    in_maps = []
    for i in range(8):
        in_maps.append(dict(x=np.ascontiguousarray(x_full[i]),
                            w1=np.asarray(W1), w2=np.asarray(W2),
                            b1=np.asarray(b1), b2=np.asarray(b2),
                            gw=np.asarray(gate_w), gb=np.asarray(gate_b).reshape(1, 8),
                            temp=np.asarray(temperature).reshape(1, 1),
                            ones_row=ones_row, sel8=sel8,
                            ident_bf=ident_bf, ident_f=ident_f))
    res = run_bass_kernel_spmd(nc, in_maps, core_ids=list(range(8)))
    return np.stack([r["xout"] for r in res.results])



def host_consts():
    ident_bf = np.eye(128, dtype=BF)
    ident_f = np.eye(128, dtype=np.float32)
    mask4 = np.zeros((128, 4 * 512), dtype=BF)
    for rel in range(4):
        r = np.arange(128)[:, None]
        j = np.arange(512)[None, :]
        mask4[:, rel * 512:(rel + 1) * 512] = (j >= r + rel * 128).astype(BF)
    ones_col = np.ones((128, 1), dtype=BF)
    return ident_bf, ident_f, mask4, ones_col


def run_l0attn(x_full, Wq, Wk, Wv, Wo):
    nc = build_l0attn()
    ident_bf, ident_f, mask4, ones_col = host_consts()
    wqT = np.ascontiguousarray(Wq.T)
    wkT = np.ascontiguousarray(Wk.T)
    wvT = np.ascontiguousarray(Wv.T)
    woT = np.ascontiguousarray(Wo.T)
    in_maps = []
    for i in range(8):
        in_maps.append(dict(x=np.ascontiguousarray(x_full[i]), wqT=wqT, wkT=wkT, wvT=wvT, woT=woT,
                            ident_bf=ident_bf, ident_f=ident_f, mask4=mask4, ones_col=ones_col))
    res = run_bass_kernel_spmd(nc, in_maps, core_ids=list(range(8)))
    return np.stack([r["xout"] for r in res.results])


def run_l0ffn(x_full, w_gate, w_up, w_down):
    nc = build_l0ffn()
    ident_bf, ident_f, _, _ = host_consts()
    wgT = np.ascontiguousarray(w_gate.T)
    wuT = np.ascontiguousarray(w_up.T)
    wdT = np.ascontiguousarray(w_down.T)
    in_maps = []
    for i in range(8):
        in_maps.append(dict(x=np.ascontiguousarray(x_full[i]), wgT=wgT, wuT=wuT, wdT=wdT,
                            ident_bf=ident_bf, ident_f=ident_f))
    res = run_bass_kernel_spmd(nc, in_maps, core_ids=list(range(8)))
    return np.stack([r["xout"] for r in res.results])


def _l1attn_host(A_log, Dp, W_xproj, W_dt, b_dt):
    wxpT = np.ascontiguousarray(np.asarray(W_xproj).T)
    sel16 = np.zeros((16, 16 * 128), dtype=BF)
    for n in range(16):
        sel16[n, n * 128:(n + 1) * 128] = 1.0
    return dict(
        wxp1T=np.ascontiguousarray(wxpT[:, 0:80]),
        wxp2T=np.ascontiguousarray(wxpT[:, 80:96]),
        wdtT=np.ascontiguousarray(np.asarray(W_dt).T),
        A_re=np.ascontiguousarray(np.asarray(A_log).reshape(8, 128, 16).transpose(1, 0, 2).reshape(128, 128)),
        b_dt_c=np.ascontiguousarray(np.asarray(b_dt).reshape(8, 128).T),
        D_c=np.ascontiguousarray(np.asarray(Dp).reshape(8, 128).T),
        sel16=sel16)


def run_l1attn(x_full, A_log, Dp, W_xproj, W_dt, b_dt):
    nc = build_l1attn()
    ident_bf, ident_f, _, _ = host_consts()
    hm = _l1attn_host(A_log, Dp, W_xproj, W_dt, b_dt)
    in_maps = []
    for i in range(8):
        in_maps.append(dict(x=np.ascontiguousarray(x_full[i]), ident_bf=ident_bf, ident_f=ident_f, **hm))
    res = run_bass_kernel_spmd(nc, in_maps, core_ids=list(range(8)))
    return np.stack([r["xout"] for r in res.results])


def _sel8_host():
    sel8 = np.zeros((8, 8 * 128), dtype=BF)
    for e in range(8):
        sel8[e, e * 128:(e + 1) * 128] = 1.0
    return sel8


def kernel(**inputs):
    g = lambda k: np.asarray(inputs[k])
    x = np.asarray(inputs["L0_attn_x"], dtype=np.float32)
    nc = build_fused()
    ident_bf, ident_f, mask4, ones_col = host_consts()
    hm1a = _l1attn_host(g("L1_attn_A_log"), g("L1_attn_D"), g("L1_attn_W_xproj"),
                        g("L1_attn_W_dt"), g("L1_attn_b_dt"))
    shared = dict(
        ident_bf=ident_bf, ident_f=ident_f, mask4=mask4, ones_col=ones_col,
        wqT=np.ascontiguousarray(g("L0_attn_Wq").T), wkT=np.ascontiguousarray(g("L0_attn_Wk").T),
        wvT=np.ascontiguousarray(g("L0_attn_Wv").T), woT=np.ascontiguousarray(g("L0_attn_Wo").T),
        wgT=np.ascontiguousarray(g("L0_ffn_w_gate").T), wuT=np.ascontiguousarray(g("L0_ffn_w_up").T),
        wdT=np.ascontiguousarray(g("L0_ffn_w_down").T),
        w1=g("L1_ffn_W1"), w2=g("L1_ffn_W2"), b1=g("L1_ffn_b1"), b2=g("L1_ffn_b2"),
        gw=g("L1_ffn_gate_w"), gb=g("L1_ffn_gate_b").reshape(1, 8),
        temp=g("L1_ffn_temperature").reshape(1, 1),
        ones_row=np.ones((1, 128), dtype=BF), sel8=_sel8_host(),
        **hm1a)
    in_maps = [dict(x=np.ascontiguousarray(x[i]), **shared) for i in range(8)]
    res = run_bass_kernel_spmd(nc, in_maps, core_ids=list(range(8)))
    return np.stack([r["xout"] for r in res.results])
